# version 05
# baseline (speedup 1.0000x reference)
import numpy as np
import ml_dtypes
import concourse.bass as bass
import concourse.mybir as mybir
from concourse.bass_utils import run_bass_kernel_spmd

F32 = mybir.dt.float32
BF16 = mybir.dt.bfloat16
AF = mybir.ActivationFunctionType
ALU = mybir.AluOpType
AX = mybir.AxisListType

S = 4096
D = 2048
TT = 1024
NPASS = S // TT
NDK = D // 128
MOE_I = 1408
SH_I = 5632
NE = 8
NG = 12
NGS = 4
NI = MOE_I // 128
EPS = 1e-5
CAP = 512
CT = CAP // 128
PADV = 1.0e6

ENGS = ("tensor", "vector", "scalar", "sync", "gpsimd")
GAP = 8


class Tracker:

    def __init__(self, nc):
        self.nc = nc
        self.q = {e: [] for e in ENGS}
        self.cnt = {}
        self.bufs = {}
        self.opidx = {e: 0 for e in ENGS}
        self.waited = {e: {} for e in ENGS}
        self.pending = []
        self.esem = {}

    def sem(self, name):
        self.cnt.setdefault(name, 0)
        return name

    def engine_sems(self, names):
        for e, s in names.items():
            self.esem[e] = self.sem(s)

    def _want(self, need, sem, val):
        if need.get(sem, 0) < val:
            need[sem] = val

    def _edge(self, eng, weng, widx, wdma):
        if weng is None:
            return False
        if wdma or weng != eng:
            return True
        return (self.opidx[eng] - widx) < GAP

    def op(self, eng, fn, reads=(), writes=(), sem=None, inc=0, close=True,
           manual_waits=()):
        need = {}
        for (msem, mval) in manual_waits:
            self._want(need, msem, mval)
        for b in reads:
            rec = self.bufs.get(b)
            if rec and rec["w"] and self._edge(eng, rec["weng"], rec["widx"], rec["wdma"]):
                self._want(need, rec["w"][0], rec["w"][1])
        for b in writes:
            rec = self.bufs.get(b)
            if rec:
                if rec["w"] and self._edge(eng, rec["weng"], rec["widx"], rec["wdma"]):
                    self._want(need, rec["w"][0], rec["w"][1])
                for rsem, (rcnt, reng, ridx, rdma) in rec["r"].items():
                    if self._edge(eng, reng, ridx, rdma):
                        self._want(need, rsem, rcnt)
        for s, v in sorted(need.items()):
            if self.waited[eng].get(s, 0) < v:
                self.q[eng].append(("wait", s, v))
                self.waited[eng][s] = v
        idx = self.opidx[eng]
        self.opidx[eng] += 1
        is_dma = inc >= 16
        ev = None
        if inc:
            self.cnt[sem] += inc
            ev = (sem, self.cnt[sem])
        self.q[eng].append(("op", fn, sem if inc else None, inc))
        if not close:
            self.pending.append((eng, idx, tuple(reads), tuple(writes)))
            return None
        if ev:
            for (peng, pidx, preads, pwrites) in self.pending:
                if peng != eng:
                    continue
                for b in preads:
                    self._note_read(b, ev, peng, pidx, is_dma)
                for b in pwrites:
                    self._note_write(b, ev, peng, pidx, is_dma)
            self.pending = [p for p in self.pending if p[0] != eng]
        for b in reads:
            self._note_read(b, ev, eng, idx, is_dma)
        for b in writes:
            self._note_write(b, ev, eng, idx, is_dma)
        return ev

    def _note_read(self, b, ev, eng, idx, dma):
        rec = self.bufs.setdefault(
            b, {"w": None, "weng": None, "widx": -1 << 30, "wdma": False, "r": {}})
        if ev is None:
            return
        rec["r"][ev[0]] = (ev[1], eng, idx, dma)

    def _note_write(self, b, ev, eng, idx, dma):
        self.bufs[b] = {"w": ev, "weng": eng, "widx": idx, "wdma": dma, "r": {}}

    def cop(self, eng, fn, reads=(), writes=(), close=True, manual_waits=()):
        return self.op(eng, fn, reads, writes, sem=self.esem[eng],
                       inc=1 if close else 0, close=close,
                       manual_waits=manual_waits)

    def barrier_wait(self, eng, sem, val):
        if self.waited[eng].get(sem, 0) < val:
            self.q[eng].append(("wait", sem, val))
            self.waited[eng][sem] = val

    def replay(self, block):
        for name in ENGS:
            items = self.q[name]

            def make(items=items):
                def body(eng):
                    for it in items:
                        if it[0] == "wait":
                            eng.wait_ge(self.nc._sems[it[1]], it[2])
                        else:
                            _, fn, sm, inc = it
                            ins = fn(eng)
                            if sm is not None:
                                ins.then_inc(self.nc._sems[sm], inc)
                return body

            getattr(block, name)(make())


def build_nc():
    nc = bass.Bass("TRN2", target_bir_lowering=False)

    x_d = nc.dram_tensor("x", [S, D], F32, kind="ExternalInput").ap()
    wq_d = nc.dram_tensor("wq", [D, D], F32, kind="ExternalInput").ap()
    wk_d = nc.dram_tensor("wk", [D, D], F32, kind="ExternalInput").ap()
    wv_d = nc.dram_tensor("wv", [D, D], F32, kind="ExternalInput").ap()
    wg_d = nc.dram_tensor("wg", [D, D], F32, kind="ExternalInput").ap()
    bqkvg_d = nc.dram_tensor("bqkvg", [4, D], F32, kind="ExternalInput").ap()
    gw9_d = nc.dram_tensor("gw9", [9, D], F32, kind="ExternalInput").ap()
    wgu_d = nc.dram_tensor("wgu", [2 * SH_I, D], F32, kind="ExternalInput").ap()
    wsd_d = nc.dram_tensor("wsd", [4 * 1024, 2816], F32, kind="ExternalInput").ap()
    w13_d = nc.dram_tensor("w13", [NE, 2 * MOE_I, D], F32, kind="ExternalInput").ap()
    w2_d = nc.dram_tensor("w2", [NE, 1024, 2816], F32, kind="ExternalInput").ap()
    idf_d = nc.dram_tensor("idf", [128, 128], F32, kind="ExternalInput").ap()
    idb_d = nc.dram_tensor("idb", [128, 128], BF16, kind="ExternalInput").ap()
    sel_d = nc.dram_tensor("sel", [9, 9 * 128], BF16, kind="ExternalInput").ap()
    cst_d = nc.dram_tensor("cst", [128, 146], F32, kind="ExternalInput").ap()
    out_d = nc.dram_tensor("out", [S, D], F32, kind="ExternalOutput").ap()

    lists_d = nc.dram_tensor("lists", [NE * CAP, 2], F32, kind="Internal").ap()
    eout_d = nc.dram_tensor("eout", [NE * CAP, D], BF16, kind="Internal").ap()
    xtok_d = nc.dram_tensor("xtok", [TT + 128, D], BF16, kind="Internal").ap()

    zq = nc.dram_tensor("zq", [D, D], BF16, kind="Internal").ap()
    zk = nc.dram_tensor("zk", [D, D], BF16, kind="Internal").ap()
    zv = nc.dram_tensor("zv", [D, D], BF16, kind="Internal").ap()
    zg = nc.dram_tensor("zg", [D, D], BF16, kind="Internal").ap()
    zgu = nc.dram_tensor("zgu", [2 * SH_I, D], BF16, kind="Internal").ap()
    zsd = nc.dram_tensor("zsd", [4 * 1024, 2816], BF16, kind="Internal").ap()
    z13 = nc.dram_tensor("z13", [NE, 2 * MOE_I, D], BF16, kind="Internal").ap()
    z2 = nc.dram_tensor("z2", [NE, 1024, 2816], BF16, kind="Internal").ap()

    from contextlib import ExitStack
    ctx = ExitStack()
    E = ctx.enter_context

    s_xn = E(nc.sbuf_tensor("s_xn", [128, NDK * TT], BF16))
    s_out = E(nc.sbuf_tensor("s_out", [128, NDK * TT], F32))
    s_ha = E(nc.sbuf_tensor("s_ha", [128, 11264], BF16))
    s_q = s_ha[:, 8192:8192 + TT]
    s_k = s_ha[:, 9216:9216 + TT]
    s_g = s_ha[:, 10240:10240 + TT]
    s_wh = E(nc.sbuf_tensor("s_wh", [128, 8192], BF16))
    s_wo0 = E(nc.sbuf_tensor("s_wo0", [128, NI * 256], BF16))
    s_wo1 = E(nc.sbuf_tensor("s_wo1", [128, NI * 256], BF16))
    s_xa = E(nc.sbuf_tensor("s_xa", [128, D], F32))
    s_xb = E(nc.sbuf_tensor("s_xb", [128, D], F32))
    s_xntm = E(nc.sbuf_tensor("s_xntm", [128, D], BF16))
    s_sg = E(nc.sbuf_tensor("s_sg", [128, TT], BF16))
    s_qg = s_sg
    s_ut = E(nc.sbuf_tensor("s_ut", [128, TT], BF16))
    s_kv = s_ut
    s_wb0 = E(nc.sbuf_tensor("s_wb0", [128, TT], BF16))
    s_wb1 = E(nc.sbuf_tensor("s_wb1", [128, TT], BF16))
    s_w9fm = E(nc.sbuf_tensor("s_w9fm", [9, TT], BF16))
    s_ones = E(nc.sbuf_tensor("s_ones", [128, TT], BF16))
    s_carry = E(nc.sbuf_tensor("s_carry", [128, NDK], F32))
    s_bias = E(nc.sbuf_tensor("s_bias", [128, 4 * NDK], F32))
    s_gw9T = E(nc.sbuf_tensor("s_gw9T", [128, NDK * 9], BF16))
    s_sel = E(nc.sbuf_tensor("s_sel", [9, 128], BF16))
    s_idf = E(nc.sbuf_tensor("s_idf", [128, 128], F32))
    s_idb = E(nc.sbuf_tensor("s_idb", [128, 128], BF16))
    s_w9tm = E(nc.sbuf_tensor("s_w9tm", [128, 16], BF16))
    s_m1 = E(nc.sbuf_tensor("s_m1", [128, 1], F32))
    s_m2 = E(nc.sbuf_tensor("s_m2", [128, 1], F32))
    s_negm1 = E(nc.sbuf_tensor("s_negm1", [128, 1], F32))
    s_sum = E(nc.sbuf_tensor("s_sum", [128, 1], F32))
    s_rinv = E(nc.sbuf_tensor("s_rinv", [128, 1], F32))
    s_l2 = E(nc.sbuf_tensor("s_l2", [128, 8], F32))
    s_exp = E(nc.sbuf_tensor("s_exp", [128, 8], F32))
    s_wtmp = E(nc.sbuf_tensor("s_wtmp", [128, 8], F32))
    s_bn = E(nc.sbuf_tensor("s_bn", [128, 24], F32))
    s_mv = E(nc.sbuf_tensor("s_mv", [128, 2], F32))
    s_lnv = E(nc.sbuf_tensor("s_lnv", [128, 1], F32))
    s_rstd = E(nc.sbuf_tensor("s_rstd", [128, 1], F32))
    s_eps = E(nc.sbuf_tensor("s_eps", [128, 1], F32))
    U32 = mybir.dt.uint32
    s_cst = E(nc.sbuf_tensor("s_cst", [128, 146], F32))
    s_xnex = E(nc.sbuf_tensor("s_xnex", [128, NDK * CAP], BF16))
    s_xg = E(nc.sbuf_tensor("s_xg", [128, D], BF16))
    s_glst = E(nc.sbuf_tensor("s_glst", [128, 16], F32))
    s_gidx = E(nc.sbuf_tensor("s_gidx", [128, 8], U32))
    s_gw = E(nc.sbuf_tensor("s_gw", [128, 16], F32))
    s_i1 = E(nc.sbuf_tensor("s_i1", [128, 8], F32))
    s_i2 = E(nc.sbuf_tensor("s_i2", [128, 8], U32))
    s_oh1 = E(nc.sbuf_tensor("s_oh1", [128, 8], F32))
    s_oh2 = E(nc.sbuf_tensor("s_oh2", [128, 8], F32))
    s_me = E(nc.sbuf_tensor("s_me", [128, 8], F32))
    s_gif = s_me
    s_pe8 = E(nc.sbuf_tensor("s_pe8", [128, 8], F32))
    s_tmp8 = E(nc.sbuf_tensor("s_tmp8", [128, 8], F32))
    s_base = E(nc.sbuf_tensor("s_base", [1, 8], F32))
    s_fl1 = E(nc.sbuf_tensor("s_fl1", [128, 1], F32))
    s_fl2 = E(nc.sbuf_tensor("s_fl2", [128, 1], F32))
    s_bg1 = E(nc.sbuf_tensor("s_bg1", [128, 8], U32))
    s_bg2 = E(nc.sbuf_tensor("s_bg2", [128, 8], U32))
    s_sc1 = E(nc.sbuf_tensor("s_sc1", [128, 4], F32))
    s_sc2 = E(nc.sbuf_tensor("s_sc2", [128, 4], F32))
    s_eop = E(nc.sbuf_tensor("s_eop", [128, 2048], BF16))
    s_cs = s_eop[:, :].bitcast(F32)
    s_fill = E(nc.sbuf_tensor("s_fill", [128, 32], F32))

    psum = [E(nc.psum_tensor(f"p{i}", [128, 512], F32)) for i in range(8)]
    pbf = [psum[i][:, :].bitcast(BF16) for i in range(8)]

    tr = Tracker(nc)
    sem_names = ["se_pe", "se_dve", "se_act", "sq_wa0", "sq_wa1",
                 "sq_wh0", "sq_wh1", "sq_wo0", "sq_wo1", "sq_xa", "sq_xb",
                 "sq_st0", "sq_st1", "sq_m0", "sq_m1", "sq_m2", "sq_m3", "sq_m4",
                 "sq_m5", "sq_sc", "sq_ll", "sq_gx", "sq_bg", "sq_eo", "sq_xt",
                 "sq_lf"]
    cast_sems = [f"sq_c{i}" for i in range(29)]
    sem_names += cast_sems
    nc._sems = {}
    for s in sem_names:
        nc._sems[s] = E(nc.semaphore(s))
        tr.sem(s)
    tr.engine_sems({"tensor": "se_pe", "vector": "se_dve", "scalar": "se_act"})

    casts = [("zq", zq, wq_d, "scr_q"), ("zk", zk, wk_d, "scr_k"),
             ("zv", zv, wv_d, "scr_v"), ("zg", zg, wg_d, "scr_g")]
    for gi in range(4):
        casts.append((f"zgu_g{gi}", zgu[gi * MOE_I:(gi + 1) * MOE_I, :],
                      wgu_d[gi * MOE_I:(gi + 1) * MOE_I, :], f"scr_gu_g{gi}"))
        casts.append((f"zgu_u{gi}", zgu[SH_I + gi * MOE_I:SH_I + (gi + 1) * MOE_I, :],
                      wgu_d[SH_I + gi * MOE_I:SH_I + (gi + 1) * MOE_I, :], f"scr_gu_u{gi}"))
    casts.append(("zsd", zsd, wsd_d, "scr_sd"))
    for e in range(NE):
        casts.append((f"z13_{e}", z13[e], w13_d[e], f"scr_13_{e}"))
        casts.append((f"z2_{e}", z2[e], w2_d[e], f"scr_2_{e}"))
    assert len(casts) == 29, len(casts)
    def emit_cast(ci):
        nm, dst, src, buf = casts[ci]
        waits = [(cast_sems[ci - 2], 16)] if ci >= 2 else []
        tr.op("gpsimd", lambda g, dst=dst, src=src: g.dma_start(out=dst, in_=src),
              writes=(buf,), sem=cast_sems[ci], inc=16, manual_waits=waits)

    for ci in range(17):
        emit_cast(ci)

    def gu_src(g, half):
        if g < 4:
            return zgu, (SH_I if half else 0) + g * MOE_I, f"scr_gu_{'u' if half else 'g'}{g}"
        e = g - 4
        return z13[e], (MOE_I if half else 0), f"scr_13_{e}"

    def w2_src(g):
        if g < 4:
            return zsd, g * 1024, "scr_sd"
        return z2[g - 4], 0, f"scr_2_{g - 4}"

    def wrow(g):
        return 8 if g < 4 else g - 4

    tr.op("sync", lambda s: s.dma_start(out=s_xa[0:9, :], in_=gw9_d),
          writes=("xa",), sem="sq_m0", inc=16)
    tr.op("sync", lambda s: s.dma_start(out=s_out[0:1, 0:4 * D],
                                        in_=bqkvg_d.rearrange("j d -> (j d)")[None, :]),
          writes=("out",), sem="sq_m1", inc=16)
    tr.op("sync", lambda s: s.dma_start(out=s_sel[:, :], in_=sel_d[:, 8 * 128:9 * 128]),
          writes=("sel",), sem="sq_m2", inc=16)
    tr.op("sync", lambda s: s.dma_start(out=s_idf[:, :], in_=idf_d),
          writes=("idf",), sem="sq_m3", inc=16)
    tr.op("sync", lambda s: s.dma_start(out=s_idb[:, :], in_=idb_d),
          writes=("idb",), sem="sq_m4", inc=16)
    tr.op("sync", lambda s: s.dma_start(out=s_cst[:, :], in_=cst_d),
          writes=("cst",), sem="sq_m5", inc=16)
    tr.cop("vector", lambda v: v.memset(s_ones[:, :], 1.0), writes=("ones",))
    tr.cop("vector", lambda v: v.memset(s_carry[:, :], 0.0), writes=("carry",))
    tr.cop("vector", lambda v: v.memset(s_eps[:, :], EPS), writes=("eps",))
    tr.cop("vector", lambda v: v.memset(s_fill[:, :], PADV), writes=("fillc",))
    tr.cop("vector", lambda v: v.memset(s_xg[:, :], 0.0), writes=("xg0",))
    for dk in range(NDK):
        tr.cop("tensor",
               lambda t, dk=dk: t.matmul(psum[0][:, dk * 9:dk * 9 + 9],
                                         s_xa[0:9, dk * 128:(dk + 1) * 128],
                                         s_idf[0:9, 0:9], start=True, stop=True),
               reads=("xa", "idf"), writes=("p0",), close=(dk == NDK - 1))
    tr.cop("vector", lambda v: v.tensor_copy(out=s_gw9T[:, :], in_=psum[0][:, 0:NDK * 9]),
           reads=("p0",), writes=("gw9T",))
    for j in range(4):
        for c in range(NDK):
            tr.cop("tensor",
                   lambda t, j=j, c=c: t.matmul(
                       psum[1][:, j * NDK + c:j * NDK + c + 1],
                       s_out[0:1, j * D + c * 128: j * D + (c + 1) * 128],
                       s_idf[0:1, 0:1], start=True, stop=True),
                   reads=("out", "idf"), writes=("p1",),
                   close=(j == 3 and c == NDK - 1))
    tr.cop("vector", lambda v: v.tensor_copy(out=s_bias[:, :], in_=psum[1][:, 0:4 * NDK]),
           reads=("p1",), writes=("bias",))

    xslot = [("xa", "sq_xa", s_xa), ("xb", "sq_xb", s_xb)]
    n_store = 0

    for p in range(NPASS):
        t0 = p * TT
        for tc in range(TT // 128):
            sb, ssem, stile = xslot[tc % 2]
            row = t0 + tc * 128
            tr.op("sync",
                  lambda s, stile=stile, row=row: s.dma_start(
                      out=stile[:, :], in_=x_d[row:row + 128, :]),
                  writes=(sb,), sem=ssem, inc=16)
            for i in range(4):
                tr.cop("vector",
                       lambda v, i=i, stile=stile: v.bn_stats(
                           s_bn[:, i * 6:(i + 1) * 6], stile[:, i * 512:(i + 1) * 512]),
                       reads=(sb,), writes=("bn",))
            tr.cop("vector", lambda v: v.bn_aggr(s_mv[:, :], s_bn[:, :]),
                   reads=("bn",), writes=("mv",))
            tr.cop("scalar", lambda sc: sc.activation(out=s_lnv[:, :], in_=s_mv[:, 1:2],
                                                      func=AF.Ln, bias=s_eps[:, 0:1],
                                                      scale=1.0),
                   reads=("mv", "eps"), writes=("lnv",))
            tr.cop("scalar", lambda sc: sc.activation(out=s_rstd[:, :], in_=s_lnv[:, :],
                                                      func=AF.Exp, scale=-0.5),
                   reads=("lnv",), writes=("rstd",))
            tr.cop("vector",
                   lambda v, stile=stile: v.tensor_scalar(
                       out=s_xntm[:, :], in0=stile[:, :], scalar1=s_mv[:, 0:1],
                       scalar2=s_rstd[:, 0:1], op0=ALU.subtract, op1=ALU.mult),
                   reads=(sb, "mv", "rstd"), writes=("xntm",))
            tr.op("scalar",
                  lambda s, tc=tc: s.dma_start(
                      out=xtok_d[tc * 128:(tc + 1) * 128, :], in_=s_xntm[:, :]),
                  reads=("xntm",), writes=("xtok",), sem="sq_xt", inc=16)
            for quad in range(4):
                bank = quad % 2
                for dj in range(4):
                    dk = quad * 4 + dj
                    tr.cop("tensor",
                           lambda t, bank=bank, dj=dj, dk=dk: t.transpose(
                               pbf[bank][:, dj * 128:(dj + 1) * 128],
                               s_xntm[:, dk * 128:(dk + 1) * 128], s_idb[:, :]),
                           reads=("xntm", "idb"), writes=(f"p{bank}",),
                           close=(dj == 3))
                dk0 = quad * 4
                tr.cop("vector",
                       lambda v, bank=bank, dk0=dk0, tc=tc: v.tensor_copy(
                           out=s_xn[:, :].rearrange("p (k t) -> p k t", k=NDK)[
                               :, dk0:dk0 + 4, tc * 128:(tc + 1) * 128],
                           in_=pbf[bank][:, 0:512].rearrange("p (k t) -> p k t", k=4)),
                       reads=(f"p{bank}",), writes=("xn",))

        tr.cop("vector", lambda v: v.memset(s_base[:, :], 0.0), writes=("base",))
        for fh in range(2):
            tr.op("gpsimd", lambda g, fh=fh: g.dma_start(
                out=lists_d[fh * 2048:(fh + 1) * 2048, :].rearrange(
                    "(p a) b -> p (a b)", p=128),
                in_=s_fill[:, :]),
                  reads=("fillc",), writes=("lists",), sem="sq_lf", inc=16)
        for tc in range(TT // 128):
            bl = 6 + tc % 2
            bp = tc % 2
            bb = 4 + tc % 2
            for dk in range(NDK):
                tr.cop("tensor",
                       lambda t, tc=tc, dk=dk, bl=bl: t.matmul(
                           psum[bl][:, 0:9],
                           s_xn[:, dk * TT + tc * 128: dk * TT + (tc + 1) * 128],
                           s_gw9T[:, dk * 9:(dk + 1) * 9],
                           start=(dk == 0), stop=(dk == NDK - 1)),
                       reads=("xn", "gw9T"), writes=(f"p{bl}",), close=(dk == NDK - 1))
            tr.cop("vector", lambda v, bl=bl: v.max(s_i1[:, :], psum[bl][:, 0:8]),
                   reads=(f"p{bl}",), writes=("i1",))
            tr.cop("vector", lambda v, bl=bl: v.max_index(s_i2[:, :], s_i1[:, :], psum[bl][:, 0:8]),
                   reads=("i1", f"p{bl}"), writes=("i2",))
            tr.cop("vector", lambda v: v.tensor_copy(
                out=s_tmp8[:, 0:2], in_=s_i2[:, 0:2]),
                   reads=("i2",), writes=("tmp8",))
            tr.cop("vector", lambda v: v.tensor_single_scalar(
                out=s_oh1[:, :], in_=s_cst[:, 128:136], scalar=s_tmp8[:, 0:1],
                op=ALU.is_equal),
                   reads=("cst", "tmp8"), writes=("oh1",))
            tr.cop("vector", lambda v: v.tensor_single_scalar(
                out=s_oh2[:, :], in_=s_cst[:, 128:136], scalar=s_tmp8[:, 1:2],
                op=ALU.is_equal),
                   reads=("cst", "tmp8"), writes=("oh2",))
            tr.cop("vector", lambda v: v.tensor_tensor(
                out=s_me[:, :], in0=s_oh1[:, :], in1=s_oh2[:, :], op=ALU.add),
                   reads=("oh1", "oh2"), writes=("me",))
            tr.cop("vector", lambda v: v.tensor_scalar(
                out=s_negm1[:, :], in0=s_i1[:, 0:1], scalar1=-1.0, scalar2=None,
                op0=ALU.mult),
                   reads=("i1",), writes=("negm1",))
            tr.cop("scalar", lambda sc, bl=bl: sc.activation(
                out=s_exp[:, :], in_=psum[bl][:, 0:8], func=AF.Exp,
                bias=s_negm1[:, 0:1], scale=1.0),
                   reads=(f"p{bl}", "negm1"), writes=("exp",))
            tr.cop("vector", lambda v: v.reduce_sum(s_sum[:, :], s_exp[:, :], axis=AX.X),
                   reads=("exp",), writes=("sum",))
            tr.cop("vector", lambda v: v.reciprocal(s_rinv[:, :], s_sum[:, :]),
                   reads=("sum",), writes=("rinv",))
            tr.cop("vector", lambda v: v.tensor_scalar(
                out=s_wtmp[:, :], in0=s_exp[:, :], scalar1=s_rinv[:, 0:1], scalar2=None,
                op0=ALU.mult),
                   reads=("exp", "rinv"), writes=("wtmp",))
            tr.cop("vector", lambda v: v.tensor_tensor(
                out=s_w9tm[:, 0:8], in0=s_wtmp[:, :], in1=s_me[:, :], op=ALU.mult),
                   reads=("wtmp", "me"), writes=("w9tm",))
            tr.cop("scalar", lambda sc, bl=bl: sc.activation(
                out=s_m1[:, :], in_=psum[bl][:, 8:9], func=AF.Exp, scale=-1.0),
                   reads=(f"p{bl}",), writes=("m1",))
            tr.cop("vector", lambda v: v.tensor_scalar(
                out=s_m2[:, :], in0=s_m1[:, :], scalar1=1.0, scalar2=None, op0=ALU.add),
                   reads=("m1",), writes=("m2",))
            tr.cop("vector", lambda v: v.reciprocal(s_sum[:, :], s_m2[:, :]),
                   reads=("m2",), writes=("sum",))
            tr.cop("vector", lambda v: v.tensor_copy(out=s_w9tm[:, 8:9], in_=s_sum[:, :]),
                   reads=("sum",), writes=("w9tm",))
            tr.cop("tensor",
                   lambda t, tc=tc: t.transpose(
                       pbf[3][0:9, tc * 128:(tc + 1) * 128], s_w9tm[:, 0:9], s_idb[:, :]),
                   reads=("w9tm", "idb"), writes=("p3",))
            tr.cop("tensor", lambda t, bb=bb: t.matmul(
                psum[bb][0:1, 0:8], s_cst[:, 145:146], s_me[:, :],
                start=True, stop=True),
                   reads=("cst", "me"), writes=(f"p{bb}",))
            tr.cop("tensor", lambda t, bp=bp: t.matmul(
                psum[bp][:, 0:8], s_cst[:, 0:128], s_me[:, :],
                start=True, stop=False),
                   reads=("cst", "me"), writes=(f"p{bp}",), close=False)
            tr.cop("tensor", lambda t, bp=bp: t.matmul(
                psum[bp][:, 0:8], s_cst[0:1, 0:128], s_base[0:1, :],
                start=False, stop=True),
                   reads=("cst", "base"), writes=(f"p{bp}",))
            tr.cop("vector", lambda v, bb=bb: v.tensor_tensor(
                out=s_base[:, :], in0=s_base[:, :], in1=psum[bb][0:1, 0:8], op=ALU.add),
                   reads=("base", f"p{bb}"), writes=("base",))
            tr.cop("vector", lambda v, bp=bp: v.tensor_tensor(
                out=s_pe8[:, :], in0=psum[bp][:, 0:8], in1=s_cst[:, 136:144],
                op=ALU.add),
                   reads=(f"p{bp}", "cst"), writes=("pe8",))
            par = tc % 2
            for cls, oh, ohb, flt, flb, bgt, bgb, sct, scb in (
                    (1, s_oh1, "oh1", s_fl1, "fl1", s_bg1, "bg1", s_sc1, "sc1"),
                    (2, s_oh2, "oh2", s_fl2, "fl2", s_bg2, "bg2", s_sc2, "sc2")):
                tr.cop("vector", lambda v, oh=oh: v.tensor_tensor(
                    out=s_tmp8[:, :], in0=s_pe8[:, :], in1=oh[:, :], op=ALU.mult),
                       reads=("pe8", ohb), writes=("tmp8",))
                tr.cop("vector", lambda v, flt=flt: v.reduce_sum(
                    flt[:, :], s_tmp8[:, :], axis=AX.X),
                       reads=("tmp8",), writes=(flb,))
                tr.cop("vector", lambda v, flt=flt: v.tensor_scalar(
                    out=flt[:, :], in0=flt[:, :], scalar1=-1.0, scalar2=None,
                    op0=ALU.add),
                       reads=(flb,), writes=(flb,))
                tr.cop("vector", lambda v, flt=flt, bgt=bgt, tc=tc: v.tensor_copy(
                    out=bgt[:, tc:tc + 1], in_=flt[:, :]),
                       reads=(flb,), writes=(bgb,))
                tr.cop("vector", lambda v, sct=sct, par=par, tc=tc: v.tensor_scalar(
                    out=sct[:, 2 * par:2 * par + 1], in0=s_cst[:, 144:145],
                    scalar1=float(tc * 128), scalar2=None, op0=ALU.add),
                       reads=("cst",), writes=(scb,))
                tr.cop("vector", lambda v, oh=oh: v.tensor_tensor(
                    out=s_tmp8[:, :], in0=s_w9tm[:, 0:8], in1=oh[:, :], op=ALU.mult),
                       reads=("w9tm", ohb), writes=("tmp8",))
                tr.cop("vector", lambda v, sct=sct, par=par: v.reduce_sum(
                    sct[:, 2 * par + 1:2 * par + 2], s_tmp8[:, :], axis=AX.X),
                       reads=("tmp8",), writes=(scb,))
                tr.op("gpsimd",
                      lambda g, sct=sct, bgt=bgt, par=par, tc=tc: g.indirect_dma_start(
                          out=lists_d,
                          out_offset=bass.IndirectOffsetOnAxis(
                              ap=bgt[:, tc:tc + 1], axis=0),
                          in_=sct[:, 2 * par:2 * par + 2],
                          in_offset=None),
                      reads=(scb, bgb), writes=("lists",), sem="sq_sc", inc=16)
        tr.cop("vector", lambda v: v.tensor_copy(out=s_w9fm[:, :], in_=pbf[3][0:9, 0:TT]),
               reads=("p3",), writes=("w9fm",))

        wa = [("wa0", "sq_wa0", 0), ("wa1", "sq_wa1", 4096)]

        def fill_attn(c, half):
            buf, sm, base = wa[half]
            srcs = ((zq, "scr_q"), (zk, "scr_k")) if half == 0 else ((zv, "scr_v"), (zg, "scr_g"))
            for pi, (zsrc, zb) in enumerate(srcs):
                off = base + pi * D
                tr.op("sync",
                      lambda s, zsrc=zsrc, c=c, off=off: s.dma_start(
                          out=s_ha[:, off:off + D],
                          in_=zsrc[c * 128:(c + 1) * 128, :]),
                      reads=(zb,), writes=(buf, "hA"), sem=sm, inc=16)

        fill_attn(0, 0)
        fill_attn(0, 1)
        for c in range(NDK):
            for pi, (buf, base) in enumerate(
                    (("wa0", 0), ("wa0", 2048), ("wa1", 4096), ("wa1", 6144))):
                for th in range(2):
                    bank = pi * 2 + th
                    for dk in range(NDK):
                        tr.cop("tensor",
                               lambda t, base=base, bank=bank, dk=dk, th=th: t.matmul(
                                   psum[bank][:, :],
                                   s_ha[:, base + dk * 128: base + (dk + 1) * 128],
                                   s_xn[:, dk * TT + th * 512: dk * TT + (th + 1) * 512],
                                   start=(dk == 0), stop=(dk == NDK - 1)),
                               reads=(buf, "xn"), writes=(f"p{bank}",),
                               close=(dk == NDK - 1))
                if pi == 1 and c < NDK - 1:
                    fill_attn(c + 1, 0)
                if pi == 3 and c < NDK - 1:
                    fill_attn(c + 1, 1)
            for th in range(2):
                tr.cop("scalar",
                       lambda sc, c=c, th=th: sc.activation(
                           out=s_q[:, th * 512:(th + 1) * 512], in_=psum[th][:, :],
                           func=AF.Identity, bias=s_bias[:, c:c + 1], scale=1.0),
                       reads=(f"p{th}", "bias"), writes=("q", "hB"))
                tr.cop("scalar",
                       lambda sc, c=c, th=th: sc.activation(
                           out=s_k[:, th * 512:(th + 1) * 512], in_=psum[2 + th][:, :],
                           func=AF.Identity, bias=s_bias[:, NDK + c:NDK + c + 1],
                           scale=1.0),
                       reads=(f"p{2 + th}", "bias"), writes=("k", "hB"))
                tr.cop("scalar",
                       lambda sc, c=c, th=th: sc.activation(
                           out=s_g[:, th * 512:(th + 1) * 512], in_=psum[6 + th][:, :],
                           func=AF.Sigmoid, bias=s_bias[:, 3 * NDK + c:3 * NDK + c + 1],
                           scale=1.0),
                       reads=(f"p{6 + th}", "bias"), writes=("g", "hB"))
                tr.cop("vector",
                       lambda v, c=c, th=th: v.scalar_tensor_tensor(
                           out=s_kv[:, th * 512:(th + 1) * 512], in0=psum[4 + th][:, :],
                           scalar=s_bias[:, 2 * NDK + c:2 * NDK + c + 1],
                           in1=s_k[:, th * 512:(th + 1) * 512],
                           op0=ALU.add, op1=ALU.mult),
                       reads=(f"p{4 + th}", "bias", "k"), writes=("kv", "utt"))
            tr.cop("vector",
                   lambda v, c=c: v.tensor_tensor_scan(
                       out=s_cs[:, :], data0=s_ones[:, :], data1=s_kv[:, :],
                       initial=s_carry[:, c:c + 1], op0=ALU.mult, op1=ALU.add),
                   reads=("ones", "kv", "utt", "carry"), writes=("cs", "eop0", "eop1"))
            tr.cop("vector",
                   lambda v, c=c: v.tensor_copy(
                       out=s_carry[:, c:c + 1], in_=s_cs[:, TT - 1:TT]),
                   reads=("cs",), writes=("carry",))
            tr.cop("vector", lambda v: v.tensor_tensor(
                out=s_qg[:, :], in0=s_q[:, :], in1=s_g[:, :], op=ALU.mult),
                   reads=("q", "g"), writes=("qg", "sgt"))
            tr.cop("vector",
                   lambda v, c=c: v.tensor_tensor(
                       out=s_out[:, c * TT:(c + 1) * TT], in0=s_qg[:, :],
                       in1=s_cs[:, :], op=ALU.mult),
                   reads=("qg", "sgt", "cs"), writes=("out",))

        wbs = [("wb0", s_wb0), ("wb1", s_wb1)]
        whs = [("wh0", "sq_wh0", 0), ("wh1", "sq_wh1", 4096)]
        wos = [("wo0", "sq_wo0", s_wo0), ("wo1", "sq_wo1", s_wo1)]

        def fill_wh(g, i, slot):
            buf, sm, base = whs[slot]
            for half in range(2):
                zsrc, rbase, zb = gu_src(g, half)
                r0 = rbase + i * 128
                off = base + half * D
                tr.op("scalar",
                      lambda s, zsrc=zsrc, r0=r0, off=off: s.dma_start(
                          out=s_wh[:, off:off + D],
                          in_=zsrc[r0:r0 + 128, :]),
                      reads=(zb,), writes=(buf,), sem=sm, inc=16)

        def fill_wo(g, ec, slot):
            buf, sm, tile = wos[slot]
            zsrc, rbase, zb = w2_src(g)
            tr.op("sync",
                  lambda s, zsrc=zsrc, rbase=rbase, ec=ec, tile=tile:
                  s.dma_start(
                      out=tile[:, :],
                      in_=zsrc[rbase + ec * 128: rbase + (ec + 1) * 128, :]),
                  reads=(zb,), writes=(buf,), sem=sm, inc=16)

        def make_wb(g, slot):
            wbuf, wtile = wbs[slot]
            r = wrow(g)
            for th in range(2):
                tr.cop("tensor",
                       lambda t, r=r, th=th: t.matmul(
                           psum[6][:, :], s_sel[:, 0:128],
                           s_w9fm[0:9, th * 512:(th + 1) * 512],
                           start=True, stop=True),
                       reads=("sel", "w9fm"), writes=("p6",), close=True)
                tr.cop("scalar",
                       lambda sc, wtile=wtile, th=th: sc.activation(
                           out=wtile[:, th * 512:(th + 1) * 512], in_=psum[6][:, :],
                           func=AF.Identity, scale=1.0),
                       reads=("p6",), writes=(wbuf,))

        def load_lists(e):
            b8 = (e % 2) * 8
            b4 = (e % 2) * 4
            tr.op("scalar",
                  lambda s, e=e, b8=b8: s.dma_start(
                      out=s_glst[:, b8:b8 + 8].rearrange("p (a b) -> p a b", b=2),
                      in_=lists_d[e * CAP:(e + 1) * CAP, :].rearrange(
                          "(a p) b -> p a b", p=128)),
                  reads=("lists",), writes=("glst",), sem="sq_ll", inc=16)
            tr.cop("vector",
                   lambda v, b8=b8, b4=b4: v.tensor_copy(
                       out=s_gif[:, b4:b4 + 4].rearrange("p (a c) -> p a c", c=1),
                       in_=s_glst[:, b8:b8 + 8].rearrange(
                           "p (a c) -> p a c", c=2)[:, :, 0:1]),
                   reads=("glst",), writes=("gif",))
            tr.cop("vector",
                   lambda v, b4=b4: v.tensor_scalar(
                       out=s_gif[:, b4:b4 + 4], in0=s_gif[:, b4:b4 + 4],
                       scalar1=float(TT), scalar2=None, op0=ALU.min),
                   reads=("gif",), writes=("gif",))
            tr.cop("vector",
                   lambda v, b4=b4: v.tensor_copy(
                       out=s_gidx[:, b4:b4 + 4], in_=s_gif[:, b4:b4 + 4]),
                   reads=("gif",), writes=("gidx",))
            b4w = (e % 4) * 4
            tr.cop("vector",
                   lambda v, b8=b8, b4w=b4w: v.tensor_copy(
                       out=s_gw[:, b4w:b4w + 4].rearrange("p (a c) -> p a c", c=1),
                       in_=s_glst[:, b8:b8 + 8].rearrange(
                           "p (a c) -> p a c", c=2)[:, :, 1:2]),
                   reads=("glst",), writes=("gw",))

        def gather_tile(e, jt):
            b4 = (e % 2) * 4
            tr.op("gpsimd",
                  lambda g, b4=b4, jt=jt: g.indirect_dma_start(
                      out=s_xg[:, :],
                      out_offset=None,
                      in_=xtok_d,
                      in_offset=bass.IndirectOffsetOnAxis(
                          ap=s_gidx[:, b4 + jt:b4 + jt + 1], axis=0)),
                  reads=("gidx", "xtok"), writes=("xg0",), sem="sq_gx", inc=16)

        def transpose_tile(e, jt):
            for hb in range(2):
                bank = 6 + hb
                for dj in range(8):
                    dk = hb * 8 + dj
                    tr.cop("tensor",
                           lambda t, dk=dk, dj=dj, bank=bank: t.transpose(
                               pbf[bank][:, dj * 128:(dj + 1) * 128],
                               s_xg[:, dk * 128:(dk + 1) * 128],
                               s_idb[:, :]),
                           reads=("xg0", "idb"), writes=(f"p{bank}",),
                           close=(dj == 7))
                tr.cop("vector",
                       lambda v, bank=bank, hb=hb, jt=jt: v.tensor_copy(
                           out=s_xnex[:, :].rearrange("p (k s) -> p k s", k=NDK)[
                               :, hb * 8:hb * 8 + 8, jt * 128:(jt + 1) * 128],
                           in_=pbf[bank][:, 0:1024].rearrange("p (k s) -> p k s", k=8)),
                       reads=(f"p{bank}",), writes=("xnex",))

        make_wb(0, 0)
        gchunk = 0
        fill_wh(0, 0, 0)
        for g in range(NGS):
            wbuf, wtile = wbs[g % 2]
            for i in range(NI):
                whbuf, whsm, whbase = whs[gchunk % 2]
                if i + 1 < NI:
                    fill_wh(g, i + 1, (gchunk + 1) % 2)
                elif g + 1 < NG:
                    fill_wh(g + 1, 0, (gchunk + 1) % 2)
                gchunk += 1
                if p == 0 and g >= 1 and i == 0:
                    emit_cast(15 + 2 * g)
                    emit_cast(16 + 2 * g)
                if g == 2 and i == 0:
                    load_lists(0)
                if g == 2 and i == 5:
                    load_lists(1)
                if g == 3 and i < 8:
                    if i % 2 == 0:
                        gather_tile(0, i // 2)
                    else:
                        transpose_tile(0, i // 2)
                for half in range(2):
                    for th in range(2):
                        bank = half * 2 + th
                        for dk in range(NDK):
                            tr.cop("tensor",
                                   lambda t, whbase=whbase, half=half, th=th,
                                   bank=bank, dk=dk: t.matmul(
                                       psum[bank][:, :],
                                       s_wh[:, whbase + half * 2048 + dk * 128:
                                            whbase + half * 2048 + (dk + 1) * 128],
                                       s_xn[:, dk * TT + th * 512:
                                            dk * TT + (th + 1) * 512],
                                       start=(dk == 0), stop=(dk == NDK - 1)),
                                   reads=(whbuf, "xn"), writes=(f"p{bank}",),
                                   close=(dk == NDK - 1))
                for th in range(2):
                    tr.cop("scalar",
                           lambda sc, th=th: sc.activation(
                               out=s_sg[:, th * 512:(th + 1) * 512], in_=psum[th][:, :],
                               func=AF.Silu, scale=1.0),
                           reads=(f"p{th}",), writes=("sgt",))
                    tr.cop("vector",
                           lambda v, th=th, wtile=wtile: v.tensor_tensor(
                               out=s_ut[:, th * 512:(th + 1) * 512],
                               in0=psum[2 + th][:, :],
                               in1=wtile[:, th * 512:(th + 1) * 512], op=ALU.mult),
                           reads=(f"p{2 + th}", wbuf), writes=("utt",))
                tr.cop("vector",
                       lambda v, i=i: v.tensor_tensor(
                           out=s_ha[:, i * 1024:(i + 1) * 1024], in0=s_sg[:, :],
                           in1=s_ut[:, :], op=ALU.mult),
                       reads=("sgt", "utt"), writes=("hA" if i < 8 else "hB",))
            if g + 1 < NGS:
                make_wb(g + 1, (g + 1) % 2)
            fill_wo(g, 0, 0)
            for ec in range(8):
                wobuf, wosm, wotile = wos[ec % 2]
                if ec + 1 < 8:
                    fill_wo(g, ec + 1, (ec + 1) % 2)
                for el in range(2):
                    ek = ec * 2 + el
                    for th in range(2):
                        bank = 4 + (ek * 2 + th) % 2
                        for i in range(NI):
                            tr.cop("tensor",
                                   lambda t, wotile=wotile, i=i, el=el, th=th,
                                   bank=bank: t.matmul(
                                       psum[bank][:, :],
                                       wotile[:, i * 256 + el * 128:
                                              i * 256 + (el + 1) * 128],
                                       s_ha[:, i * 1024 + th * 512:
                                            i * 1024 + (th + 1) * 512],
                                       start=(i == 0), stop=(i == NI - 1)),
                                   reads=(wobuf, "hA" if i < 8 else "hB"),
                                   writes=(f"p{bank}",),
                                   close=(i == NI - 1))
                        tr.cop("vector",
                               lambda v, ek=ek, th=th, bank=bank: v.tensor_tensor(
                                   out=s_out[:, ek * TT + th * 512:
                                             ek * TT + (th + 1) * 512],
                                   in0=psum[bank][:, :],
                                   in1=s_out[:, ek * TT + th * 512:
                                             ek * TT + (th + 1) * 512],
                                   op=ALU.add),
                               reads=(f"p{bank}", "out"), writes=("out",))
        for e in range(NE):
            g = NGS + e
            b4w = (e % 4) * 4
            for i in range(NI):
                whbuf, whsm, whbase = whs[gchunk % 2]
                if i + 1 < NI:
                    fill_wh(g, i + 1, (gchunk + 1) % 2)
                elif g + 1 < NG:
                    fill_wh(g + 1, 0, (gchunk + 1) % 2)
                gchunk += 1
                if p == 0 and e in (0, 1, 2) and i == 0:
                    emit_cast(23 + 2 * e)
                    emit_cast(24 + 2 * e)
                if e + 2 < NE and i == 1:
                    load_lists(e + 2)
                for half in range(2):
                    for dk in range(NDK):
                        tr.cop("tensor",
                               lambda t, whbase=whbase, half=half, dk=dk: t.matmul(
                                   psum[half][:, :],
                                   s_wh[:, whbase + half * 2048 + dk * 128:
                                        whbase + half * 2048 + (dk + 1) * 128],
                                   s_xnex[:, dk * CAP:(dk + 1) * CAP],
                                   start=(dk == 0), stop=(dk == NDK - 1)),
                               reads=(whbuf, "xnex"), writes=(f"p{half}",),
                               close=(dk == NDK - 1))
                tr.cop("scalar",
                       lambda sc: sc.activation(
                           out=s_sg[:, 0:512], in_=psum[0][:, :],
                           func=AF.Silu, scale=1.0),
                       reads=("p0",), writes=("sgt",))
                tr.cop("vector",
                       lambda v: v.tensor_copy(
                           out=s_ut[:, 0:512], in_=psum[1][:, :]),
                       reads=("p1",), writes=("utt",))
                tr.cop("vector",
                       lambda v, i=i: v.tensor_tensor(
                           out=s_ha[:, i * 512:(i + 1) * 512], in0=s_sg[:, 0:512],
                           in1=s_ut[:, 0:512], op=ALU.mult),
                       reads=("sgt", "utt"), writes=("hA",))
            fill_wo(g, 0, 0)
            for ec in range(8):
                wobuf, wosm, wotile = wos[ec % 2]
                if ec + 1 < 8:
                    fill_wo(g, ec + 1, (ec + 1) % 2)
                if e + 1 < NE:
                    if ec % 2 == 0:
                        gather_tile(e + 1, ec // 2)
                    else:
                        transpose_tile(e + 1, ec // 2)
                for i in range(NI):
                    for jt in range(CT):
                        tr.cop("tensor",
                               lambda t, wotile=wotile, i=i, jt=jt: t.matmul(
                                   psum[2 + jt][:, 0:256],
                                   s_ha[:, i * 512 + jt * 128:
                                        i * 512 + (jt + 1) * 128],
                                   wotile[:, i * 256:(i + 1) * 256],
                                   start=(i == 0), stop=(i == NI - 1)),
                               reads=(wobuf, "hA"), writes=(f"p{2 + jt}",),
                               close=(i == NI - 1))
                for jt in range(CT):
                    tr.cop("scalar",
                           lambda sc, jt=jt, ec=ec, b4w=b4w: sc.activation(
                               out=s_eop[:, (ec % 2) * 1024 + jt * 256:
                                         (ec % 2) * 1024 + (jt + 1) * 256],
                               in_=psum[2 + jt][:, 0:256], func=AF.Identity,
                               scale=s_gw[:, b4w + jt:b4w + jt + 1]),
                           reads=(f"p{2 + jt}", "gw"), writes=(f"eop{ec % 2}",))
                    tr.op("scalar",
                          lambda s, e=e, jt=jt, ec=ec: s.dma_start(
                              out=eout_d[e * CAP + jt * 128: e * CAP + (jt + 1) * 128,
                                         ec * 256:(ec + 1) * 256],
                              in_=s_eop[:, (ec % 2) * 1024 + jt * 256:
                                        (ec % 2) * 1024 + (jt + 1) * 256]),
                          reads=(f"eop{ec % 2}",), writes=("eout",), sem="sq_eo", inc=16)
        def bg_slot(slot):
            if slot < 4:
                return s_xnex[:, slot * 2048:(slot + 1) * 2048]
            return s_ha[:, (slot - 4) * 2048:(slot - 3) * 2048]

        def back_gather(tc, cls):
            slot = (2 * tc + cls) % 8
            st = bg_slot(slot)
            bgt, bgb = (s_bg1, "bg1") if cls == 0 else (s_bg2, "bg2")
            tr.op("gpsimd",
                  lambda g, st=st, bgt=bgt, tc=tc: g.indirect_dma_start(
                      out=st, out_offset=None, in_=eout_d,
                      in_offset=bass.IndirectOffsetOnAxis(
                          ap=bgt[:, tc:tc + 1], axis=0)),
                  reads=(bgb, "eout"),
                  writes=(f"bgs{slot}", "xnex" if slot < 4 else "hA"),
                  sem="sq_bg", inc=16)

        for tc in range(4):
            back_gather(tc, 0)
            back_gather(tc, 1)
        otm = [(s_wh[:, 0:4096].bitcast(F32), ("wh0",)),
               (s_wh[:, 4096:8192].bitcast(F32), ("wh1",))]
        for tc in range(TT // 128):
            sb, ssem, stile = xslot[tc % 2]
            row = t0 + tc * 128
            tr.op("scalar",
                  lambda s, stile=stile, row=row: s.dma_start(
                      out=stile[:, :], in_=x_d[row:row + 128, :]),
                  writes=(sb,), sem=ssem, inc=16)
            ot, otbufs = otm[tc % 2]
            slot1 = (2 * tc) % 8
            slot2 = (2 * tc + 1) % 8
            g1, g2 = bg_slot(slot1), bg_slot(slot2)
            for quad in range(4):
                bank = quad + 4 * (tc % 2)
                for ej in range(4):
                    ek = quad * 4 + ej
                    tr.cop("tensor",
                           lambda t, bank=bank, ej=ej, ek=ek, tc=tc: t.transpose(
                               psum[bank][:, ej * 128:(ej + 1) * 128],
                               s_out[:, ek * TT + tc * 128: ek * TT + (tc + 1) * 128],
                               s_idf[:, :]),
                           reads=("out", "idf"), writes=(f"p{bank}",),
                           close=False)
                    tr.cop("tensor",
                           lambda t, bank=bank, ej=ej, quad=quad, g1=g1: t.matmul(
                               psum[bank][:, ej * 128:(ej + 1) * 128],
                               s_idb[:, 0:128],
                               g1[:, quad * 512 + ej * 128: quad * 512 + (ej + 1) * 128],
                               start=False, stop=False, skip_group_check=True),
                           reads=(f"bgs{slot1}", "idb",
                                  "xnex" if slot1 < 4 else "hA"),
                           writes=(f"p{bank}",), close=False)
                    tr.cop("tensor",
                           lambda t, bank=bank, ej=ej, quad=quad, g2=g2: t.matmul(
                               psum[bank][:, ej * 128:(ej + 1) * 128],
                               s_idb[:, 0:128],
                               g2[:, quad * 512 + ej * 128: quad * 512 + (ej + 1) * 128],
                               start=False, stop=(ej == 3), skip_group_check=True),
                           reads=(f"bgs{slot2}", "idb",
                                  "xnex" if slot2 < 4 else "hA"),
                           writes=(f"p{bank}",), close=(ej == 3))
                tr.cop("vector",
                       lambda v, bank=bank, quad=quad, ot=ot, stile=stile:
                       v.tensor_tensor(
                           out=ot[:, quad * 512:(quad + 1) * 512],
                           in0=psum[bank][:, 0:512],
                           in1=stile[:, quad * 512:(quad + 1) * 512], op=ALU.add),
                       reads=(f"p{bank}", sb), writes=otbufs)
            tr.op("sync" if tc % 2 == 0 else "scalar",
                  lambda s, ot=ot, row=row: s.dma_start(
                      out=out_d[row:row + 128, :], in_=ot[:, :]),
                  reads=otbufs, writes=(f"sto{tc % 2}",),
                  sem=f"sq_st{tc % 2}", inc=16)
            n_store += 1
            if tc + 4 < TT // 128:
                back_gather(tc + 4, 0)
                back_gather(tc + 4, 1)

    tr.barrier_wait("vector", "sq_st0", 16 * (n_store // 2))
    tr.barrier_wait("vector", "sq_st1", 16 * (n_store // 2))

    block = E(nc.Block())
    tr.replay(block)
    ctx.close()
    return nc


_NC_CACHE = {}


def _get_nc():
    if "nc" not in _NC_CACHE:
        _NC_CACHE["nc"] = build_nc()
    return _NC_CACHE["nc"]


NEH = 8


def make_in_maps(inputs):
    x = np.ascontiguousarray(np.asarray(inputs["L0_attn_x"], dtype=np.float32))
    bqkvg = np.stack([np.asarray(inputs["L0_attn_bq"]), np.asarray(inputs["L0_attn_bk"]),
                      np.asarray(inputs["L0_attn_bv"]), np.asarray(inputs["L0_attn_bg"])],
                     axis=0)
    bqkvg = np.ascontiguousarray(bqkvg, dtype=np.float32)
    gw9 = np.concatenate([np.asarray(inputs["L0_ffn_gate_w"]),
                          np.asarray(inputs["L0_ffn_shared_gate_w"])], axis=0)
    gw9 = np.ascontiguousarray(gw9, dtype=np.float32)
    idf = np.eye(128, dtype=np.float32)
    idb = np.eye(128).astype(ml_dtypes.bfloat16)
    cst = np.zeros((128, 146), dtype=np.float32)
    cst[:, 0:128] = np.triu(np.ones((128, 128), dtype=np.float32))
    cst[:, 128:136] = np.arange(8, dtype=np.float32)[None, :]
    cst[:, 136:144] = 512.0 * np.arange(8, dtype=np.float32)[None, :]
    cst[:, 144] = np.arange(128, dtype=np.float32)
    cst[:, 145] = 1.0
    sel = np.zeros((9, 9, 128), dtype=np.float32)
    for r in range(9):
        sel[r, r, :] = 1.0
    sel = np.ascontiguousarray(sel.reshape(9, 9 * 128)).astype(ml_dtypes.bfloat16)
    def sw_up(w, rc):
        w = np.asarray(w, dtype=np.float32)
        return np.ascontiguousarray(
            w.reshape(rc, 128, 16, 128).transpose(0, 3, 2, 1).reshape(rc * 128, 2048))

    w2f = np.asarray(inputs["L0_ffn_w2"], dtype=np.float32)
    w2sw = np.ascontiguousarray(
        w2f.reshape(NEH, 8, 2, 128, 11, 128).transpose(0, 1, 5, 4, 2, 3)
        .reshape(NEH, 1024, 2816))
    wsdf = np.asarray(inputs["L0_ffn_shared_down_w"], dtype=np.float32)
    wsdsw = np.ascontiguousarray(
        wsdf.reshape(8, 2, 128, 4, 11, 128).transpose(3, 0, 5, 4, 1, 2)
        .reshape(4 * 1024, 2816))
    common = dict(
        wq=sw_up(inputs["L0_attn_Wq"], 16),
        wk=sw_up(inputs["L0_attn_Wk"], 16),
        wv=sw_up(inputs["L0_attn_Wv"], 16),
        wg=sw_up(inputs["L0_attn_Wg"], 16),
        bqkvg=bqkvg, gw9=gw9,
        wgu=sw_up(inputs["L0_ffn_shared_gate_up_w"], 88),
        wsd=wsdsw,
        w13=np.ascontiguousarray(
            np.asarray(inputs["L0_ffn_w13"], dtype=np.float32)
            .reshape(NEH, 22, 128, 16, 128).transpose(0, 1, 4, 3, 2)
            .reshape(NEH, 2816, 2048)),
        w2=w2sw,
        idf=idf, idb=idb, sel=sel, cst=cst,
    )
    n = x.shape[0]
    return [dict(common, x=np.ascontiguousarray(x[i])) for i in range(n)]


def kernel(**inputs):
    in_maps = make_in_maps(inputs)
    nc = _get_nc()
    res = run_bass_kernel_spmd(nc, in_maps, core_ids=list(range(8)))
    return np.stack([r["out"] for r in res.results], axis=0)


# version 06
# speedup vs baseline: 1.0044x; 1.0044x over previous
import numpy as np
import ml_dtypes
import concourse.bass as bass
import concourse.mybir as mybir
from concourse.bass_utils import run_bass_kernel_spmd

F32 = mybir.dt.float32
BF16 = mybir.dt.bfloat16
AF = mybir.ActivationFunctionType
ALU = mybir.AluOpType
AX = mybir.AxisListType

S = 4096
D = 2048
TT = 1024
NPASS = S // TT
NDK = D // 128
MOE_I = 1408
SH_I = 5632
NE = 8
NG = 12
NGS = 4
NI = MOE_I // 128
EPS = 1e-5
CAP = 512
CT = CAP // 128
PADV = 1.0e6

ENGS = ("tensor", "vector", "scalar", "sync", "gpsimd")
GAP = 8


class Tracker:

    def __init__(self, nc):
        self.nc = nc
        self.q = {e: [] for e in ENGS}
        self.cnt = {}
        self.bufs = {}
        self.opidx = {e: 0 for e in ENGS}
        self.waited = {e: {} for e in ENGS}
        self.pending = []
        self.esem = {}

    def sem(self, name):
        self.cnt.setdefault(name, 0)
        return name

    def engine_sems(self, names):
        for e, s in names.items():
            self.esem[e] = self.sem(s)

    def _want(self, need, sem, val):
        if need.get(sem, 0) < val:
            need[sem] = val

    def _edge(self, eng, weng, widx, wdma):
        if weng is None:
            return False
        if wdma or weng != eng:
            return True
        return (self.opidx[eng] - widx) < GAP

    def op(self, eng, fn, reads=(), writes=(), sem=None, inc=0, close=True,
           manual_waits=()):
        need = {}
        for (msem, mval) in manual_waits:
            self._want(need, msem, mval)
        for b in reads:
            rec = self.bufs.get(b)
            if rec and rec["w"] and self._edge(eng, rec["weng"], rec["widx"], rec["wdma"]):
                self._want(need, rec["w"][0], rec["w"][1])
        for b in writes:
            rec = self.bufs.get(b)
            if rec:
                if rec["w"] and self._edge(eng, rec["weng"], rec["widx"], rec["wdma"]):
                    self._want(need, rec["w"][0], rec["w"][1])
                for rsem, (rcnt, reng, ridx, rdma) in rec["r"].items():
                    if self._edge(eng, reng, ridx, rdma):
                        self._want(need, rsem, rcnt)
        for s, v in sorted(need.items()):
            if self.waited[eng].get(s, 0) < v:
                self.q[eng].append(("wait", s, v))
                self.waited[eng][s] = v
        idx = self.opidx[eng]
        self.opidx[eng] += 1
        is_dma = inc >= 16
        ev = None
        if inc:
            self.cnt[sem] += inc
            ev = (sem, self.cnt[sem])
        self.q[eng].append(("op", fn, sem if inc else None, inc))
        if not close:
            self.pending.append((eng, idx, tuple(reads), tuple(writes)))
            return None
        if ev:
            for (peng, pidx, preads, pwrites) in self.pending:
                if peng != eng:
                    continue
                for b in preads:
                    self._note_read(b, ev, peng, pidx, is_dma)
                for b in pwrites:
                    self._note_write(b, ev, peng, pidx, is_dma)
            self.pending = [p for p in self.pending if p[0] != eng]
        for b in reads:
            self._note_read(b, ev, eng, idx, is_dma)
        for b in writes:
            self._note_write(b, ev, eng, idx, is_dma)
        return ev

    def _note_read(self, b, ev, eng, idx, dma):
        rec = self.bufs.setdefault(
            b, {"w": None, "weng": None, "widx": -1 << 30, "wdma": False, "r": {}})
        if ev is None:
            return
        rec["r"][ev[0]] = (ev[1], eng, idx, dma)

    def _note_write(self, b, ev, eng, idx, dma):
        self.bufs[b] = {"w": ev, "weng": eng, "widx": idx, "wdma": dma, "r": {}}

    def cop(self, eng, fn, reads=(), writes=(), close=True, manual_waits=()):
        return self.op(eng, fn, reads, writes, sem=self.esem[eng],
                       inc=1 if close else 0, close=close,
                       manual_waits=manual_waits)

    def barrier_wait(self, eng, sem, val):
        if self.waited[eng].get(sem, 0) < val:
            self.q[eng].append(("wait", sem, val))
            self.waited[eng][sem] = val

    def replay(self, block):
        for name in ENGS:
            items = self.q[name]

            def make(items=items):
                def body(eng):
                    for it in items:
                        if it[0] == "wait":
                            eng.wait_ge(self.nc._sems[it[1]], it[2])
                        else:
                            _, fn, sm, inc = it
                            ins = fn(eng)
                            if sm is not None:
                                ins.then_inc(self.nc._sems[sm], inc)
                return body

            getattr(block, name)(make())


def build_nc():
    nc = bass.Bass("TRN2", target_bir_lowering=False)

    x_d = nc.dram_tensor("x", [S, D], F32, kind="ExternalInput").ap()
    wq_d = nc.dram_tensor("wq", [D, D], F32, kind="ExternalInput").ap()
    wk_d = nc.dram_tensor("wk", [D, D], F32, kind="ExternalInput").ap()
    wv_d = nc.dram_tensor("wv", [D, D], F32, kind="ExternalInput").ap()
    wg_d = nc.dram_tensor("wg", [D, D], F32, kind="ExternalInput").ap()
    bqkvg_d = nc.dram_tensor("bqkvg", [4, D], F32, kind="ExternalInput").ap()
    gw9_d = nc.dram_tensor("gw9", [9, D], F32, kind="ExternalInput").ap()
    wgu_d = nc.dram_tensor("wgu", [2 * SH_I, D], F32, kind="ExternalInput").ap()
    wsd_d = nc.dram_tensor("wsd", [4 * 1024, 2816], F32, kind="ExternalInput").ap()
    w13_d = nc.dram_tensor("w13", [NE, 2 * MOE_I, D], F32, kind="ExternalInput").ap()
    w2_d = nc.dram_tensor("w2", [NE, 1024, 2816], F32, kind="ExternalInput").ap()
    idf_d = nc.dram_tensor("idf", [128, 128], F32, kind="ExternalInput").ap()
    idb_d = nc.dram_tensor("idb", [128, 128], BF16, kind="ExternalInput").ap()
    sel_d = nc.dram_tensor("sel", [9, 9 * 128], BF16, kind="ExternalInput").ap()
    cst_d = nc.dram_tensor("cst", [128, 138], F32, kind="ExternalInput").ap()
    out_d = nc.dram_tensor("out", [S, D], F32, kind="ExternalOutput").ap()

    lists_d = nc.dram_tensor("lists", [NE * CAP, 2], F32, kind="Internal").ap()
    eout_d = nc.dram_tensor("eout", [NE * CAP, D], BF16, kind="Internal").ap()
    xtok_d = nc.dram_tensor("xtok", [TT + 128, D], BF16, kind="Internal").ap()

    zq = nc.dram_tensor("zq", [D, D], BF16, kind="Internal").ap()
    zk = nc.dram_tensor("zk", [D, D], BF16, kind="Internal").ap()
    zv = nc.dram_tensor("zv", [D, D], BF16, kind="Internal").ap()
    zg = nc.dram_tensor("zg", [D, D], BF16, kind="Internal").ap()
    zgu = nc.dram_tensor("zgu", [2 * SH_I, D], BF16, kind="Internal").ap()
    zsd = nc.dram_tensor("zsd", [4 * 1024, 2816], BF16, kind="Internal").ap()
    z13 = nc.dram_tensor("z13", [NE, 2 * MOE_I, D], BF16, kind="Internal").ap()
    z2 = nc.dram_tensor("z2", [NE, 1024, 2816], BF16, kind="Internal").ap()

    from contextlib import ExitStack
    ctx = ExitStack()
    E = ctx.enter_context

    s_xn = E(nc.sbuf_tensor("s_xn", [128, NDK * TT], BF16))
    s_out = E(nc.sbuf_tensor("s_out", [128, NDK * TT], F32))
    s_ha = E(nc.sbuf_tensor("s_ha", [128, 11264], BF16))
    s_q = s_ha[:, 8192:8192 + TT]
    s_k = s_ha[:, 9216:9216 + TT]
    s_g = s_ha[:, 10240:10240 + TT]
    s_wh = E(nc.sbuf_tensor("s_wh", [128, 8192], BF16))
    s_wo0 = E(nc.sbuf_tensor("s_wo0", [128, NI * 256], BF16))
    s_wo1 = E(nc.sbuf_tensor("s_wo1", [128, NI * 256], BF16))
    s_xa = E(nc.sbuf_tensor("s_xa", [128, D], F32))
    s_xb = E(nc.sbuf_tensor("s_xb", [128, D], F32))
    s_xntm = E(nc.sbuf_tensor("s_xntm", [128, D], BF16))
    s_sg = E(nc.sbuf_tensor("s_sg", [128, TT], BF16))
    s_qg = s_sg
    s_ut = E(nc.sbuf_tensor("s_ut", [128, TT], BF16))
    s_kv = s_ut
    s_wb0 = E(nc.sbuf_tensor("s_wb0", [128, TT], BF16))
    s_wb1 = E(nc.sbuf_tensor("s_wb1", [128, TT], BF16))
    s_w9fm = E(nc.sbuf_tensor("s_w9fm", [9, TT], BF16))
    s_ones = E(nc.sbuf_tensor("s_ones", [128, TT], BF16))
    s_carry = E(nc.sbuf_tensor("s_carry", [128, NDK], F32))
    s_bias = E(nc.sbuf_tensor("s_bias", [128, 4 * NDK], F32))
    s_gw9T = E(nc.sbuf_tensor("s_gw9T", [128, NDK * 9], BF16))
    s_sel = E(nc.sbuf_tensor("s_sel", [9, 128], BF16))
    s_idf = E(nc.sbuf_tensor("s_idf", [128, 128], F32))
    s_idb = E(nc.sbuf_tensor("s_idb", [128, 128], BF16))
    s_w9tm = E(nc.sbuf_tensor("s_w9tm", [128, 32], BF16))
    s_m1 = E(nc.sbuf_tensor("s_m1", [128, 1], F32))
    s_m2 = E(nc.sbuf_tensor("s_m2", [128, 1], F32))
    s_negm1 = E(nc.sbuf_tensor("s_negm1", [128, 1], F32))
    s_sum = E(nc.sbuf_tensor("s_sum", [128, 1], F32))
    s_rinv = E(nc.sbuf_tensor("s_rinv", [128, 1], F32))
    s_exp = E(nc.sbuf_tensor("s_exp", [128, 8], F32))
    s_bn = E(nc.sbuf_tensor("s_bn", [128, 24], F32))
    s_mv = E(nc.sbuf_tensor("s_mv", [128, 2], F32))
    s_lnv = E(nc.sbuf_tensor("s_lnv", [128, 1], F32))
    s_rstd = E(nc.sbuf_tensor("s_rstd", [128, 1], F32))
    s_eps = E(nc.sbuf_tensor("s_eps", [128, 1], F32))
    U32 = mybir.dt.uint32
    s_cst = E(nc.sbuf_tensor("s_cst", [128, 138], F32))
    s_xnex = E(nc.sbuf_tensor("s_xnex", [128, NDK * CAP], BF16))
    s_xg = E(nc.sbuf_tensor("s_xg", [128, D], BF16))
    s_glst = E(nc.sbuf_tensor("s_glst", [128, 16], F32))
    s_gidx = E(nc.sbuf_tensor("s_gidx", [128, 8], U32))
    s_gw = E(nc.sbuf_tensor("s_gw", [128, 16], F32))
    s_i1 = E(nc.sbuf_tensor("s_i1", [128, 8], F32))
    s_i2 = E(nc.sbuf_tensor("s_i2", [128, 8], U32))
    s_oh1 = E(nc.sbuf_tensor("s_oh1", [128, 8], F32))
    s_oh2 = E(nc.sbuf_tensor("s_oh2", [128, 8], F32))
    s_me = E(nc.sbuf_tensor("s_me", [128, 16], F32))
    s_gif = s_me
    s_pe8 = E(nc.sbuf_tensor("s_pe8", [128, 8], F32))
    s_tmp8 = E(nc.sbuf_tensor("s_tmp8", [128, 8], F32))
    s_base = E(nc.sbuf_tensor("s_base", [1, 8], F32))
    s_fl1 = E(nc.sbuf_tensor("s_fl1", [128, 1], F32))
    s_fl2 = E(nc.sbuf_tensor("s_fl2", [128, 1], F32))
    s_bg1 = E(nc.sbuf_tensor("s_bg1", [128, 8], U32))
    s_bg2 = E(nc.sbuf_tensor("s_bg2", [128, 8], U32))
    s_sc1 = E(nc.sbuf_tensor("s_sc1", [128, 4], F32))
    s_sc2 = E(nc.sbuf_tensor("s_sc2", [128, 4], F32))
    s_eop = E(nc.sbuf_tensor("s_eop", [128, 2048], BF16))
    s_cs = s_eop[:, :].bitcast(F32)
    s_fill = E(nc.sbuf_tensor("s_fill", [128, 16], F32))

    psum = [E(nc.psum_tensor(f"p{i}", [128, 512], F32)) for i in range(8)]
    pbf = [psum[i][:, :].bitcast(BF16) for i in range(8)]

    tr = Tracker(nc)
    sem_names = ["se_pe", "se_dve", "se_act", "sq_wa0", "sq_wa1",
                 "sq_wh0", "sq_wh1", "sq_wo0", "sq_wo1", "sq_xa", "sq_xb",
                 "sq_st0", "sq_st1", "sq_m0", "sq_m1", "sq_m2", "sq_m3", "sq_m4",
                 "sq_m5", "sq_sc", "sq_ll", "sq_gx", "sq_bg", "sq_eo", "sq_xt",
                 "sq_lf"]
    cast_sems = [f"sq_c{i}" for i in range(29)]
    sem_names += cast_sems
    nc._sems = {}
    for s in sem_names:
        nc._sems[s] = E(nc.semaphore(s))
        tr.sem(s)
    tr.engine_sems({"tensor": "se_pe", "vector": "se_dve", "scalar": "se_act"})

    casts = [("zq", zq, wq_d, "scr_q"), ("zk", zk, wk_d, "scr_k"),
             ("zv", zv, wv_d, "scr_v"), ("zg", zg, wg_d, "scr_g")]
    for gi in range(4):
        casts.append((f"zgu_g{gi}", zgu[gi * MOE_I:(gi + 1) * MOE_I, :],
                      wgu_d[gi * MOE_I:(gi + 1) * MOE_I, :], f"scr_gu_g{gi}"))
        casts.append((f"zgu_u{gi}", zgu[SH_I + gi * MOE_I:SH_I + (gi + 1) * MOE_I, :],
                      wgu_d[SH_I + gi * MOE_I:SH_I + (gi + 1) * MOE_I, :], f"scr_gu_u{gi}"))
    casts.append(("zsd", zsd, wsd_d, "scr_sd"))
    for e in range(NE):
        casts.append((f"z13_{e}", z13[e], w13_d[e], f"scr_13_{e}"))
        casts.append((f"z2_{e}", z2[e], w2_d[e], f"scr_2_{e}"))
    assert len(casts) == 29, len(casts)
    def emit_cast(ci):
        nm, dst, src, buf = casts[ci]
        waits = [(cast_sems[ci - 2], 16)] if ci >= 2 else []
        tr.op("gpsimd", lambda g, dst=dst, src=src: g.dma_start(out=dst, in_=src),
              writes=(buf,), sem=cast_sems[ci], inc=16, manual_waits=waits)

    for ci in range(17):
        emit_cast(ci)

    def gu_src(g, half):
        if g < 4:
            return zgu, (SH_I if half else 0) + g * MOE_I, f"scr_gu_{'u' if half else 'g'}{g}"
        e = g - 4
        return z13[e], (MOE_I if half else 0), f"scr_13_{e}"

    def w2_src(g):
        if g < 4:
            return zsd, g * 1024, "scr_sd"
        return z2[g - 4], 0, f"scr_2_{g - 4}"

    def wrow(g):
        return 8 if g < 4 else g - 4

    tr.op("sync", lambda s: s.dma_start(out=s_xa[0:9, :], in_=gw9_d),
          writes=("xa",), sem="sq_m0", inc=16)
    tr.op("sync", lambda s: s.dma_start(out=s_out[0:1, 0:4 * D],
                                        in_=bqkvg_d.rearrange("j d -> (j d)")[None, :]),
          writes=("out",), sem="sq_m1", inc=16)
    tr.op("sync", lambda s: s.dma_start(out=s_sel[:, :], in_=sel_d[:, 8 * 128:9 * 128]),
          writes=("sel",), sem="sq_m2", inc=16)
    tr.op("sync", lambda s: s.dma_start(out=s_idf[:, :], in_=idf_d),
          writes=("idf",), sem="sq_m3", inc=16)
    tr.op("sync", lambda s: s.dma_start(out=s_idb[:, :], in_=idb_d),
          writes=("idb",), sem="sq_m4", inc=16)
    tr.op("sync", lambda s: s.dma_start(out=s_cst[:, :], in_=cst_d),
          writes=("cst",), sem="sq_m5", inc=16)
    tr.cop("vector", lambda v: v.memset(s_ones[:, :], 1.0), writes=("ones",))
    tr.cop("vector", lambda v: v.memset(s_carry[:, :], 0.0), writes=("carry",))
    tr.cop("vector", lambda v: v.memset(s_eps[:, :], EPS), writes=("eps",))
    tr.cop("vector", lambda v: v.memset(s_fill[:, :], PADV), writes=("fillc",))
    tr.cop("vector", lambda v: v.memset(s_xg[:, :], 0.0), writes=("xg0",))
    for dk in range(NDK):
        tr.cop("tensor",
               lambda t, dk=dk: t.matmul(psum[0][:, dk * 9:dk * 9 + 9],
                                         s_xa[0:9, dk * 128:(dk + 1) * 128],
                                         s_idf[0:9, 0:9], start=True, stop=True),
               reads=("xa", "idf"), writes=("p0",), close=(dk == NDK - 1))
    tr.cop("vector", lambda v: v.tensor_copy(out=s_gw9T[:, :], in_=psum[0][:, 0:NDK * 9]),
           reads=("p0",), writes=("gw9T",))
    for j in range(4):
        for c in range(NDK):
            tr.cop("tensor",
                   lambda t, j=j, c=c: t.matmul(
                       psum[1][:, j * NDK + c:j * NDK + c + 1],
                       s_out[0:1, j * D + c * 128: j * D + (c + 1) * 128],
                       s_idf[0:1, 0:1], start=True, stop=True),
                   reads=("out", "idf"), writes=("p1",),
                   close=(j == 3 and c == NDK - 1))
    tr.cop("vector", lambda v: v.tensor_copy(out=s_bias[:, :], in_=psum[1][:, 0:4 * NDK]),
           reads=("p1",), writes=("bias",))

    xslot = [("xa", "sq_xa", s_xa), ("xb", "sq_xb", s_xb)]
    xslot3 = xslot
    n_store = 0

    for p in range(NPASS):
        t0 = p * TT
        for tc in range(TT // 128):
            sb, ssem, stile = xslot3[tc % 2]
            row = t0 + tc * 128
            tr.op("sync" if tc % 2 == 0 else "scalar",
                  lambda s, stile=stile, row=row: s.dma_start(
                      out=stile[:, :], in_=x_d[row:row + 128, :]),
                  writes=(sb,), sem=ssem, inc=16)
            for i in range(4):
                tr.cop("vector",
                       lambda v, i=i, stile=stile: v.bn_stats(
                           s_bn[:, i * 6:(i + 1) * 6], stile[:, i * 512:(i + 1) * 512]),
                       reads=(sb,), writes=("bn",))
            tr.cop("vector", lambda v: v.bn_aggr(s_mv[:, :], s_bn[:, :]),
                   reads=("bn",), writes=("mv",))
            tr.cop("scalar", lambda sc: sc.activation(out=s_lnv[:, :], in_=s_mv[:, 1:2],
                                                      func=AF.Ln, bias=s_eps[:, 0:1],
                                                      scale=1.0),
                   reads=("mv", "eps"), writes=("lnv",))
            tr.cop("scalar", lambda sc: sc.activation(out=s_rstd[:, :], in_=s_lnv[:, :],
                                                      func=AF.Exp, scale=-0.5),
                   reads=("lnv",), writes=("rstd",))
            tr.cop("vector",
                   lambda v, stile=stile: v.tensor_scalar(
                       out=s_xntm[:, :], in0=stile[:, :], scalar1=s_mv[:, 0:1],
                       scalar2=s_rstd[:, 0:1], op0=ALU.subtract, op1=ALU.mult),
                   reads=(sb, "mv", "rstd"), writes=("xntm",))
            tr.op("scalar",
                  lambda s, tc=tc: s.dma_start(
                      out=xtok_d[tc * 128:(tc + 1) * 128, :], in_=s_xntm[:, :]),
                  reads=("xntm",), writes=("xtok",), sem="sq_xt", inc=16)
            for quad in range(4):
                bank = quad % 2
                for dj in range(4):
                    dk = quad * 4 + dj
                    tr.cop("tensor",
                           lambda t, bank=bank, dj=dj, dk=dk: t.transpose(
                               pbf[bank][:, dj * 128:(dj + 1) * 128],
                               s_xntm[:, dk * 128:(dk + 1) * 128], s_idb[:, :]),
                           reads=("xntm", "idb"), writes=(f"p{bank}",),
                           close=(dj == 3))
                dk0 = quad * 4
                tr.cop("vector",
                       lambda v, bank=bank, dk0=dk0, tc=tc: v.tensor_copy(
                           out=s_xn[:, :].rearrange("p (k t) -> p k t", k=NDK)[
                               :, dk0:dk0 + 4, tc * 128:(tc + 1) * 128],
                           in_=pbf[bank][:, 0:512].rearrange("p (k t) -> p k t", k=4)),
                       reads=(f"p{bank}",), writes=("xn",))

        tr.cop("vector", lambda v: v.memset(s_base[:, :], 0.0), writes=("base",))
        for fh in range(4):
            tr.op("gpsimd", lambda g, fh=fh: g.dma_start(
                out=lists_d[fh * 1024:(fh + 1) * 1024, :].rearrange(
                    "(p a) b -> p (a b)", p=128),
                in_=s_fill[:, :]),
                  reads=("fillc",), writes=("lists",), sem="sq_lf", inc=16)
        def r_logits(tc):
            bl = 6 + tc % 2
            for dk in range(NDK):
                tr.cop("tensor",
                       lambda t, tc=tc, dk=dk, bl=bl: t.matmul(
                           psum[bl][:, 0:9],
                           s_xn[:, dk * TT + tc * 128: dk * TT + (tc + 1) * 128],
                           s_gw9T[:, dk * 9:(dk + 1) * 9],
                           start=(dk == 0), stop=(dk == NDK - 1)),
                       reads=("xn", "gw9T"), writes=(f"p{bl}",), close=(dk == NDK - 1))

        def r_chain(tc):
            bl = 6 + tc % 2
            par = tc % 2
            w9c = par * 16
            mec = par * 8
            tr.cop("vector", lambda v, bl=bl: v.max(s_i1[:, :], psum[bl][:, 0:8]),
                   reads=(f"p{bl}",), writes=("i1",))
            tr.cop("vector", lambda v, bl=bl: v.max_index(s_i2[:, :], s_i1[:, :], psum[bl][:, 0:8]),
                   reads=("i1", f"p{bl}"), writes=("i2",))
            tr.cop("vector", lambda v: v.tensor_copy(
                out=s_tmp8[:, 0:2], in_=s_i2[:, 0:2]),
                   reads=("i2",), writes=("tmp8",))
            tr.cop("vector", lambda v: v.tensor_single_scalar(
                out=s_oh1[:, :], in_=s_cst[:, 128:136], scalar=s_tmp8[:, 0:1],
                op=ALU.is_equal),
                   reads=("cst", "tmp8"), writes=("oh1",))
            tr.cop("vector", lambda v: v.tensor_single_scalar(
                out=s_oh2[:, :], in_=s_cst[:, 128:136], scalar=s_tmp8[:, 1:2],
                op=ALU.is_equal),
                   reads=("cst", "tmp8"), writes=("oh2",))
            tr.cop("vector", lambda v, mec=mec: v.tensor_tensor(
                out=s_me[:, mec:mec + 8], in0=s_oh1[:, :], in1=s_oh2[:, :], op=ALU.add),
                   reads=("oh1", "oh2"), writes=(f"me{par}",))
            tr.cop("vector", lambda v: v.tensor_scalar(
                out=s_negm1[:, :], in0=s_i1[:, 0:1], scalar1=-1.0, scalar2=None,
                op0=ALU.mult),
                   reads=("i1",), writes=("negm1",))
            tr.cop("scalar", lambda sc, bl=bl: sc.activation(
                out=s_exp[:, :], in_=psum[bl][:, 0:8], func=AF.Exp,
                bias=s_negm1[:, 0:1], scale=1.0),
                   reads=(f"p{bl}", "negm1"), writes=("exp",))
            tr.cop("vector", lambda v: v.reduce_sum(s_sum[:, :], s_exp[:, :], axis=AX.X),
                   reads=("exp",), writes=("sum",))
            tr.cop("vector", lambda v: v.reciprocal(s_rinv[:, :], s_sum[:, :]),
                   reads=("sum",), writes=("rinv",))
            tr.cop("vector", lambda v: v.tensor_scalar(
                out=s_exp[:, :], in0=s_exp[:, :], scalar1=s_rinv[:, 0:1], scalar2=None,
                op0=ALU.mult),
                   reads=("exp", "rinv"), writes=("exp",))
            tr.cop("vector", lambda v, w9c=w9c, mec=mec: v.tensor_tensor(
                out=s_w9tm[:, w9c:w9c + 8], in0=s_exp[:, :],
                in1=s_me[:, mec:mec + 8], op=ALU.mult),
                   reads=("exp", f"me{par}"), writes=(f"w9tm{par}",))
            tr.cop("scalar", lambda sc, bl=bl: sc.activation(
                out=s_m1[:, :], in_=psum[bl][:, 8:9], func=AF.Exp, scale=-1.0),
                   reads=(f"p{bl}",), writes=("m1",))
            tr.cop("vector", lambda v: v.tensor_scalar(
                out=s_m2[:, :], in0=s_m1[:, :], scalar1=1.0, scalar2=None, op0=ALU.add),
                   reads=("m1",), writes=("m2",))
            tr.cop("vector", lambda v: v.reciprocal(s_sum[:, :], s_m2[:, :]),
                   reads=("m2",), writes=("sum",))
            tr.cop("vector", lambda v, w9c=w9c: v.tensor_copy(
                out=s_w9tm[:, w9c + 8:w9c + 9], in_=s_sum[:, :]),
                   reads=("sum",), writes=(f"w9tm{par}",))

        def r_pe(tc):
            bp = tc % 2
            bb = 4 + tc % 2
            par = tc % 2
            w9c = par * 16
            mec = par * 8
            tr.cop("tensor",
                   lambda t, tc=tc, w9c=w9c: t.transpose(
                       pbf[3][0:9, tc * 128:(tc + 1) * 128],
                       s_w9tm[:, w9c:w9c + 9], s_idb[:, :]),
                   reads=(f"w9tm{par}", "idb"), writes=("p3",))
            tr.cop("tensor", lambda t, bb=bb, mec=mec: t.matmul(
                psum[bb][0:1, 0:8], s_cst[:, 137:138], s_me[:, mec:mec + 8],
                start=True, stop=True),
                   reads=("cst", f"me{par}"), writes=(f"p{bb}",))
            tr.cop("tensor", lambda t, bp=bp, mec=mec: t.matmul(
                psum[bp][:, 0:8], s_cst[:, 0:128], s_me[:, mec:mec + 8],
                start=True, stop=False),
                   reads=("cst", f"me{par}"), writes=(f"p{bp}",), close=False)
            tr.cop("tensor", lambda t, bp=bp: t.matmul(
                psum[bp][:, 0:8], s_cst[0:1, 0:128], s_base[0:1, :],
                start=False, stop=True),
                   reads=("cst", "base"), writes=(f"p{bp}",))

        def r_post(tc):
            bp = tc % 2
            bb = 4 + tc % 2
            par = tc % 2
            w9c = par * 16
            tr.cop("vector", lambda v, bb=bb: v.tensor_tensor(
                out=s_base[:, :], in0=s_base[:, :], in1=psum[bb][0:1, 0:8], op=ALU.add),
                   reads=("base", f"p{bb}"), writes=("base",))
            tr.cop("vector", lambda v, bp=bp: v.scalar_tensor_tensor(
                out=s_pe8[:, :], in0=s_cst[:, 128:136], scalar=512.0,
                in1=psum[bp][:, 0:8], op0=ALU.mult, op1=ALU.add),
                   reads=(f"p{bp}", "cst"), writes=("pe8",))
            for cls, oh, ohb, flt, flb, bgt, bgb, sct, scb in (
                    (1, s_oh1, "oh1", s_fl1, "fl1", s_bg1, "bg1", s_sc1, "sc1"),
                    (2, s_oh2, "oh2", s_fl2, "fl2", s_bg2, "bg2", s_sc2, "sc2")):
                tr.cop("vector", lambda v, oh=oh: v.tensor_tensor(
                    out=s_tmp8[:, :], in0=s_pe8[:, :], in1=oh[:, :], op=ALU.mult),
                       reads=("pe8", ohb), writes=("tmp8",))
                tr.cop("vector", lambda v, flt=flt: v.reduce_sum(
                    flt[:, :], s_tmp8[:, :], axis=AX.X),
                       reads=("tmp8",), writes=(flb,))
                tr.cop("vector", lambda v, flt=flt: v.tensor_scalar(
                    out=flt[:, :], in0=flt[:, :], scalar1=-1.0, scalar2=None,
                    op0=ALU.add),
                       reads=(flb,), writes=(flb,))
                tr.cop("vector", lambda v, flt=flt, bgt=bgt, tc=tc: v.tensor_copy(
                    out=bgt[:, tc:tc + 1], in_=flt[:, :]),
                       reads=(flb,), writes=(bgb,))
                tr.cop("vector", lambda v, sct=sct, par=par, tc=tc: v.tensor_scalar(
                    out=sct[:, 2 * par:2 * par + 1], in0=s_cst[:, 136:137],
                    scalar1=float(tc * 128), scalar2=None, op0=ALU.add),
                       reads=("cst",), writes=(scb,))
                tr.cop("vector", lambda v, oh=oh, w9c=w9c: v.tensor_tensor(
                    out=s_tmp8[:, :], in0=s_w9tm[:, w9c:w9c + 8], in1=oh[:, :],
                    op=ALU.mult),
                       reads=(f"w9tm{par}", ohb), writes=("tmp8",))
                tr.cop("vector", lambda v, sct=sct, par=par: v.reduce_sum(
                    sct[:, 2 * par + 1:2 * par + 2], s_tmp8[:, :], axis=AX.X),
                       reads=("tmp8",), writes=(scb,))
                tr.op("gpsimd",
                      lambda g, sct=sct, bgt=bgt, par=par, tc=tc: g.indirect_dma_start(
                          out=lists_d,
                          out_offset=bass.IndirectOffsetOnAxis(
                              ap=bgt[:, tc:tc + 1], axis=0),
                          in_=sct[:, 2 * par:2 * par + 2],
                          in_offset=None),
                      reads=(scb, bgb), writes=("lists",), sem="sq_sc", inc=16)

        r_logits(0)
        for tc in range(TT // 128):
            r_chain(tc)
            if tc + 1 < TT // 128:
                r_logits(tc + 1)
            r_pe(tc)
            r_post(tc)
        tr.cop("vector", lambda v: v.tensor_copy(out=s_w9fm[:, :], in_=pbf[3][0:9, 0:TT]),
               reads=("p3",), writes=("w9fm",))

        wa = [("wa0", "sq_wa0", 0), ("wa1", "sq_wa1", 4096)]

        def fill_attn(c, half):
            buf, sm, base = wa[half]
            srcs = ((zq, "scr_q"), (zk, "scr_k")) if half == 0 else ((zv, "scr_v"), (zg, "scr_g"))
            for pi, (zsrc, zb) in enumerate(srcs):
                off = base + pi * D
                tr.op("sync",
                      lambda s, zsrc=zsrc, c=c, off=off: s.dma_start(
                          out=s_ha[:, off:off + D],
                          in_=zsrc[c * 128:(c + 1) * 128, :]),
                      reads=(zb,), writes=(buf, "hA"), sem=sm, inc=16)

        fill_attn(0, 0)
        fill_attn(0, 1)
        for c in range(NDK):
            for pi, (buf, base) in enumerate(
                    (("wa0", 0), ("wa0", 2048), ("wa1", 4096), ("wa1", 6144))):
                for th in range(2):
                    bank = pi * 2 + th
                    for dk in range(NDK):
                        tr.cop("tensor",
                               lambda t, base=base, bank=bank, dk=dk, th=th: t.matmul(
                                   psum[bank][:, :],
                                   s_ha[:, base + dk * 128: base + (dk + 1) * 128],
                                   s_xn[:, dk * TT + th * 512: dk * TT + (th + 1) * 512],
                                   start=(dk == 0), stop=(dk == NDK - 1)),
                               reads=(buf, "xn"), writes=(f"p{bank}",),
                               close=(dk == NDK - 1))
                if pi == 1 and c < NDK - 1:
                    fill_attn(c + 1, 0)
                if pi == 3 and c < NDK - 1:
                    fill_attn(c + 1, 1)
            for th in range(2):
                tr.cop("scalar",
                       lambda sc, c=c, th=th: sc.activation(
                           out=s_q[:, th * 512:(th + 1) * 512], in_=psum[th][:, :],
                           func=AF.Identity, bias=s_bias[:, c:c + 1], scale=1.0),
                       reads=(f"p{th}", "bias"), writes=("q", "hB"))
                tr.cop("scalar",
                       lambda sc, c=c, th=th: sc.activation(
                           out=s_k[:, th * 512:(th + 1) * 512], in_=psum[2 + th][:, :],
                           func=AF.Identity, bias=s_bias[:, NDK + c:NDK + c + 1],
                           scale=1.0),
                       reads=(f"p{2 + th}", "bias"), writes=("k", "hB"))
                tr.cop("scalar",
                       lambda sc, c=c, th=th: sc.activation(
                           out=s_g[:, th * 512:(th + 1) * 512], in_=psum[6 + th][:, :],
                           func=AF.Sigmoid, bias=s_bias[:, 3 * NDK + c:3 * NDK + c + 1],
                           scale=1.0),
                       reads=(f"p{6 + th}", "bias"), writes=("g", "hB"))
                tr.cop("vector",
                       lambda v, c=c, th=th: v.scalar_tensor_tensor(
                           out=s_kv[:, th * 512:(th + 1) * 512], in0=psum[4 + th][:, :],
                           scalar=s_bias[:, 2 * NDK + c:2 * NDK + c + 1],
                           in1=s_k[:, th * 512:(th + 1) * 512],
                           op0=ALU.add, op1=ALU.mult),
                       reads=(f"p{4 + th}", "bias", "k"), writes=("kv", "utt0", "utt1"))
            tr.cop("vector",
                   lambda v, c=c: v.tensor_tensor_scan(
                       out=s_cs[:, :], data0=s_ones[:, :], data1=s_kv[:, :],
                       initial=s_carry[:, c:c + 1], op0=ALU.mult, op1=ALU.add),
                   reads=("ones", "kv", "utt0", "utt1", "carry"), writes=("cs", "eop0", "eop1"))
            tr.cop("vector",
                   lambda v, c=c: v.tensor_copy(
                       out=s_carry[:, c:c + 1], in_=s_cs[:, TT - 1:TT]),
                   reads=("cs",), writes=("carry",))
            tr.cop("vector", lambda v: v.tensor_tensor(
                out=s_qg[:, :], in0=s_q[:, :], in1=s_g[:, :], op=ALU.mult),
                   reads=("q", "g"), writes=("qg", "sgt0", "sgt1"))
            tr.cop("vector",
                   lambda v, c=c: v.tensor_tensor(
                       out=s_out[:, c * TT:(c + 1) * TT], in0=s_qg[:, :],
                       in1=s_cs[:, :], op=ALU.mult),
                   reads=("qg", "sgt0", "sgt1", "cs"), writes=("out",))

        wbs = [("wb0", s_wb0), ("wb1", s_wb1)]
        whs = [("wh0", "sq_wh0", 0), ("wh1", "sq_wh1", 4096)]
        wos = [("wo0", "sq_wo0", s_wo0), ("wo1", "sq_wo1", s_wo1)]

        def fill_wh(g, i, slot):
            buf, sm, base = whs[slot]
            for half in range(2):
                zsrc, rbase, zb = gu_src(g, half)
                r0 = rbase + i * 128
                off = base + half * D
                tr.op("scalar",
                      lambda s, zsrc=zsrc, r0=r0, off=off: s.dma_start(
                          out=s_wh[:, off:off + D],
                          in_=zsrc[r0:r0 + 128, :]),
                      reads=(zb,), writes=(buf,), sem=sm, inc=16)

        def fill_wo(g, ec, slot):
            buf, sm, tile = wos[slot]
            zsrc, rbase, zb = w2_src(g)
            tr.op("sync",
                  lambda s, zsrc=zsrc, rbase=rbase, ec=ec, tile=tile:
                  s.dma_start(
                      out=tile[:, :],
                      in_=zsrc[rbase + ec * 128: rbase + (ec + 1) * 128, :]),
                  reads=(zb,), writes=(buf,), sem=sm, inc=16)

        def make_wb(g, slot):
            wbuf, wtile = wbs[slot]
            r = wrow(g)
            for th in range(2):
                tr.cop("tensor",
                       lambda t, r=r, th=th: t.matmul(
                           psum[6][:, :], s_sel[:, 0:128],
                           s_w9fm[0:9, th * 512:(th + 1) * 512],
                           start=True, stop=True),
                       reads=("sel", "w9fm"), writes=("p6",), close=True)
                tr.cop("scalar",
                       lambda sc, wtile=wtile, th=th: sc.activation(
                           out=wtile[:, th * 512:(th + 1) * 512], in_=psum[6][:, :],
                           func=AF.Identity, scale=1.0),
                       reads=("p6",), writes=(wbuf,))

        def load_lists(e):
            b8 = (e % 2) * 8
            b4 = (e % 2) * 4
            tr.op("scalar",
                  lambda s, e=e, b8=b8: s.dma_start(
                      out=s_glst[:, b8:b8 + 8].rearrange("p (a b) -> p a b", b=2),
                      in_=lists_d[e * CAP:(e + 1) * CAP, :].rearrange(
                          "(a p) b -> p a b", p=128)),
                  reads=("lists",), writes=("glst",), sem="sq_ll", inc=16)
            tr.cop("vector",
                   lambda v, b8=b8, b4=b4: v.tensor_copy(
                       out=s_gif[:, b4:b4 + 4].rearrange("p (a c) -> p a c", c=1),
                       in_=s_glst[:, b8:b8 + 8].rearrange(
                           "p (a c) -> p a c", c=2)[:, :, 0:1]),
                   reads=("glst",), writes=("gif", "me0", "me1"))
            tr.cop("vector",
                   lambda v, b4=b4: v.tensor_scalar(
                       out=s_gif[:, b4:b4 + 4], in0=s_gif[:, b4:b4 + 4],
                       scalar1=float(TT), scalar2=None, op0=ALU.min),
                   reads=("gif",), writes=("gif",))
            tr.cop("vector",
                   lambda v, b4=b4: v.tensor_copy(
                       out=s_gidx[:, b4:b4 + 4], in_=s_gif[:, b4:b4 + 4]),
                   reads=("gif",), writes=("gidx",))
            b4w = (e % 4) * 4
            tr.cop("vector",
                   lambda v, b8=b8, b4w=b4w: v.tensor_copy(
                       out=s_gw[:, b4w:b4w + 4].rearrange("p (a c) -> p a c", c=1),
                       in_=s_glst[:, b8:b8 + 8].rearrange(
                           "p (a c) -> p a c", c=2)[:, :, 1:2]),
                   reads=("glst",), writes=("gw",))

        def gather_tile(e, jt):
            b4 = (e % 2) * 4
            tr.op("gpsimd",
                  lambda g, b4=b4, jt=jt: g.indirect_dma_start(
                      out=s_xg[:, :],
                      out_offset=None,
                      in_=xtok_d,
                      in_offset=bass.IndirectOffsetOnAxis(
                          ap=s_gidx[:, b4 + jt:b4 + jt + 1], axis=0)),
                  reads=("gidx", "xtok"), writes=("xg0",), sem="sq_gx", inc=16)

        def transpose_tile(e, jt):
            for hb in range(2):
                bank = 6 + hb
                for dj in range(8):
                    dk = hb * 8 + dj
                    tr.cop("tensor",
                           lambda t, dk=dk, dj=dj, bank=bank: t.transpose(
                               pbf[bank][:, dj * 128:(dj + 1) * 128],
                               s_xg[:, dk * 128:(dk + 1) * 128],
                               s_idb[:, :]),
                           reads=("xg0", "idb"), writes=(f"p{bank}",),
                           close=(dj == 7))
                tr.cop("vector",
                       lambda v, bank=bank, hb=hb, jt=jt: v.tensor_copy(
                           out=s_xnex[:, :].rearrange("p (k s) -> p k s", k=NDK)[
                               :, hb * 8:hb * 8 + 8, jt * 128:(jt + 1) * 128],
                           in_=pbf[bank][:, 0:1024].rearrange("p (k s) -> p k s", k=8)),
                       reads=(f"p{bank}",), writes=("xnex",))

        make_wb(0, 0)
        gchunk = 0
        fill_wh(0, 0, 0)
        for g in range(NGS):
            wbuf, wtile = wbs[g % 2]
            for i in range(NI):
                whbuf, whsm, whbase = whs[gchunk % 2]
                if i + 1 < NI:
                    fill_wh(g, i + 1, (gchunk + 1) % 2)
                elif g + 1 < NG:
                    fill_wh(g + 1, 0, (gchunk + 1) % 2)
                gchunk += 1
                if p == 0 and g >= 1 and i == 0:
                    emit_cast(15 + 2 * g)
                    emit_cast(16 + 2 * g)
                if g == 2 and i == 0:
                    load_lists(0)
                if g == 2 and i == 5:
                    load_lists(1)
                if g == 3 and i < 8:
                    if i % 2 == 0:
                        gather_tile(0, i // 2)
                    else:
                        transpose_tile(0, i // 2)
                for half in range(2):
                    for th in range(2):
                        bank = half * 2 + th
                        for dk in range(NDK):
                            tr.cop("tensor",
                                   lambda t, whbase=whbase, half=half, th=th,
                                   bank=bank, dk=dk: t.matmul(
                                       psum[bank][:, :],
                                       s_wh[:, whbase + half * 2048 + dk * 128:
                                            whbase + half * 2048 + (dk + 1) * 128],
                                       s_xn[:, dk * TT + th * 512:
                                            dk * TT + (th + 1) * 512],
                                       start=(dk == 0), stop=(dk == NDK - 1)),
                                   reads=(whbuf, "xn"), writes=(f"p{bank}",),
                                   close=(dk == NDK - 1))
                for th in range(2):
                    tr.cop("scalar",
                           lambda sc, th=th: sc.activation(
                               out=s_sg[:, th * 512:(th + 1) * 512], in_=psum[th][:, :],
                               func=AF.Silu, scale=1.0),
                           reads=(f"p{th}",), writes=(f"sgt{th}",))
                    tr.cop("vector",
                           lambda v, th=th, wtile=wtile: v.tensor_tensor(
                               out=s_ut[:, th * 512:(th + 1) * 512],
                               in0=psum[2 + th][:, :],
                               in1=wtile[:, th * 512:(th + 1) * 512], op=ALU.mult),
                           reads=(f"p{2 + th}", wbuf), writes=(f"utt{th}",))
                tr.cop("vector",
                       lambda v, i=i: v.tensor_tensor(
                           out=s_ha[:, i * 1024:(i + 1) * 1024], in0=s_sg[:, :],
                           in1=s_ut[:, :], op=ALU.mult),
                       reads=("sgt0", "sgt1", "utt0", "utt1"), writes=("hA" if i < 8 else "hB",))
            if g + 1 < NGS:
                make_wb(g + 1, (g + 1) % 2)
            fill_wo(g, 0, 0)
            for ec in range(8):
                wobuf, wosm, wotile = wos[ec % 2]
                if ec + 1 < 8:
                    fill_wo(g, ec + 1, (ec + 1) % 2)
                for el in range(2):
                    ek = ec * 2 + el
                    for th in range(2):
                        bank = 4 + (ek * 2 + th) % 2
                        for i in range(NI):
                            tr.cop("tensor",
                                   lambda t, wotile=wotile, i=i, el=el, th=th,
                                   bank=bank: t.matmul(
                                       psum[bank][:, :],
                                       wotile[:, i * 256 + el * 128:
                                              i * 256 + (el + 1) * 128],
                                       s_ha[:, i * 1024 + th * 512:
                                            i * 1024 + (th + 1) * 512],
                                       start=(i == 0), stop=(i == NI - 1)),
                                   reads=(wobuf, "hA" if i < 8 else "hB"),
                                   writes=(f"p{bank}",),
                                   close=(i == NI - 1))
                        tr.cop("vector",
                               lambda v, ek=ek, th=th, bank=bank: v.tensor_tensor(
                                   out=s_out[:, ek * TT + th * 512:
                                             ek * TT + (th + 1) * 512],
                                   in0=psum[bank][:, :],
                                   in1=s_out[:, ek * TT + th * 512:
                                             ek * TT + (th + 1) * 512],
                                   op=ALU.add),
                               reads=(f"p{bank}", "out"), writes=("out",))
        for e in range(NE):
            g = NGS + e
            b4w = (e % 4) * 4
            for i in range(NI):
                whbuf, whsm, whbase = whs[gchunk % 2]
                if i + 1 < NI:
                    fill_wh(g, i + 1, (gchunk + 1) % 2)
                elif g + 1 < NG:
                    fill_wh(g + 1, 0, (gchunk + 1) % 2)
                gchunk += 1
                if p == 0 and e in (0, 1, 2) and i == 0:
                    emit_cast(23 + 2 * e)
                    emit_cast(24 + 2 * e)
                if e + 2 < NE and i == 1:
                    load_lists(e + 2)
                ub = 2 * (i % 2)
                for half in range(2):
                    for dk in range(NDK):
                        tr.cop("tensor",
                               lambda t, whbase=whbase, half=half, dk=dk, ub=ub: t.matmul(
                                   psum[ub + half][:, :],
                                   s_wh[:, whbase + half * 2048 + dk * 128:
                                        whbase + half * 2048 + (dk + 1) * 128],
                                   s_xnex[:, dk * CAP:(dk + 1) * CAP],
                                   start=(dk == 0), stop=(dk == NDK - 1)),
                               reads=(whbuf, "xnex"), writes=(f"p{ub + half}",),
                               close=(dk == NDK - 1))
                par = i % 2
                tr.cop("scalar",
                       lambda sc, ub=ub, par=par: sc.activation(
                           out=s_sg[:, par * 512:(par + 1) * 512], in_=psum[ub][:, :],
                           func=AF.Silu, scale=1.0),
                       reads=(f"p{ub}",), writes=(f"sgt{par}",))
                tr.cop("vector",
                       lambda v, ub=ub, par=par: v.tensor_copy(
                           out=s_ut[:, par * 512:(par + 1) * 512],
                           in_=psum[ub + 1][:, :]),
                       reads=(f"p{ub + 1}",), writes=(f"utt{par}",))
                tr.cop("vector",
                       lambda v, i=i, par=par: v.tensor_tensor(
                           out=s_ha[:, i * 512:(i + 1) * 512],
                           in0=s_sg[:, par * 512:(par + 1) * 512],
                           in1=s_ut[:, par * 512:(par + 1) * 512], op=ALU.mult),
                       reads=(f"sgt{par}", f"utt{par}"), writes=("hA",))
            fill_wo(g, 0, 0)
            for ec in range(8):
                wobuf, wosm, wotile = wos[ec % 2]
                if ec + 1 < 8:
                    fill_wo(g, ec + 1, (ec + 1) % 2)
                if e + 1 < NE:
                    if ec % 2 == 0:
                        gather_tile(e + 1, ec // 2)
                    else:
                        transpose_tile(e + 1, ec // 2)
                for jt in range(CT):
                    for i in range(NI):
                        tr.cop("tensor",
                               lambda t, wotile=wotile, i=i, jt=jt: t.matmul(
                                   psum[2 + jt][:, 0:256],
                                   s_ha[:, i * 512 + jt * 128:
                                        i * 512 + (jt + 1) * 128],
                                   wotile[:, i * 256:(i + 1) * 256],
                                   start=(i == 0), stop=(i == NI - 1)),
                               reads=(wobuf, "hA"), writes=(f"p{2 + jt}",),
                               close=(i == NI - 1))
                    tr.cop("scalar",
                           lambda sc, jt=jt, ec=ec, b4w=b4w: sc.activation(
                               out=s_eop[:, (ec % 2) * 1024 + jt * 256:
                                         (ec % 2) * 1024 + (jt + 1) * 256],
                               in_=psum[2 + jt][:, 0:256], func=AF.Identity,
                               scale=s_gw[:, b4w + jt:b4w + jt + 1]),
                           reads=(f"p{2 + jt}", "gw"), writes=(f"eop{ec % 2}",))
                    tr.op("scalar",
                          lambda s, e=e, jt=jt, ec=ec: s.dma_start(
                              out=eout_d[e * CAP + jt * 128: e * CAP + (jt + 1) * 128,
                                         ec * 256:(ec + 1) * 256],
                              in_=s_eop[:, (ec % 2) * 1024 + jt * 256:
                                        (ec % 2) * 1024 + (jt + 1) * 256]),
                          reads=(f"eop{ec % 2}",), writes=("eout",), sem="sq_eo", inc=16)
        def bg_slot(slot):
            if slot < 4:
                return s_xnex[:, slot * 2048:(slot + 1) * 2048]
            return s_ha[:, (slot - 4) * 2048:(slot - 3) * 2048]

        def back_gather(tc, cls):
            slot = (2 * tc + cls) % 8
            st = bg_slot(slot)
            bgt, bgb = (s_bg1, "bg1") if cls == 0 else (s_bg2, "bg2")
            tr.op("gpsimd",
                  lambda g, st=st, bgt=bgt, tc=tc: g.indirect_dma_start(
                      out=st, out_offset=None, in_=eout_d,
                      in_offset=bass.IndirectOffsetOnAxis(
                          ap=bgt[:, tc:tc + 1], axis=0)),
                  reads=(bgb, "eout"),
                  writes=(f"bgs{slot}", "xnex" if slot < 4 else "hA"),
                  sem="sq_bg", inc=16)

        for tc in range(4):
            back_gather(tc, 0)
            back_gather(tc, 1)
        otm = [(s_wh[:, 0:4096].bitcast(F32), ("wh0",)),
               (s_wh[:, 4096:8192].bitcast(F32), ("wh1",))]
        for tc in range(TT // 128):
            sb, ssem, stile = xslot[tc % 2]
            row = t0 + tc * 128
            tr.op("scalar",
                  lambda s, stile=stile, row=row: s.dma_start(
                      out=stile[:, :], in_=x_d[row:row + 128, :]),
                  writes=(sb,), sem=ssem, inc=16)
            ot, otbufs = otm[tc % 2]
            slot1 = (2 * tc) % 8
            slot2 = (2 * tc + 1) % 8
            g1, g2 = bg_slot(slot1), bg_slot(slot2)
            for quad in range(4):
                bank = quad
                gbank = 4 + quad
                for ej in range(4):
                    ek = quad * 4 + ej
                    tr.cop("tensor",
                           lambda t, bank=bank, ej=ej, ek=ek, tc=tc: t.transpose(
                               psum[bank][:, ej * 128:(ej + 1) * 128],
                               s_out[:, ek * TT + tc * 128: ek * TT + (tc + 1) * 128],
                               s_idf[:, :]),
                           reads=("out", "idf"), writes=(f"p{bank}",),
                           close=(ej == 3))
                tr.cop("tensor",
                       lambda t, gbank=gbank, quad=quad, g1=g1: t.matmul(
                           psum[gbank][:, 0:512],
                           s_idb[:, 0:128],
                           g1[:, quad * 512:(quad + 1) * 512],
                           start=True, stop=False),
                       reads=(f"bgs{slot1}", "idb",
                              "xnex" if slot1 < 4 else "hA"),
                       writes=(f"p{gbank}",), close=False)
                tr.cop("tensor",
                       lambda t, gbank=gbank, quad=quad, g2=g2: t.matmul(
                           psum[gbank][:, 0:512],
                           s_idb[:, 0:128],
                           g2[:, quad * 512:(quad + 1) * 512],
                           start=False, stop=True),
                       reads=(f"bgs{slot2}", "idb",
                              "xnex" if slot2 < 4 else "hA"),
                       writes=(f"p{gbank}",), close=True)
                tr.cop("vector",
                       lambda v, bank=bank, quad=quad, ot=ot, stile=stile:
                       v.tensor_tensor(
                           out=ot[:, quad * 512:(quad + 1) * 512],
                           in0=psum[bank][:, 0:512],
                           in1=stile[:, quad * 512:(quad + 1) * 512], op=ALU.add),
                       reads=(f"p{bank}", sb), writes=otbufs)
                tr.cop("vector",
                       lambda v, gbank=gbank, quad=quad, ot=ot:
                       v.tensor_tensor(
                           out=ot[:, quad * 512:(quad + 1) * 512],
                           in0=psum[gbank][:, 0:512],
                           in1=ot[:, quad * 512:(quad + 1) * 512], op=ALU.add),
                       reads=(f"p{gbank}",) + otbufs, writes=otbufs)
            tr.op("sync" if tc % 2 == 0 else "scalar",
                  lambda s, ot=ot, row=row: s.dma_start(
                      out=out_d[row:row + 128, :], in_=ot[:, :]),
                  reads=otbufs, writes=(f"sto{tc % 2}",),
                  sem=f"sq_st{tc % 2}", inc=16)
            n_store += 1
            if tc + 4 < TT // 128:
                back_gather(tc + 4, 0)
                back_gather(tc + 4, 1)

    tr.barrier_wait("vector", "sq_st0", 16 * (n_store // 2))
    tr.barrier_wait("vector", "sq_st1", 16 * (n_store // 2))

    block = E(nc.Block())
    tr.replay(block)
    ctx.close()
    return nc


_NC_CACHE = {}


def _get_nc():
    if "nc" not in _NC_CACHE:
        _NC_CACHE["nc"] = build_nc()
    return _NC_CACHE["nc"]


NEH = 8


def make_in_maps(inputs):
    x = np.ascontiguousarray(np.asarray(inputs["L0_attn_x"], dtype=np.float32))
    bqkvg = np.stack([np.asarray(inputs["L0_attn_bq"]), np.asarray(inputs["L0_attn_bk"]),
                      np.asarray(inputs["L0_attn_bv"]), np.asarray(inputs["L0_attn_bg"])],
                     axis=0)
    bqkvg = np.ascontiguousarray(bqkvg, dtype=np.float32)
    gw9 = np.concatenate([np.asarray(inputs["L0_ffn_gate_w"]),
                          np.asarray(inputs["L0_ffn_shared_gate_w"])], axis=0)
    gw9 = np.ascontiguousarray(gw9, dtype=np.float32)
    idf = np.eye(128, dtype=np.float32)
    idb = np.eye(128).astype(ml_dtypes.bfloat16)
    cst = np.zeros((128, 138), dtype=np.float32)
    cst[:, 0:128] = np.triu(np.ones((128, 128), dtype=np.float32))
    cst[:, 128:136] = np.arange(8, dtype=np.float32)[None, :]
    cst[:, 136] = np.arange(128, dtype=np.float32)
    cst[:, 137] = 1.0
    sel = np.zeros((9, 9, 128), dtype=np.float32)
    for r in range(9):
        sel[r, r, :] = 1.0
    sel = np.ascontiguousarray(sel.reshape(9, 9 * 128)).astype(ml_dtypes.bfloat16)
    def sw_up(w, rc):
        w = np.asarray(w, dtype=np.float32)
        return np.ascontiguousarray(
            w.reshape(rc, 128, 16, 128).transpose(0, 3, 2, 1).reshape(rc * 128, 2048))

    w2f = np.asarray(inputs["L0_ffn_w2"], dtype=np.float32)
    w2sw = np.ascontiguousarray(
        w2f.reshape(NEH, 8, 2, 128, 11, 128).transpose(0, 1, 5, 4, 2, 3)
        .reshape(NEH, 1024, 2816))
    wsdf = np.asarray(inputs["L0_ffn_shared_down_w"], dtype=np.float32)
    wsdsw = np.ascontiguousarray(
        wsdf.reshape(8, 2, 128, 4, 11, 128).transpose(3, 0, 5, 4, 1, 2)
        .reshape(4 * 1024, 2816))
    common = dict(
        wq=sw_up(inputs["L0_attn_Wq"], 16),
        wk=sw_up(inputs["L0_attn_Wk"], 16),
        wv=sw_up(inputs["L0_attn_Wv"], 16),
        wg=sw_up(inputs["L0_attn_Wg"], 16),
        bqkvg=bqkvg, gw9=gw9,
        wgu=sw_up(inputs["L0_ffn_shared_gate_up_w"], 88),
        wsd=wsdsw,
        w13=np.ascontiguousarray(
            np.asarray(inputs["L0_ffn_w13"], dtype=np.float32)
            .reshape(NEH, 22, 128, 16, 128).transpose(0, 1, 4, 3, 2)
            .reshape(NEH, 2816, 2048)),
        w2=w2sw,
        idf=idf, idb=idb, sel=sel, cst=cst,
    )
    n = x.shape[0]
    return [dict(common, x=np.ascontiguousarray(x[i])) for i in range(n)]


def kernel(**inputs):
    in_maps = make_in_maps(inputs)
    nc = _get_nc()
    res = run_bass_kernel_spmd(nc, in_maps, core_ids=list(range(8)))
    return np.stack([r["out"] for r in res.results], axis=0)


# version 08
# speedup vs baseline: 1.0425x; 1.0379x over previous
import numpy as np
import ml_dtypes
import concourse.bass as bass
import concourse.mybir as mybir
from concourse.bass_utils import run_bass_kernel_spmd

F32 = mybir.dt.float32
BF16 = mybir.dt.bfloat16
AF = mybir.ActivationFunctionType
ALU = mybir.AluOpType
AX = mybir.AxisListType

S = 4096
D = 2048
TT = 1024
NPASS = S // TT
NDK = D // 128
MOE_I = 1408
SH_I = 5632
NE = 8
NG = 12
NGS = 4
NI = MOE_I // 128
EPS = 1e-5
CAP = 512
CT = CAP // 128
PADV = 1.0e6

ENGS = ("tensor", "vector", "scalar", "sync", "gpsimd")
GAP = 8


class Tracker:

    def __init__(self, nc):
        self.nc = nc
        self.q = {e: [] for e in ENGS}
        self.cnt = {}
        self.bufs = {}
        self.opidx = {e: 0 for e in ENGS}
        self.waited = {e: {} for e in ENGS}
        self.pending = []
        self.esem = {}

    def sem(self, name):
        self.cnt.setdefault(name, 0)
        return name

    def engine_sems(self, names):
        for e, s in names.items():
            self.esem[e] = self.sem(s)

    def _want(self, need, sem, val):
        if need.get(sem, 0) < val:
            need[sem] = val

    def _edge(self, eng, weng, widx, wdma):
        if weng is None:
            return False
        if wdma or weng != eng:
            return True
        return (self.opidx[eng] - widx) < GAP

    def op(self, eng, fn, reads=(), writes=(), sem=None, inc=0, close=True,
           manual_waits=()):
        need = {}
        for (msem, mval) in manual_waits:
            self._want(need, msem, mval)
        for b in reads:
            rec = self.bufs.get(b)
            if rec and rec["w"] and self._edge(eng, rec["weng"], rec["widx"], rec["wdma"]):
                self._want(need, rec["w"][0], rec["w"][1])
        for b in writes:
            rec = self.bufs.get(b)
            if rec:
                if rec["w"] and self._edge(eng, rec["weng"], rec["widx"], rec["wdma"]):
                    self._want(need, rec["w"][0], rec["w"][1])
                for rsem, (rcnt, reng, ridx, rdma) in rec["r"].items():
                    if self._edge(eng, reng, ridx, rdma):
                        self._want(need, rsem, rcnt)
        for s, v in sorted(need.items()):
            if self.waited[eng].get(s, 0) < v:
                self.q[eng].append(("wait", s, v))
                self.waited[eng][s] = v
        idx = self.opidx[eng]
        self.opidx[eng] += 1
        is_dma = inc >= 16
        ev = None
        if inc:
            self.cnt[sem] += inc
            ev = (sem, self.cnt[sem])
        self.q[eng].append(("op", fn, sem if inc else None, inc))
        if not close:
            self.pending.append((eng, idx, tuple(reads), tuple(writes)))
            return None
        if ev:
            for (peng, pidx, preads, pwrites) in self.pending:
                if peng != eng:
                    continue
                for b in preads:
                    self._note_read(b, ev, peng, pidx, is_dma)
                for b in pwrites:
                    self._note_write(b, ev, peng, pidx, is_dma)
            self.pending = [p for p in self.pending if p[0] != eng]
        for b in reads:
            self._note_read(b, ev, eng, idx, is_dma)
        for b in writes:
            self._note_write(b, ev, eng, idx, is_dma)
        return ev

    def _note_read(self, b, ev, eng, idx, dma):
        rec = self.bufs.setdefault(
            b, {"w": None, "weng": None, "widx": -1 << 30, "wdma": False, "r": {}})
        if ev is None:
            return
        rec["r"][ev[0]] = (ev[1], eng, idx, dma)

    def _note_write(self, b, ev, eng, idx, dma):
        self.bufs[b] = {"w": ev, "weng": eng, "widx": idx, "wdma": dma, "r": {}}

    def cop(self, eng, fn, reads=(), writes=(), close=True, manual_waits=()):
        return self.op(eng, fn, reads, writes, sem=self.esem[eng],
                       inc=1 if close else 0, close=close,
                       manual_waits=manual_waits)

    def barrier_wait(self, eng, sem, val):
        if self.waited[eng].get(sem, 0) < val:
            self.q[eng].append(("wait", sem, val))
            self.waited[eng][sem] = val

    def replay(self, block):
        for name in ENGS:
            items = self.q[name]

            def make(items=items):
                def body(eng):
                    for it in items:
                        if it[0] == "wait":
                            eng.wait_ge(self.nc._sems[it[1]], it[2])
                        else:
                            _, fn, sm, inc = it
                            ins = fn(eng)
                            if sm is not None:
                                ins.then_inc(self.nc._sems[sm], inc)
                return body

            getattr(block, name)(make())


def build_nc():
    nc = bass.Bass("TRN2", target_bir_lowering=False)

    x_d = nc.dram_tensor("x", [S, D], F32, kind="ExternalInput").ap()
    wq_d = nc.dram_tensor("wq", [D, D], F32, kind="ExternalInput").ap()
    wk_d = nc.dram_tensor("wk", [D, D], F32, kind="ExternalInput").ap()
    wv_d = nc.dram_tensor("wv", [D, D], F32, kind="ExternalInput").ap()
    wg_d = nc.dram_tensor("wg", [D, D], F32, kind="ExternalInput").ap()
    bqkvg_d = nc.dram_tensor("bqkvg", [4, D], F32, kind="ExternalInput").ap()
    gw9_d = nc.dram_tensor("gw9", [9, D], F32, kind="ExternalInput").ap()
    wgu_d = nc.dram_tensor("wgu", [2 * SH_I, D], F32, kind="ExternalInput").ap()
    wsd_d = nc.dram_tensor("wsd", [4 * 1024, 2816], F32, kind="ExternalInput").ap()
    w13_d = nc.dram_tensor("w13", [NE, 2 * MOE_I, D], F32, kind="ExternalInput").ap()
    w2_d = nc.dram_tensor("w2", [NE, 1024, 2816], F32, kind="ExternalInput").ap()
    idf_d = nc.dram_tensor("idf", [128, 128], F32, kind="ExternalInput").ap()
    idb_d = nc.dram_tensor("idb", [128, 128], BF16, kind="ExternalInput").ap()
    sel_d = nc.dram_tensor("sel", [9, 9 * 128], BF16, kind="ExternalInput").ap()
    cst_d = nc.dram_tensor("cst", [128, 138], F32, kind="ExternalInput").ap()
    out_d = nc.dram_tensor("out", [S, D], F32, kind="ExternalOutput").ap()

    lists_d = nc.dram_tensor("lists", [NE * CAP, 2], F32, kind="Internal").ap()
    eout_d = nc.dram_tensor("eout", [NE * CAP, D], BF16, kind="Internal").ap()
    xtok_d = nc.dram_tensor("xtok", [TT + 128, D], BF16, kind="Internal").ap()

    zq = nc.dram_tensor("zq", [D, D], BF16, kind="Internal").ap()
    zk = nc.dram_tensor("zk", [D, D], BF16, kind="Internal").ap()
    zv = nc.dram_tensor("zv", [D, D], BF16, kind="Internal").ap()
    zg = nc.dram_tensor("zg", [D, D], BF16, kind="Internal").ap()
    zgu = nc.dram_tensor("zgu", [2 * SH_I, D], BF16, kind="Internal").ap()
    zsd = nc.dram_tensor("zsd", [4 * 1024, 2816], BF16, kind="Internal").ap()
    z13 = nc.dram_tensor("z13", [NE, 2 * MOE_I, D], BF16, kind="Internal").ap()
    z2 = nc.dram_tensor("z2", [NE, 1024, 2816], BF16, kind="Internal").ap()

    from contextlib import ExitStack
    ctx = ExitStack()
    E = ctx.enter_context

    s_xn = E(nc.sbuf_tensor("s_xn", [128, NDK * TT], BF16))
    s_out = E(nc.sbuf_tensor("s_out", [128, NDK * TT], F32))
    s_ha = E(nc.sbuf_tensor("s_ha", [128, 11264], BF16))
    s_q = s_ha[:, 8192:8192 + TT]
    s_k = s_ha[:, 9216:9216 + TT]
    s_g = s_ha[:, 10240:10240 + TT]
    s_wh = E(nc.sbuf_tensor("s_wh", [128, 8192], BF16))
    s_wo0 = E(nc.sbuf_tensor("s_wo0", [128, NI * 256], BF16))
    s_wo1 = E(nc.sbuf_tensor("s_wo1", [128, NI * 256], BF16))
    s_xa = E(nc.sbuf_tensor("s_xa", [128, D], F32))
    s_xb = E(nc.sbuf_tensor("s_xb", [128, D], F32))
    s_xntm = E(nc.sbuf_tensor("s_xntm", [128, D], BF16))
    s_sg = E(nc.sbuf_tensor("s_sg", [128, TT], BF16))
    s_qg = s_sg
    s_ut = E(nc.sbuf_tensor("s_ut", [128, TT], BF16))
    s_kv = s_ut
    s_wb0 = E(nc.sbuf_tensor("s_wb0", [128, TT], BF16))
    s_wb1 = E(nc.sbuf_tensor("s_wb1", [128, TT], BF16))
    s_w9fm = E(nc.sbuf_tensor("s_w9fm", [9, TT], BF16))
    s_ones = E(nc.sbuf_tensor("s_ones", [128, TT], BF16))
    s_carry = E(nc.sbuf_tensor("s_carry", [128, NDK], F32))
    s_bias = E(nc.sbuf_tensor("s_bias", [128, 4 * NDK], F32))
    s_gw9T = E(nc.sbuf_tensor("s_gw9T", [128, NDK * 9], BF16))
    s_sel = E(nc.sbuf_tensor("s_sel", [9, 128], BF16))
    s_idf = E(nc.sbuf_tensor("s_idf", [128, 128], F32))
    s_idb = E(nc.sbuf_tensor("s_idb", [128, 128], BF16))
    s_w9tm = E(nc.sbuf_tensor("s_w9tm", [128, 32], BF16))
    s_m1 = E(nc.sbuf_tensor("s_m1", [128, 1], F32))
    s_m2 = E(nc.sbuf_tensor("s_m2", [128, 1], F32))
    s_negm1 = E(nc.sbuf_tensor("s_negm1", [128, 1], F32))
    s_sum = E(nc.sbuf_tensor("s_sum", [128, 1], F32))
    s_rinv = E(nc.sbuf_tensor("s_rinv", [128, 1], F32))
    s_exp = E(nc.sbuf_tensor("s_exp", [128, 8], F32))
    s_bn = E(nc.sbuf_tensor("s_bn", [128, 24], F32))
    s_mv = E(nc.sbuf_tensor("s_mv", [128, 2], F32))
    s_lnv = E(nc.sbuf_tensor("s_lnv", [128, 1], F32))
    s_rstd = E(nc.sbuf_tensor("s_rstd", [128, 1], F32))
    s_eps = E(nc.sbuf_tensor("s_eps", [128, 1], F32))
    U32 = mybir.dt.uint32
    s_cst = E(nc.sbuf_tensor("s_cst", [128, 138], F32))
    s_xnex = E(nc.sbuf_tensor("s_xnex", [128, NDK * CAP], BF16))
    s_xg = E(nc.sbuf_tensor("s_xg", [128, D], BF16))
    s_glst = E(nc.sbuf_tensor("s_glst", [128, 16], F32))
    s_gidx = E(nc.sbuf_tensor("s_gidx", [128, 8], U32))
    s_gw = E(nc.sbuf_tensor("s_gw", [128, 16], F32))
    s_i1 = E(nc.sbuf_tensor("s_i1", [128, 8], F32))
    s_i2 = E(nc.sbuf_tensor("s_i2", [128, 8], U32))
    s_oh1 = E(nc.sbuf_tensor("s_oh1", [128, 8], F32))
    s_oh2 = E(nc.sbuf_tensor("s_oh2", [128, 8], F32))
    s_me = E(nc.sbuf_tensor("s_me", [128, 16], F32))
    s_gif = s_me
    s_pe8 = E(nc.sbuf_tensor("s_pe8", [128, 8], F32))
    s_tmp8 = E(nc.sbuf_tensor("s_tmp8", [128, 8], F32))
    s_base = E(nc.sbuf_tensor("s_base", [1, 8], F32))
    s_fl1 = E(nc.sbuf_tensor("s_fl1", [128, 1], F32))
    s_fl2 = E(nc.sbuf_tensor("s_fl2", [128, 1], F32))
    s_bg1 = E(nc.sbuf_tensor("s_bg1", [128, 8], U32))
    s_bg2 = E(nc.sbuf_tensor("s_bg2", [128, 8], U32))
    s_sc1 = E(nc.sbuf_tensor("s_sc1", [128, 4], F32))
    s_sc2 = E(nc.sbuf_tensor("s_sc2", [128, 4], F32))
    s_eop = E(nc.sbuf_tensor("s_eop", [128, 2048], BF16))
    s_cs = s_eop[:, :].bitcast(F32)
    s_fill = E(nc.sbuf_tensor("s_fill", [128, 16], F32))

    psum = [E(nc.psum_tensor(f"p{i}", [128, 512], F32)) for i in range(8)]
    pbf = [psum[i][:, :].bitcast(BF16) for i in range(8)]

    tr = Tracker(nc)
    sem_names = ["se_pe", "se_dve", "se_act", "sq_wa0", "sq_wa1",
                 "sq_wh0", "sq_wh1", "sq_wo0", "sq_wo1", "sq_xa", "sq_xb",
                 "sq_xc",
                 "sq_st0", "sq_st1", "sq_m0", "sq_m1", "sq_m2", "sq_m3", "sq_m4",
                 "sq_m5", "sq_sc", "sq_ll", "sq_gx", "sq_bg", "sq_eo", "sq_xt",
                 "sq_lf"]
    cast_sems = [f"sq_c{i}" for i in range(29)]
    sem_names += cast_sems
    nc._sems = {}
    for s in sem_names:
        nc._sems[s] = E(nc.semaphore(s))
        tr.sem(s)
    tr.engine_sems({"tensor": "se_pe", "vector": "se_dve", "scalar": "se_act"})

    def z13_pieces(e):
        return [(f"scr_13_{e}p{j}", z13[e][j * 704:(j + 1) * 704, :],
                 w13_d[e][j * 704:(j + 1) * 704, :]) for j in range(4)]

    def z2_pieces(e):
        return [(f"scr_2_{e}p{j}", z2[e][j * 512:(j + 1) * 512, :],
                 w2_d[e][j * 512:(j + 1) * 512, :]) for j in range(2)]

    upfront = []
    attn_mats = [("q", zq, wq_d), ("k", zk, wk_d), ("v", zv, wv_d), ("g", zg, wg_d)]
    for ch in range(4):
        for nm, zdst, wsrc in attn_mats:
            upfront.append((f"scr_{nm}{ch}", zdst[ch * 512:(ch + 1) * 512, :],
                            wsrc[ch * 512:(ch + 1) * 512, :]))
    for gi in range(4):
        upfront.append((f"scr_gu_g{gi}", zgu[gi * MOE_I:(gi + 1) * MOE_I, :],
                        wgu_d[gi * MOE_I:(gi + 1) * MOE_I, :]))
        upfront.append((f"scr_gu_u{gi}", zgu[SH_I + gi * MOE_I:SH_I + (gi + 1) * MOE_I, :],
                        wgu_d[SH_I + gi * MOE_I:SH_I + (gi + 1) * MOE_I, :]))
    upfront.append(("scr_sd", zsd, wsd_d))
    for e in (0, 1):
        upfront += z13_pieces(e) + z2_pieces(e)
    deferred = {}
    for g in (1, 2, 3):
        deferred[g] = z13_pieces(g + 1) + z2_pieces(g + 1)
    for e in (0, 1, 2):
        deferred[("r", e)] = z13_pieces(e + 5) + z2_pieces(e + 5)
    n_cast = len(upfront) + sum(len(v) for v in deferred.values())

    cast_state = {"n": 0, "evs": []}

    def emit_piece(pc, look=2):
        nm, dst, src = pc
        evs = cast_state["evs"]
        waits = [evs[-look]] if len(evs) >= look else []
        sem = cast_sems[cast_state["n"] % len(cast_sems)]
        ev = tr.op("gpsimd", lambda g, dst=dst, src=src: g.dma_start(out=dst, in_=src),
                   writes=(nm,), sem=sem, inc=16, manual_waits=waits)
        evs.append(ev)
        cast_state["n"] += 1

    for pc in upfront:
        emit_piece(pc, look=2)

    def gu_src(g, half, i):
        if g < 4:
            return zgu, (SH_I if half else 0) + g * MOE_I, f"scr_gu_{'u' if half else 'g'}{g}"
        e = g - 4
        rbase = MOE_I if half else 0
        return z13[e], rbase, f"scr_13_{e}p{(rbase + i * 128) // 704}"

    def w2_src(g, ec):
        if g < 4:
            return zsd, g * 1024, "scr_sd"
        return z2[g - 4], 0, f"scr_2_{g - 4}p{(ec * 128) // 512}"

    def wrow(g):
        return 8 if g < 4 else g - 4

    tr.op("sync", lambda s: s.dma_start(out=s_xa[0:9, :], in_=gw9_d),
          writes=("xa",), sem="sq_m0", inc=16)
    tr.op("sync", lambda s: s.dma_start(out=s_out[0:1, 0:4 * D],
                                        in_=bqkvg_d.rearrange("j d -> (j d)")[None, :]),
          writes=("out",), sem="sq_m1", inc=16)
    tr.op("sync", lambda s: s.dma_start(out=s_sel[:, :], in_=sel_d[:, 8 * 128:9 * 128]),
          writes=("sel",), sem="sq_m2", inc=16)
    tr.op("sync", lambda s: s.dma_start(out=s_idf[:, :], in_=idf_d),
          writes=("idf",), sem="sq_m3", inc=16)
    tr.op("sync", lambda s: s.dma_start(out=s_idb[:, :], in_=idb_d),
          writes=("idb",), sem="sq_m4", inc=16)
    tr.op("sync", lambda s: s.dma_start(out=s_cst[:, :], in_=cst_d),
          writes=("cst",), sem="sq_m5", inc=16)
    tr.cop("vector", lambda v: v.memset(s_ones[:, :], 1.0), writes=("ones",))
    tr.cop("vector", lambda v: v.memset(s_carry[:, :], 0.0), writes=("carry",))
    tr.cop("vector", lambda v: v.memset(s_eps[:, :], EPS), writes=("eps",))
    tr.cop("vector", lambda v: v.memset(s_fill[:, :], PADV), writes=("fillc",))
    tr.cop("vector", lambda v: v.memset(s_xg[:, :], 0.0), writes=("xg0",))
    for dk in range(NDK):
        tr.cop("tensor",
               lambda t, dk=dk: t.matmul(psum[0][:, dk * 9:dk * 9 + 9],
                                         s_xa[0:9, dk * 128:(dk + 1) * 128],
                                         s_idf[0:9, 0:9], start=True, stop=True),
               reads=("xa", "idf"), writes=("p0",), close=(dk == NDK - 1))
    tr.cop("vector", lambda v: v.tensor_copy(out=s_gw9T[:, :], in_=psum[0][:, 0:NDK * 9]),
           reads=("p0",), writes=("gw9T",))
    for j in range(4):
        for c in range(NDK):
            tr.cop("tensor",
                   lambda t, j=j, c=c: t.matmul(
                       psum[1][:, j * NDK + c:j * NDK + c + 1],
                       s_out[0:1, j * D + c * 128: j * D + (c + 1) * 128],
                       s_idf[0:1, 0:1], start=True, stop=True),
                   reads=("out", "idf"), writes=("p1",),
                   close=(j == 3 and c == NDK - 1))
    tr.cop("vector", lambda v: v.tensor_copy(out=s_bias[:, :], in_=psum[1][:, 0:4 * NDK]),
           reads=("p1",), writes=("bias",))

    xslot = [("xa", "sq_xa", s_xa), ("xb", "sq_xb", s_xb)]
    s_xc = s_xnex[:, 0:4096].bitcast(F32)
    xslot3 = [("xa", "sq_xa", s_xa), ("xb", "sq_xb", s_xb),
              ("xc", "sq_xc", s_xc)]
    n_store = 0

    for p in range(NPASS):
        t0 = p * TT
        for tc in range(TT // 128):
            sb, ssem, stile = xslot3[tc % 3]
            row = t0 + tc * 128
            lq = ("sync", "scalar", "sync" if p == 0 else "gpsimd")[tc % 3]
            tr.op(lq,
                  lambda s, stile=stile, row=row: s.dma_start(
                      out=stile[:, :], in_=x_d[row:row + 128, :]),
                  writes=(sb,) if sb != "xc" else ("xc", "xnex"), sem=ssem, inc=16)
            for i in range(4):
                tr.cop("vector",
                       lambda v, i=i, stile=stile: v.bn_stats(
                           s_bn[:, i * 6:(i + 1) * 6], stile[:, i * 512:(i + 1) * 512]),
                       reads=(sb,), writes=("bn",))
            tr.cop("vector", lambda v: v.bn_aggr(s_mv[:, :], s_bn[:, :]),
                   reads=("bn",), writes=("mv",))
            tr.cop("scalar", lambda sc: sc.activation(out=s_lnv[:, :], in_=s_mv[:, 1:2],
                                                      func=AF.Ln, bias=s_eps[:, 0:1],
                                                      scale=1.0),
                   reads=("mv", "eps"), writes=("lnv",))
            tr.cop("scalar", lambda sc: sc.activation(out=s_rstd[:, :], in_=s_lnv[:, :],
                                                      func=AF.Exp, scale=-0.5),
                   reads=("lnv",), writes=("rstd",))
            tr.cop("vector",
                   lambda v, stile=stile: v.tensor_scalar(
                       out=s_xntm[:, :], in0=stile[:, :], scalar1=s_mv[:, 0:1],
                       scalar2=s_rstd[:, 0:1], op0=ALU.subtract, op1=ALU.mult),
                   reads=(sb, "mv", "rstd"), writes=("xntm",))
            tr.op("scalar",
                  lambda s, tc=tc: s.dma_start(
                      out=xtok_d[tc * 128:(tc + 1) * 128, :], in_=s_xntm[:, :]),
                  reads=("xntm",), writes=("xtok",), sem="sq_xt", inc=16)
            for quad in range(4):
                bank = quad % 2
                for dj in range(4):
                    dk = quad * 4 + dj
                    tr.cop("tensor",
                           lambda t, bank=bank, dj=dj, dk=dk: t.transpose(
                               pbf[bank][:, dj * 128:(dj + 1) * 128],
                               s_xntm[:, dk * 128:(dk + 1) * 128], s_idb[:, :]),
                           reads=("xntm", "idb"), writes=(f"p{bank}",),
                           close=(dj == 3))
                dk0 = quad * 4
                tr.cop("vector",
                       lambda v, bank=bank, dk0=dk0, tc=tc: v.tensor_copy(
                           out=s_xn[:, :].rearrange("p (k t) -> p k t", k=NDK)[
                               :, dk0:dk0 + 4, tc * 128:(tc + 1) * 128],
                           in_=pbf[bank][:, 0:512].rearrange("p (k t) -> p k t", k=4)),
                       reads=(f"p{bank}",), writes=("xn",))

        tr.cop("vector", lambda v: v.memset(s_base[:, :], 0.0), writes=("base",))
        for fh in range(4):
            tr.op("gpsimd", lambda g, fh=fh: g.dma_start(
                out=lists_d[fh * 1024:(fh + 1) * 1024, :].rearrange(
                    "(p a) b -> p (a b)", p=128),
                in_=s_fill[:, :]),
                  reads=("fillc",), writes=("lists",), sem="sq_lf", inc=16)
        def r_logits(tc):
            bl = 6 + tc % 2
            for dk in range(NDK):
                tr.cop("tensor",
                       lambda t, tc=tc, dk=dk, bl=bl: t.matmul(
                           psum[bl][:, 0:9],
                           s_xn[:, dk * TT + tc * 128: dk * TT + (tc + 1) * 128],
                           s_gw9T[:, dk * 9:(dk + 1) * 9],
                           start=(dk == 0), stop=(dk == NDK - 1)),
                       reads=("xn", "gw9T"), writes=(f"p{bl}",), close=(dk == NDK - 1))

        def r_chain_a(tc):
            bl = 6 + tc % 2
            tr.cop("vector", lambda v, bl=bl: v.max(s_i1[:, :], psum[bl][:, 0:8]),
                   reads=(f"p{bl}",), writes=("i1",))
            tr.cop("vector", lambda v, bl=bl: v.max_index(s_i2[:, :], s_i1[:, :], psum[bl][:, 0:8]),
                   reads=("i1", f"p{bl}"), writes=("i2",))

        def r_chain_b(tc):
            bl = 6 + tc % 2
            par = tc % 2
            w9c = par * 16
            mec = par * 8
            tr.cop("vector", lambda v: v.tensor_copy(
                out=s_tmp8[:, 0:2], in_=s_i2[:, 0:2]),
                   reads=("i2",), writes=("tmp8",))
            tr.cop("vector", lambda v: v.tensor_single_scalar(
                out=s_oh1[:, :], in_=s_cst[:, 128:136], scalar=s_tmp8[:, 0:1],
                op=ALU.is_equal),
                   reads=("cst", "tmp8"), writes=("oh1",))
            tr.cop("vector", lambda v: v.tensor_single_scalar(
                out=s_oh2[:, :], in_=s_cst[:, 128:136], scalar=s_tmp8[:, 1:2],
                op=ALU.is_equal),
                   reads=("cst", "tmp8"), writes=("oh2",))
            tr.cop("vector", lambda v, mec=mec: v.tensor_tensor(
                out=s_me[:, mec:mec + 8], in0=s_oh1[:, :], in1=s_oh2[:, :], op=ALU.add),
                   reads=("oh1", "oh2"), writes=(f"me{par}",))
            tr.cop("vector", lambda v: v.tensor_scalar(
                out=s_negm1[:, :], in0=s_i1[:, 0:1], scalar1=-1.0, scalar2=None,
                op0=ALU.mult),
                   reads=("i1",), writes=("negm1",))
            tr.cop("scalar", lambda sc, bl=bl: sc.activation(
                out=s_exp[:, :], in_=psum[bl][:, 0:8], func=AF.Exp,
                bias=s_negm1[:, 0:1], scale=1.0, accum_out=s_sum[:, 0:1]),
                   reads=(f"p{bl}", "negm1"), writes=("exp", "sum"))
            tr.cop("scalar", lambda sc, bl=bl, w9c=w9c: sc.activation(
                out=s_w9tm[:, w9c + 8:w9c + 9], in_=psum[bl][:, 8:9],
                func=AF.Sigmoid, scale=1.0),
                   reads=(f"p{bl}",), writes=(f"w9tm{par}",))
            tr.cop("vector", lambda v: v.reciprocal(s_rinv[:, :], s_sum[:, :]),
                   reads=("sum",), writes=("rinv",))
            tr.cop("vector", lambda v: v.tensor_scalar(
                out=s_exp[:, :], in0=s_exp[:, :], scalar1=s_rinv[:, 0:1], scalar2=None,
                op0=ALU.mult),
                   reads=("exp", "rinv"), writes=("exp",))
            tr.cop("vector", lambda v, w9c=w9c, mec=mec: v.tensor_tensor(
                out=s_w9tm[:, w9c:w9c + 8], in0=s_exp[:, :],
                in1=s_me[:, mec:mec + 8], op=ALU.mult),
                   reads=("exp", f"me{par}"), writes=(f"w9tm{par}",))

        def r_pe(tc):
            bp = tc % 2
            bb = 4 + tc % 2
            par = tc % 2
            w9c = par * 16
            mec = par * 8
            tr.cop("tensor",
                   lambda t, tc=tc, w9c=w9c: t.transpose(
                       pbf[3][0:9, tc * 128:(tc + 1) * 128],
                       s_w9tm[:, w9c:w9c + 9], s_idb[:, :]),
                   reads=(f"w9tm{par}", "idb"), writes=("p3",))
            tr.cop("tensor", lambda t, bb=bb, mec=mec: t.matmul(
                psum[bb][0:1, 0:8], s_cst[:, 137:138], s_me[:, mec:mec + 8],
                start=True, stop=True),
                   reads=("cst", f"me{par}"), writes=(f"p{bb}",))
            tr.cop("tensor", lambda t, bp=bp, mec=mec: t.matmul(
                psum[bp][:, 0:8], s_cst[:, 0:128], s_me[:, mec:mec + 8],
                start=True, stop=False),
                   reads=("cst", f"me{par}"), writes=(f"p{bp}",), close=False)
            tr.cop("tensor", lambda t, bp=bp: t.matmul(
                psum[bp][:, 0:8], s_cst[0:1, 0:128], s_base[0:1, :],
                start=False, stop=True),
                   reads=("cst", "base"), writes=(f"p{bp}",))

        def r_post(tc):
            bp = tc % 2
            bb = 4 + tc % 2
            par = tc % 2
            w9c = par * 16
            tr.cop("vector", lambda v, bb=bb: v.tensor_tensor(
                out=s_base[:, :], in0=s_base[:, :], in1=psum[bb][0:1, 0:8], op=ALU.add),
                   reads=("base", f"p{bb}"), writes=("base",))
            tr.cop("vector", lambda v, bp=bp: v.scalar_tensor_tensor(
                out=s_pe8[:, :], in0=s_cst[:, 128:136], scalar=512.0,
                in1=psum[bp][:, 0:8], op0=ALU.mult, op1=ALU.add),
                   reads=(f"p{bp}", "cst"), writes=("pe8",))
            for cls, oh, ohb, flt, flb, bgt, bgb, sct, scb in (
                    (1, s_oh1, "oh1", s_fl1, "fl1", s_bg1, "bg1", s_sc1, "sc1"),
                    (2, s_oh2, "oh2", s_fl2, "fl2", s_bg2, "bg2", s_sc2, "sc2")):
                tr.cop("vector", lambda v, oh=oh: v.tensor_tensor(
                    out=s_tmp8[:, :], in0=s_pe8[:, :], in1=oh[:, :], op=ALU.mult),
                       reads=("pe8", ohb), writes=("tmp8",))
                tr.cop("vector", lambda v, flt=flt: v.reduce_sum(
                    flt[:, :], s_tmp8[:, :], axis=AX.X),
                       reads=("tmp8",), writes=(flb,))
                tr.cop("vector", lambda v, flt=flt: v.tensor_scalar(
                    out=flt[:, :], in0=flt[:, :], scalar1=-1.0, scalar2=None,
                    op0=ALU.add),
                       reads=(flb,), writes=(flb,))
                tr.cop("vector", lambda v, flt=flt, bgt=bgt, tc=tc: v.tensor_copy(
                    out=bgt[:, tc:tc + 1], in_=flt[:, :]),
                       reads=(flb,), writes=(bgb,))
                tr.cop("vector", lambda v, sct=sct, par=par, tc=tc: v.tensor_scalar(
                    out=sct[:, 2 * par:2 * par + 1], in0=s_cst[:, 136:137],
                    scalar1=float(tc * 128), scalar2=None, op0=ALU.add),
                       reads=("cst",), writes=(scb,))
                tr.cop("vector", lambda v, oh=oh, w9c=w9c: v.tensor_tensor(
                    out=s_tmp8[:, :], in0=s_w9tm[:, w9c:w9c + 8], in1=oh[:, :],
                    op=ALU.mult),
                       reads=(f"w9tm{par}", ohb), writes=("tmp8",))
                tr.cop("vector", lambda v, sct=sct, par=par: v.reduce_sum(
                    sct[:, 2 * par + 1:2 * par + 2], s_tmp8[:, :], axis=AX.X),
                       reads=("tmp8",), writes=(scb,))
                tr.op("gpsimd",
                      lambda g, sct=sct, bgt=bgt, par=par, tc=tc: g.indirect_dma_start(
                          out=lists_d,
                          out_offset=bass.IndirectOffsetOnAxis(
                              ap=bgt[:, tc:tc + 1], axis=0),
                          in_=sct[:, 2 * par:2 * par + 2],
                          in_offset=None),
                      reads=(scb, bgb), writes=("lists",), sem="sq_sc", inc=16)

        NTC = TT // 128
        r_logits(0)
        r_logits(1)
        for tc in range(NTC):
            r_chain_a(tc)
            if tc > 0:
                r_post(tc - 1)
            r_chain_b(tc)
            if tc + 2 < NTC:
                r_logits(tc + 2)
            r_pe(tc)
        r_post(NTC - 1)
        tr.cop("vector", lambda v: v.tensor_copy(out=s_w9fm[:, :], in_=pbf[3][0:9, 0:TT]),
               reads=("p3",), writes=("w9fm",))

        wa = [("wa0", "sq_wa0", 0), ("wa1", "sq_wa1", 4096)]

        def fill_attn(c, half):
            buf, sm, base = wa[half]
            ch = c // 4
            srcs = (((zq, f"scr_q{ch}"), (zk, f"scr_k{ch}")) if half == 0
                    else ((zv, f"scr_v{ch}"), (zg, f"scr_g{ch}")))
            for pi, (zsrc, zb) in enumerate(srcs):
                off = base + pi * D
                tr.op("sync",
                      lambda s, zsrc=zsrc, c=c, off=off: s.dma_start(
                          out=s_ha[:, off:off + D],
                          in_=zsrc[c * 128:(c + 1) * 128, :]),
                      reads=(zb,), writes=(buf, "hA"), sem=sm, inc=16)

        fill_attn(0, 0)
        fill_attn(0, 1)
        for c in range(NDK):
            for pi, (buf, base) in enumerate(
                    (("wa0", 0), ("wa0", 2048), ("wa1", 4096), ("wa1", 6144))):
                for th in range(2):
                    bank = pi * 2 + th
                    for dk in range(NDK):
                        tr.cop("tensor",
                               lambda t, base=base, bank=bank, dk=dk, th=th: t.matmul(
                                   psum[bank][:, :],
                                   s_ha[:, base + dk * 128: base + (dk + 1) * 128],
                                   s_xn[:, dk * TT + th * 512: dk * TT + (th + 1) * 512],
                                   start=(dk == 0), stop=(dk == NDK - 1)),
                               reads=(buf, "xn"), writes=(f"p{bank}",),
                               close=(dk == NDK - 1))
                if pi == 1 and c < NDK - 1:
                    fill_attn(c + 1, 0)
                if pi == 3 and c < NDK - 1:
                    fill_attn(c + 1, 1)
            for th in range(2):
                tr.cop("scalar",
                       lambda sc, c=c, th=th: sc.activation(
                           out=s_q[:, th * 512:(th + 1) * 512], in_=psum[th][:, :],
                           func=AF.Identity, bias=s_bias[:, c:c + 1], scale=1.0),
                       reads=(f"p{th}", "bias"), writes=("q", "hB"))
                tr.cop("scalar",
                       lambda sc, c=c, th=th: sc.activation(
                           out=s_k[:, th * 512:(th + 1) * 512], in_=psum[2 + th][:, :],
                           func=AF.Identity, bias=s_bias[:, NDK + c:NDK + c + 1],
                           scale=1.0),
                       reads=(f"p{2 + th}", "bias"), writes=("k", "hB"))
                tr.cop("scalar",
                       lambda sc, c=c, th=th: sc.activation(
                           out=s_g[:, th * 512:(th + 1) * 512], in_=psum[6 + th][:, :],
                           func=AF.Sigmoid, bias=s_bias[:, 3 * NDK + c:3 * NDK + c + 1],
                           scale=1.0),
                       reads=(f"p{6 + th}", "bias"), writes=("g", "hB"))
                tr.cop("vector",
                       lambda v, c=c, th=th: v.scalar_tensor_tensor(
                           out=s_kv[:, th * 512:(th + 1) * 512], in0=psum[4 + th][:, :],
                           scalar=s_bias[:, 2 * NDK + c:2 * NDK + c + 1],
                           in1=s_k[:, th * 512:(th + 1) * 512],
                           op0=ALU.add, op1=ALU.mult),
                       reads=(f"p{4 + th}", "bias", "k"), writes=("kv", "utt0", "utt1"))
            tr.cop("vector",
                   lambda v, c=c: v.tensor_tensor_scan(
                       out=s_cs[:, :], data0=s_ones[:, :], data1=s_kv[:, :],
                       initial=s_carry[:, c:c + 1], op0=ALU.mult, op1=ALU.add),
                   reads=("ones", "kv", "utt0", "utt1", "carry"), writes=("cs", "eop0", "eop1"))
            tr.cop("vector",
                   lambda v, c=c: v.tensor_copy(
                       out=s_carry[:, c:c + 1], in_=s_cs[:, TT - 1:TT]),
                   reads=("cs",), writes=("carry",))
            tr.cop("vector", lambda v: v.tensor_tensor(
                out=s_qg[:, :], in0=s_q[:, :], in1=s_g[:, :], op=ALU.mult),
                   reads=("q", "g"), writes=("qg", "sgt0", "sgt1"))
            tr.cop("vector",
                   lambda v, c=c: v.tensor_tensor(
                       out=s_out[:, c * TT:(c + 1) * TT], in0=s_qg[:, :],
                       in1=s_cs[:, :], op=ALU.mult),
                   reads=("qg", "sgt0", "sgt1", "cs"), writes=("out",))

        wbs = [("wb0", s_wb0), ("wb1", s_wb1)]
        whs = [("wh0", "sq_wh0", 0), ("wh1", "sq_wh1", 4096)]
        wos = [("wo0", "sq_wo0", s_wo0), ("wo1", "sq_wo1", s_wo1)]

        def fill_wh(g, i, slot):
            buf, sm, base = whs[slot]
            for half in range(2):
                zsrc, rbase, zb = gu_src(g, half, i)
                r0 = rbase + i * 128
                off = base + half * D
                tr.op("scalar" if half == 0 else "sync",
                      lambda s, zsrc=zsrc, r0=r0, off=off: s.dma_start(
                          out=s_wh[:, off:off + D],
                          in_=zsrc[r0:r0 + 128, :]),
                      reads=(zb,), writes=(buf,), sem=sm, inc=16)

        def fill_wo(g, ec, slot):
            buf, sm, tile = wos[slot]
            zsrc, rbase, zb = w2_src(g, ec)
            tr.op("sync",
                  lambda s, zsrc=zsrc, rbase=rbase, ec=ec, tile=tile:
                  s.dma_start(
                      out=tile[:, :],
                      in_=zsrc[rbase + ec * 128: rbase + (ec + 1) * 128, :]),
                  reads=(zb,), writes=(buf,), sem=sm, inc=16)

        def make_wb(g, slot):
            wbuf, wtile = wbs[slot]
            r = wrow(g)
            for th in range(2):
                tr.cop("tensor",
                       lambda t, r=r, th=th: t.matmul(
                           psum[6][:, :], s_sel[:, 0:128],
                           s_w9fm[0:9, th * 512:(th + 1) * 512],
                           start=True, stop=True),
                       reads=("sel", "w9fm"), writes=("p6",), close=True)
                tr.cop("scalar",
                       lambda sc, wtile=wtile, th=th: sc.activation(
                           out=wtile[:, th * 512:(th + 1) * 512], in_=psum[6][:, :],
                           func=AF.Identity, scale=1.0),
                       reads=("p6",), writes=(wbuf,))

        def load_lists(e):
            b8 = (e % 2) * 8
            b4 = (e % 2) * 4
            tr.op("scalar",
                  lambda s, e=e, b8=b8: s.dma_start(
                      out=s_glst[:, b8:b8 + 8].rearrange("p (a b) -> p a b", b=2),
                      in_=lists_d[e * CAP:(e + 1) * CAP, :].rearrange(
                          "(a p) b -> p a b", p=128)),
                  reads=("lists",), writes=("glst",), sem="sq_ll", inc=16)
            tr.cop("vector",
                   lambda v, b8=b8, b4=b4: v.tensor_copy(
                       out=s_gif[:, b4:b4 + 4].rearrange("p (a c) -> p a c", c=1),
                       in_=s_glst[:, b8:b8 + 8].rearrange(
                           "p (a c) -> p a c", c=2)[:, :, 0:1]),
                   reads=("glst",), writes=("gif", "me0", "me1"))
            tr.cop("vector",
                   lambda v, b4=b4: v.tensor_scalar(
                       out=s_gif[:, b4:b4 + 4], in0=s_gif[:, b4:b4 + 4],
                       scalar1=float(TT), scalar2=None, op0=ALU.min),
                   reads=("gif",), writes=("gif",))
            tr.cop("vector",
                   lambda v, b4=b4: v.tensor_copy(
                       out=s_gidx[:, b4:b4 + 4], in_=s_gif[:, b4:b4 + 4]),
                   reads=("gif",), writes=("gidx",))
            b4w = (e % 4) * 4
            tr.cop("vector",
                   lambda v, b8=b8, b4w=b4w: v.tensor_copy(
                       out=s_gw[:, b4w:b4w + 4].rearrange("p (a c) -> p a c", c=1),
                       in_=s_glst[:, b8:b8 + 8].rearrange(
                           "p (a c) -> p a c", c=2)[:, :, 1:2]),
                   reads=("glst",), writes=("gw",))

        def gather_tile(e, jt):
            b4 = (e % 2) * 4
            tr.op("gpsimd",
                  lambda g, b4=b4, jt=jt: g.indirect_dma_start(
                      out=s_xg[:, :],
                      out_offset=None,
                      in_=xtok_d,
                      in_offset=bass.IndirectOffsetOnAxis(
                          ap=s_gidx[:, b4 + jt:b4 + jt + 1], axis=0)),
                  reads=("gidx", "xtok"), writes=("xg0",), sem="sq_gx", inc=16)

        def transpose_tile(e, jt):
            for hb in range(2):
                bank = 6 + hb
                for dj in range(8):
                    dk = hb * 8 + dj
                    tr.cop("tensor",
                           lambda t, dk=dk, dj=dj, bank=bank: t.transpose(
                               pbf[bank][:, dj * 128:(dj + 1) * 128],
                               s_xg[:, dk * 128:(dk + 1) * 128],
                               s_idb[:, :]),
                           reads=("xg0", "idb"), writes=(f"p{bank}",),
                           close=(dj == 7))
                tr.cop("vector",
                       lambda v, bank=bank, hb=hb, jt=jt: v.tensor_copy(
                           out=s_xnex[:, :].rearrange("p (k s) -> p k s", k=NDK)[
                               :, hb * 8:hb * 8 + 8, jt * 128:(jt + 1) * 128],
                           in_=pbf[bank][:, 0:1024].rearrange("p (k s) -> p k s", k=8)),
                       reads=(f"p{bank}",), writes=("xnex",))

        make_wb(0, 0)
        gchunk = 0
        fill_wh(0, 0, 0)
        for g in range(NGS):
            wbuf, wtile = wbs[g % 2]
            for i in range(NI):
                whbuf, whsm, whbase = whs[gchunk % 2]
                if i + 1 < NI:
                    fill_wh(g, i + 1, (gchunk + 1) % 2)
                elif g + 1 < NG:
                    fill_wh(g + 1, 0, (gchunk + 1) % 2)
                gchunk += 1
                if g == 2 and i == 0:
                    load_lists(0)
                if g == 2 and i == 5:
                    load_lists(1)
                if g == 3 and i < 8:
                    if i % 2 == 0:
                        gather_tile(0, i // 2)
                    else:
                        transpose_tile(0, i // 2)
                if p == 0 and g >= 1 and i % 2 == 0:
                    emit_piece(deferred[g][i // 2], look=3)
                for half in range(2):
                    for th in range(2):
                        bank = half * 2 + th
                        for dk in range(NDK):
                            tr.cop("tensor",
                                   lambda t, whbase=whbase, half=half, th=th,
                                   bank=bank, dk=dk: t.matmul(
                                       psum[bank][:, :],
                                       s_wh[:, whbase + half * 2048 + dk * 128:
                                            whbase + half * 2048 + (dk + 1) * 128],
                                       s_xn[:, dk * TT + th * 512:
                                            dk * TT + (th + 1) * 512],
                                       start=(dk == 0), stop=(dk == NDK - 1)),
                                   reads=(whbuf, "xn"), writes=(f"p{bank}",),
                                   close=(dk == NDK - 1))
                for th in range(2):
                    tr.cop("scalar",
                           lambda sc, th=th: sc.activation(
                               out=s_sg[:, th * 512:(th + 1) * 512], in_=psum[th][:, :],
                               func=AF.Silu, scale=1.0),
                           reads=(f"p{th}",), writes=(f"sgt{th}",))
                    tr.cop("vector",
                           lambda v, th=th, wtile=wtile: v.tensor_tensor(
                               out=s_ut[:, th * 512:(th + 1) * 512],
                               in0=psum[2 + th][:, :],
                               in1=wtile[:, th * 512:(th + 1) * 512], op=ALU.mult),
                           reads=(f"p{2 + th}", wbuf), writes=(f"utt{th}",))
                tr.cop("vector",
                       lambda v, i=i: v.tensor_tensor(
                           out=s_ha[:, i * 1024:(i + 1) * 1024], in0=s_sg[:, :],
                           in1=s_ut[:, :], op=ALU.mult),
                       reads=("sgt0", "sgt1", "utt0", "utt1"), writes=("hA" if i < 8 else "hB",))
            if g + 1 < NGS:
                make_wb(g + 1, (g + 1) % 2)
            fill_wo(g, 0, 0)
            for ec in range(8):
                wobuf, wosm, wotile = wos[ec % 2]
                if ec + 1 < 8:
                    fill_wo(g, ec + 1, (ec + 1) % 2)
                for el in range(2):
                    ek = ec * 2 + el
                    for th in range(2):
                        bank = 4 + (ek * 2 + th) % 2
                        for i in range(NI):
                            tr.cop("tensor",
                                   lambda t, wotile=wotile, i=i, el=el, th=th,
                                   bank=bank: t.matmul(
                                       psum[bank][:, :],
                                       wotile[:, i * 256 + el * 128:
                                              i * 256 + (el + 1) * 128],
                                       s_ha[:, i * 1024 + th * 512:
                                            i * 1024 + (th + 1) * 512],
                                       start=(i == 0), stop=(i == NI - 1)),
                                   reads=(wobuf, "hA" if i < 8 else "hB"),
                                   writes=(f"p{bank}",),
                                   close=(i == NI - 1))
                        tr.cop("vector",
                               lambda v, ek=ek, th=th, bank=bank: v.tensor_tensor(
                                   out=s_out[:, ek * TT + th * 512:
                                             ek * TT + (th + 1) * 512],
                                   in0=psum[bank][:, :],
                                   in1=s_out[:, ek * TT + th * 512:
                                             ek * TT + (th + 1) * 512],
                                   op=ALU.add),
                               reads=(f"p{bank}", "out"), writes=("out",))
        for e in range(NE):
            g = NGS + e
            b4w = (e % 4) * 4
            for i in range(NI):
                whbuf, whsm, whbase = whs[gchunk % 2]
                if i + 1 < NI:
                    fill_wh(g, i + 1, (gchunk + 1) % 2)
                elif g + 1 < NG:
                    fill_wh(g + 1, 0, (gchunk + 1) % 2)
                gchunk += 1
                if p == 0 and e in (0, 1, 2) and i % 2 == 0:
                    emit_piece(deferred[("r", e)][i // 2], look=3)
                if e + 2 < NE and i == 1:
                    load_lists(e + 2)
                ub = 2 * (i % 2)
                for half in range(2):
                    for dk in range(NDK):
                        tr.cop("tensor",
                               lambda t, whbase=whbase, half=half, dk=dk, ub=ub: t.matmul(
                                   psum[ub + half][:, :],
                                   s_wh[:, whbase + half * 2048 + dk * 128:
                                        whbase + half * 2048 + (dk + 1) * 128],
                                   s_xnex[:, dk * CAP:(dk + 1) * CAP],
                                   start=(dk == 0), stop=(dk == NDK - 1)),
                               reads=(whbuf, "xnex"), writes=(f"p{ub + half}",),
                               close=(dk == NDK - 1))
                par = i % 2
                tr.cop("scalar",
                       lambda sc, ub=ub, par=par: sc.activation(
                           out=s_sg[:, par * 512:(par + 1) * 512], in_=psum[ub][:, :],
                           func=AF.Silu, scale=1.0),
                       reads=(f"p{ub}",), writes=(f"sgt{par}",))
                tr.cop("vector",
                       lambda v, ub=ub, par=par: v.tensor_copy(
                           out=s_ut[:, par * 512:(par + 1) * 512],
                           in_=psum[ub + 1][:, :]),
                       reads=(f"p{ub + 1}",), writes=(f"utt{par}",))
                tr.cop("vector",
                       lambda v, i=i, par=par: v.tensor_tensor(
                           out=s_ha[:, i * 512:(i + 1) * 512],
                           in0=s_sg[:, par * 512:(par + 1) * 512],
                           in1=s_ut[:, par * 512:(par + 1) * 512], op=ALU.mult),
                       reads=(f"sgt{par}", f"utt{par}"), writes=("hA",))
            fill_wo(g, 0, 0)
            for ec in range(8):
                wobuf, wosm, wotile = wos[ec % 2]
                if ec + 1 < 8:
                    fill_wo(g, ec + 1, (ec + 1) % 2)
                if e + 1 < NE:
                    if ec % 2 == 0:
                        gather_tile(e + 1, ec // 2)
                    else:
                        transpose_tile(e + 1, ec // 2)
                for jt in range(CT):
                    for i in range(NI):
                        tr.cop("tensor",
                               lambda t, wotile=wotile, i=i, jt=jt: t.matmul(
                                   psum[2 + jt][:, 0:256],
                                   s_ha[:, i * 512 + jt * 128:
                                        i * 512 + (jt + 1) * 128],
                                   wotile[:, i * 256:(i + 1) * 256],
                                   start=(i == 0), stop=(i == NI - 1)),
                               reads=(wobuf, "hA"), writes=(f"p{2 + jt}",),
                               close=(i == NI - 1))
                    tr.cop("scalar",
                           lambda sc, jt=jt, ec=ec, b4w=b4w: sc.activation(
                               out=s_eop[:, (ec % 2) * 1024 + jt * 256:
                                         (ec % 2) * 1024 + (jt + 1) * 256],
                               in_=psum[2 + jt][:, 0:256], func=AF.Identity,
                               scale=s_gw[:, b4w + jt:b4w + jt + 1]),
                           reads=(f"p{2 + jt}", "gw"), writes=(f"eop{ec % 2}",))
                    tr.op("scalar",
                          lambda s, e=e, jt=jt, ec=ec: s.dma_start(
                              out=eout_d[e * CAP + jt * 128: e * CAP + (jt + 1) * 128,
                                         ec * 256:(ec + 1) * 256],
                              in_=s_eop[:, (ec % 2) * 1024 + jt * 256:
                                        (ec % 2) * 1024 + (jt + 1) * 256]),
                          reads=(f"eop{ec % 2}",), writes=("eout",), sem="sq_eo", inc=16)
        def bg_slot(slot):
            if slot < 4:
                return s_xnex[:, slot * 2048:(slot + 1) * 2048]
            return s_ha[:, (slot - 4) * 2048:(slot - 3) * 2048]

        def back_gather(tc, cls):
            slot = (2 * tc + cls) % 8
            st = bg_slot(slot)
            bgt, bgb = (s_bg1, "bg1") if cls == 0 else (s_bg2, "bg2")
            tr.op("gpsimd",
                  lambda g, st=st, bgt=bgt, tc=tc: g.indirect_dma_start(
                      out=st, out_offset=None, in_=eout_d,
                      in_offset=bass.IndirectOffsetOnAxis(
                          ap=bgt[:, tc:tc + 1], axis=0)),
                  reads=(bgb, "eout"),
                  writes=(f"bgs{slot}", "xnex" if slot < 4 else "hA"),
                  sem="sq_bg", inc=16)

        for tc in range(4):
            back_gather(tc, 0)
            back_gather(tc, 1)
        otm = [(s_wh[:, 0:4096].bitcast(F32), ("wh0",)),
               (s_wh[:, 4096:8192].bitcast(F32), ("wh1",))]
        for tc in range(TT // 128):
            sb, ssem, stile = xslot[tc % 2]
            row = t0 + tc * 128
            tr.op("scalar",
                  lambda s, stile=stile, row=row: s.dma_start(
                      out=stile[:, :], in_=x_d[row:row + 128, :]),
                  writes=(sb,), sem=ssem, inc=16)
            ot, otbufs = otm[tc % 2]
            slot1 = (2 * tc) % 8
            slot2 = (2 * tc + 1) % 8
            g1, g2 = bg_slot(slot1), bg_slot(slot2)
            for quad in range(4):
                bank = quad
                gbank = 4 + quad
                for ej in range(4):
                    ek = quad * 4 + ej
                    tr.cop("tensor",
                           lambda t, bank=bank, ej=ej, ek=ek, tc=tc: t.transpose(
                               psum[bank][:, ej * 128:(ej + 1) * 128],
                               s_out[:, ek * TT + tc * 128: ek * TT + (tc + 1) * 128],
                               s_idf[:, :]),
                           reads=("out", "idf"), writes=(f"p{bank}",),
                           close=(ej == 3))
                tr.cop("tensor",
                       lambda t, gbank=gbank, quad=quad, g1=g1: t.matmul(
                           psum[gbank][:, 0:512],
                           s_idb[:, 0:128],
                           g1[:, quad * 512:(quad + 1) * 512],
                           start=True, stop=False),
                       reads=(f"bgs{slot1}", "idb",
                              "xnex" if slot1 < 4 else "hA"),
                       writes=(f"p{gbank}",), close=False)
                tr.cop("tensor",
                       lambda t, gbank=gbank, quad=quad, g2=g2: t.matmul(
                           psum[gbank][:, 0:512],
                           s_idb[:, 0:128],
                           g2[:, quad * 512:(quad + 1) * 512],
                           start=False, stop=True),
                       reads=(f"bgs{slot2}", "idb",
                              "xnex" if slot2 < 4 else "hA"),
                       writes=(f"p{gbank}",), close=True)
                tr.cop("vector",
                       lambda v, bank=bank, quad=quad, ot=ot, stile=stile:
                       v.tensor_tensor(
                           out=ot[:, quad * 512:(quad + 1) * 512],
                           in0=psum[bank][:, 0:512],
                           in1=stile[:, quad * 512:(quad + 1) * 512], op=ALU.add),
                       reads=(f"p{bank}", sb), writes=otbufs)
                tr.cop("vector",
                       lambda v, gbank=gbank, quad=quad, ot=ot:
                       v.tensor_tensor(
                           out=ot[:, quad * 512:(quad + 1) * 512],
                           in0=psum[gbank][:, 0:512],
                           in1=ot[:, quad * 512:(quad + 1) * 512], op=ALU.add),
                       reads=(f"p{gbank}",) + otbufs, writes=otbufs)
            tr.op("sync" if tc % 2 == 0 else "scalar",
                  lambda s, ot=ot, row=row: s.dma_start(
                      out=out_d[row:row + 128, :], in_=ot[:, :]),
                  reads=otbufs, writes=(f"sto{tc % 2}",),
                  sem=f"sq_st{tc % 2}", inc=16)
            n_store += 1
            if tc + 4 < TT // 128:
                back_gather(tc + 4, 0)
                back_gather(tc + 4, 1)

    tr.barrier_wait("vector", "sq_st0", 16 * (n_store // 2))
    tr.barrier_wait("vector", "sq_st1", 16 * (n_store // 2))

    block = E(nc.Block())
    tr.replay(block)
    ctx.close()
    return nc


_NC_CACHE = {}


def _get_nc():
    if "nc" not in _NC_CACHE:
        _NC_CACHE["nc"] = build_nc()
    return _NC_CACHE["nc"]


NEH = 8


def make_in_maps(inputs):
    x = np.ascontiguousarray(np.asarray(inputs["L0_attn_x"], dtype=np.float32))
    bqkvg = np.stack([np.asarray(inputs["L0_attn_bq"]), np.asarray(inputs["L0_attn_bk"]),
                      np.asarray(inputs["L0_attn_bv"]), np.asarray(inputs["L0_attn_bg"])],
                     axis=0)
    bqkvg = np.ascontiguousarray(bqkvg, dtype=np.float32)
    gw9 = np.concatenate([np.asarray(inputs["L0_ffn_gate_w"]),
                          np.asarray(inputs["L0_ffn_shared_gate_w"])], axis=0)
    gw9 = np.ascontiguousarray(gw9, dtype=np.float32)
    idf = np.eye(128, dtype=np.float32)
    idb = np.eye(128).astype(ml_dtypes.bfloat16)
    cst = np.zeros((128, 138), dtype=np.float32)
    cst[:, 0:128] = np.triu(np.ones((128, 128), dtype=np.float32))
    cst[:, 128:136] = np.arange(8, dtype=np.float32)[None, :]
    cst[:, 136] = np.arange(128, dtype=np.float32)
    cst[:, 137] = 1.0
    sel = np.zeros((9, 9, 128), dtype=np.float32)
    for r in range(9):
        sel[r, r, :] = 1.0
    sel = np.ascontiguousarray(sel.reshape(9, 9 * 128)).astype(ml_dtypes.bfloat16)
    def sw_up(w, rc):
        w = np.asarray(w, dtype=np.float32)
        return np.ascontiguousarray(
            w.reshape(rc, 128, 16, 128).transpose(0, 3, 2, 1).reshape(rc * 128, 2048))

    w2f = np.asarray(inputs["L0_ffn_w2"], dtype=np.float32)
    w2sw = np.ascontiguousarray(
        w2f.reshape(NEH, 8, 2, 128, 11, 128).transpose(0, 1, 5, 4, 2, 3)
        .reshape(NEH, 1024, 2816))
    wsdf = np.asarray(inputs["L0_ffn_shared_down_w"], dtype=np.float32)
    wsdsw = np.ascontiguousarray(
        wsdf.reshape(8, 2, 128, 4, 11, 128).transpose(3, 0, 5, 4, 1, 2)
        .reshape(4 * 1024, 2816))
    common = dict(
        wq=sw_up(inputs["L0_attn_Wq"], 16),
        wk=sw_up(inputs["L0_attn_Wk"], 16),
        wv=sw_up(inputs["L0_attn_Wv"], 16),
        wg=sw_up(inputs["L0_attn_Wg"], 16),
        bqkvg=bqkvg, gw9=gw9,
        wgu=sw_up(inputs["L0_ffn_shared_gate_up_w"], 88),
        wsd=wsdsw,
        w13=np.ascontiguousarray(
            np.asarray(inputs["L0_ffn_w13"], dtype=np.float32)
            .reshape(NEH, 22, 128, 16, 128).transpose(0, 1, 4, 3, 2)
            .reshape(NEH, 2816, 2048)),
        w2=w2sw,
        idf=idf, idb=idb, sel=sel, cst=cst,
    )
    n = x.shape[0]
    return [dict(common, x=np.ascontiguousarray(x[i])) for i in range(n)]


def kernel(**inputs):
    in_maps = make_in_maps(inputs)
    nc = _get_nc()
    res = run_bass_kernel_spmd(nc, in_maps, core_ids=list(range(8)))
    return np.stack([r["out"] for r in res.results], axis=0)


# version 09
# speedup vs baseline: 1.0949x; 1.0503x over previous
import numpy as np
import ml_dtypes
import concourse.bass as bass
import concourse.mybir as mybir
from concourse.bass_utils import run_bass_kernel_spmd

F32 = mybir.dt.float32
BF16 = mybir.dt.bfloat16
AF = mybir.ActivationFunctionType
ALU = mybir.AluOpType
AX = mybir.AxisListType

S = 4096
D = 2048
TT = 1024
NPASS = S // TT
NDK = D // 128
MOE_I = 1408
SH_I = 5632
NE = 8
NG = 12
NGS = 4
NI = MOE_I // 128
EPS = 1e-5
CAP = 512
CT = CAP // 128
PADV = 1.0e6

ENGS = ("tensor", "vector", "scalar", "sync", "gpsimd")
GAP = 8


class Tracker:

    def __init__(self, nc):
        self.nc = nc
        self.q = {e: [] for e in ENGS}
        self.cnt = {}
        self.bufs = {}
        self.opidx = {e: 0 for e in ENGS}
        self.waited = {e: {} for e in ENGS}
        self.pending = []
        self.esem = {}

    def sem(self, name):
        self.cnt.setdefault(name, 0)
        return name

    def engine_sems(self, names):
        for e, s in names.items():
            self.esem[e] = self.sem(s)

    def _want(self, need, sem, val):
        if need.get(sem, 0) < val:
            need[sem] = val

    def _edge(self, eng, weng, widx, wdma):
        if weng is None:
            return False
        if wdma or weng != eng:
            return True
        return (self.opidx[eng] - widx) < GAP

    def op(self, eng, fn, reads=(), writes=(), sem=None, inc=0, close=True,
           manual_waits=()):
        need = {}
        for (msem, mval) in manual_waits:
            self._want(need, msem, mval)
        for b in reads:
            rec = self.bufs.get(b)
            if rec and rec["w"] and self._edge(eng, rec["weng"], rec["widx"], rec["wdma"]):
                self._want(need, rec["w"][0], rec["w"][1])
        for b in writes:
            rec = self.bufs.get(b)
            if rec:
                if rec["w"] and self._edge(eng, rec["weng"], rec["widx"], rec["wdma"]):
                    self._want(need, rec["w"][0], rec["w"][1])
                for rsem, (rcnt, reng, ridx, rdma) in rec["r"].items():
                    if self._edge(eng, reng, ridx, rdma):
                        self._want(need, rsem, rcnt)
        for s, v in sorted(need.items()):
            if self.waited[eng].get(s, 0) < v:
                self.q[eng].append(("wait", s, v))
                self.waited[eng][s] = v
        idx = self.opidx[eng]
        self.opidx[eng] += 1
        is_dma = inc >= 16
        ev = None
        if inc:
            self.cnt[sem] += inc
            ev = (sem, self.cnt[sem])
        self.q[eng].append(("op", fn, sem if inc else None, inc))
        if not close:
            self.pending.append((eng, idx, tuple(reads), tuple(writes)))
            return None
        if ev:
            for (peng, pidx, preads, pwrites) in self.pending:
                if peng != eng:
                    continue
                for b in preads:
                    self._note_read(b, ev, peng, pidx, is_dma)
                for b in pwrites:
                    self._note_write(b, ev, peng, pidx, is_dma)
            self.pending = [p for p in self.pending if p[0] != eng]
        for b in reads:
            self._note_read(b, ev, eng, idx, is_dma)
        for b in writes:
            self._note_write(b, ev, eng, idx, is_dma)
        return ev

    def _note_read(self, b, ev, eng, idx, dma):
        rec = self.bufs.setdefault(
            b, {"w": None, "weng": None, "widx": -1 << 30, "wdma": False, "r": {}})
        if ev is None:
            return
        rec["r"][ev[0]] = (ev[1], eng, idx, dma)

    def _note_write(self, b, ev, eng, idx, dma):
        self.bufs[b] = {"w": ev, "weng": eng, "widx": idx, "wdma": dma, "r": {}}

    def cop(self, eng, fn, reads=(), writes=(), close=True, manual_waits=()):
        return self.op(eng, fn, reads, writes, sem=self.esem[eng],
                       inc=1 if close else 0, close=close,
                       manual_waits=manual_waits)

    def barrier_wait(self, eng, sem, val):
        if self.waited[eng].get(sem, 0) < val:
            self.q[eng].append(("wait", sem, val))
            self.waited[eng][sem] = val

    def replay(self, block):
        for name in ENGS:
            items = self.q[name]

            def make(items=items):
                def body(eng):
                    for it in items:
                        if it[0] == "wait":
                            eng.wait_ge(self.nc._sems[it[1]], it[2])
                        else:
                            _, fn, sm, inc = it
                            ins = fn(eng)
                            if sm is not None:
                                ins.then_inc(self.nc._sems[sm], inc)
                return body

            getattr(block, name)(make())


def build_nc():
    nc = bass.Bass("TRN2", target_bir_lowering=False)

    x_d = nc.dram_tensor("x", [S, D], F32, kind="ExternalInput").ap()
    wq_d = nc.dram_tensor("wq", [D, D], F32, kind="ExternalInput").ap()
    wk_d = nc.dram_tensor("wk", [D, D], F32, kind="ExternalInput").ap()
    wv_d = nc.dram_tensor("wv", [D, D], F32, kind="ExternalInput").ap()
    wg_d = nc.dram_tensor("wg", [D, D], F32, kind="ExternalInput").ap()
    bqkvg_d = nc.dram_tensor("bqkvg", [4, D], F32, kind="ExternalInput").ap()
    gw9_d = nc.dram_tensor("gw9", [9, D], F32, kind="ExternalInput").ap()
    wgu_d = nc.dram_tensor("wgu", [2 * SH_I, D], F32, kind="ExternalInput").ap()
    wsd_d = nc.dram_tensor("wsd", [4 * 1024, 2816], F32, kind="ExternalInput").ap()
    w13_d = nc.dram_tensor("w13", [NE, 2 * MOE_I, D], F32, kind="ExternalInput").ap()
    w2_d = nc.dram_tensor("w2", [NE, 1024, 2816], F32, kind="ExternalInput").ap()
    idf_d = nc.dram_tensor("idf", [128, 128], F32, kind="ExternalInput").ap()
    idb_d = nc.dram_tensor("idb", [128, 128], BF16, kind="ExternalInput").ap()
    sel_d = nc.dram_tensor("sel", [9, 9 * 128], BF16, kind="ExternalInput").ap()
    cst_d = nc.dram_tensor("cst", [128, 138], F32, kind="ExternalInput").ap()
    out_d = nc.dram_tensor("out", [S, D], F32, kind="ExternalOutput").ap()

    lists_d = nc.dram_tensor("lists", [NE * CAP, 2], F32, kind="Internal").ap()
    eout_d = nc.dram_tensor("eout", [NE * CAP, D], BF16, kind="Internal").ap()
    xtok_d = nc.dram_tensor("xtok", [TT + 128, D], BF16, kind="Internal").ap()

    zq = nc.dram_tensor("zq", [D, D], BF16, kind="Internal").ap()
    zk = nc.dram_tensor("zk", [D, D], BF16, kind="Internal").ap()
    zv = nc.dram_tensor("zv", [D, D], BF16, kind="Internal").ap()
    zg = nc.dram_tensor("zg", [D, D], BF16, kind="Internal").ap()
    zgu = nc.dram_tensor("zgu", [2 * SH_I, D], BF16, kind="Internal").ap()
    zsd = nc.dram_tensor("zsd", [4 * 1024, 2816], BF16, kind="Internal").ap()
    z13 = nc.dram_tensor("z13", [NE, 2 * MOE_I, D], BF16, kind="Internal").ap()
    z2 = nc.dram_tensor("z2", [NE, 1024, 2816], BF16, kind="Internal").ap()

    from contextlib import ExitStack
    ctx = ExitStack()
    E = ctx.enter_context

    s_xn = E(nc.sbuf_tensor("s_xn", [128, NDK * TT], BF16))
    s_out = E(nc.sbuf_tensor("s_out", [128, NDK * TT], F32))
    s_ha = E(nc.sbuf_tensor("s_ha", [128, 11264], BF16))
    s_q = s_ha[:, 8192:8192 + TT]
    s_k = s_ha[:, 9216:9216 + TT]
    s_g = s_ha[:, 10240:10240 + TT]
    s_wh = E(nc.sbuf_tensor("s_wh", [128, 8192], BF16))
    s_wo0 = E(nc.sbuf_tensor("s_wo0", [128, NI * 256], BF16))
    s_wo1 = E(nc.sbuf_tensor("s_wo1", [128, NI * 256], BF16))
    s_xa = E(nc.sbuf_tensor("s_xa", [128, D], F32))
    s_xb = E(nc.sbuf_tensor("s_xb", [128, D], F32))
    s_xntm = E(nc.sbuf_tensor("s_xntm", [128, D], BF16))
    s_sg = E(nc.sbuf_tensor("s_sg", [128, TT], BF16))
    s_qg = s_sg
    s_ut = E(nc.sbuf_tensor("s_ut", [128, TT], BF16))
    s_kv = s_ut
    s_wb0 = E(nc.sbuf_tensor("s_wb0", [128, TT], BF16))
    s_wb1 = E(nc.sbuf_tensor("s_wb1", [128, TT], BF16))
    s_w9fm = E(nc.sbuf_tensor("s_w9fm", [9, TT], BF16))
    s_ones = E(nc.sbuf_tensor("s_ones", [128, TT], BF16))
    s_carry = E(nc.sbuf_tensor("s_carry", [128, NDK], F32))
    s_bias = E(nc.sbuf_tensor("s_bias", [128, 4 * NDK], F32))
    s_gw9T = E(nc.sbuf_tensor("s_gw9T", [128, NDK * 9], BF16))
    s_sel = E(nc.sbuf_tensor("s_sel", [9, 128], BF16))
    s_idf = E(nc.sbuf_tensor("s_idf", [128, 128], F32))
    s_idb = E(nc.sbuf_tensor("s_idb", [128, 128], BF16))
    s_w9tm = E(nc.sbuf_tensor("s_w9tm", [128, 32], BF16))
    s_m1 = E(nc.sbuf_tensor("s_m1", [128, 1], F32))
    s_m2 = E(nc.sbuf_tensor("s_m2", [128, 1], F32))
    s_negm1 = E(nc.sbuf_tensor("s_negm1", [128, 1], F32))
    s_sum = E(nc.sbuf_tensor("s_sum", [128, 1], F32))
    s_rinv = E(nc.sbuf_tensor("s_rinv", [128, 1], F32))
    s_exp = E(nc.sbuf_tensor("s_exp", [128, 8], F32))
    s_bn = E(nc.sbuf_tensor("s_bn", [128, 24], F32))
    s_mv = E(nc.sbuf_tensor("s_mv", [128, 2], F32))
    s_lnv = E(nc.sbuf_tensor("s_lnv", [128, 1], F32))
    s_rstd = E(nc.sbuf_tensor("s_rstd", [128, 1], F32))
    s_eps = E(nc.sbuf_tensor("s_eps", [128, 1], F32))
    U32 = mybir.dt.uint32
    s_cst = E(nc.sbuf_tensor("s_cst", [128, 138], F32))
    s_xnex = E(nc.sbuf_tensor("s_xnex", [128, NDK * CAP], BF16))
    s_xg = E(nc.sbuf_tensor("s_xg", [128, D], BF16))
    s_glst = E(nc.sbuf_tensor("s_glst", [128, 16], F32))
    s_gidx = E(nc.sbuf_tensor("s_gidx", [128, 8], U32))
    s_gw = E(nc.sbuf_tensor("s_gw", [128, 16], F32))
    s_i1 = E(nc.sbuf_tensor("s_i1", [128, 8], F32))
    s_i2 = E(nc.sbuf_tensor("s_i2", [128, 8], U32))
    s_oh1 = E(nc.sbuf_tensor("s_oh1", [128, 8], F32))
    s_oh2 = E(nc.sbuf_tensor("s_oh2", [128, 8], F32))
    s_me = E(nc.sbuf_tensor("s_me", [128, 16], F32))
    s_gif = s_me
    s_pe8 = E(nc.sbuf_tensor("s_pe8", [128, 8], F32))
    s_tmp8 = E(nc.sbuf_tensor("s_tmp8", [128, 8], F32))
    s_base = E(nc.sbuf_tensor("s_base", [1, 8], F32))
    s_fl1 = E(nc.sbuf_tensor("s_fl1", [128, 1], F32))
    s_fl2 = E(nc.sbuf_tensor("s_fl2", [128, 1], F32))
    s_bg1 = E(nc.sbuf_tensor("s_bg1", [128, 8], U32))
    s_bg2 = E(nc.sbuf_tensor("s_bg2", [128, 8], U32))
    s_sc1 = E(nc.sbuf_tensor("s_sc1", [128, 4], F32))
    s_sc2 = E(nc.sbuf_tensor("s_sc2", [128, 4], F32))
    s_eop = E(nc.sbuf_tensor("s_eop", [128, 2048], BF16))
    s_cs = s_eop[:, :].bitcast(F32)
    s_fill = E(nc.sbuf_tensor("s_fill", [128, 16], F32))

    psum = [E(nc.psum_tensor(f"p{i}", [128, 512], F32)) for i in range(8)]
    pbf = [psum[i][:, :].bitcast(BF16) for i in range(8)]

    tr = Tracker(nc)
    sem_names = ["se_pe", "se_dve", "se_act", "sq_wa0", "sq_wa1",
                 "sq_wh0", "sq_wh1", "sq_wo0", "sq_wo1", "sq_xa", "sq_xb",
                 "sq_xc",
                 "sq_st0", "sq_st1", "sq_m0", "sq_m1", "sq_m2", "sq_m3", "sq_m4",
                 "sq_m5", "sq_sc", "sq_ll", "sq_gx", "sq_bg", "sq_eo", "sq_xt",
                 "sq_lf"]
    cast_sems = [f"sq_c{i}" for i in range(29)]
    sem_names += cast_sems
    nc._sems = {}
    for s in sem_names:
        nc._sems[s] = E(nc.semaphore(s))
        tr.sem(s)
    tr.engine_sems({"tensor": "se_pe", "vector": "se_dve", "scalar": "se_act"})

    def z13_pieces(e):
        return [(f"scr_13_{e}p{j}", z13[e][j * 704:(j + 1) * 704, :],
                 w13_d[e][j * 704:(j + 1) * 704, :]) for j in range(4)]

    def z2_pieces(e):
        return [(f"scr_2_{e}p{j}", z2[e][j * 512:(j + 1) * 512, :],
                 w2_d[e][j * 512:(j + 1) * 512, :]) for j in range(2)]

    upfront = []
    attn_mats = [("q", zq, wq_d), ("k", zk, wk_d), ("v", zv, wv_d), ("g", zg, wg_d)]
    for ch in range(4):
        for nm, zdst, wsrc in attn_mats:
            upfront.append((f"scr_{nm}{ch}", zdst[ch * 512:(ch + 1) * 512, :],
                            wsrc[ch * 512:(ch + 1) * 512, :]))
    for gi in range(4):
        upfront.append((f"scr_gu_g{gi}", zgu[gi * MOE_I:(gi + 1) * MOE_I, :],
                        wgu_d[gi * MOE_I:(gi + 1) * MOE_I, :]))
        upfront.append((f"scr_gu_u{gi}", zgu[SH_I + gi * MOE_I:SH_I + (gi + 1) * MOE_I, :],
                        wgu_d[SH_I + gi * MOE_I:SH_I + (gi + 1) * MOE_I, :]))
    upfront.append(("scr_sd", zsd, wsd_d))
    for e in (0, 1):
        upfront += z13_pieces(e) + z2_pieces(e)
    deferred = {}
    for g in (1, 2, 3):
        deferred[g] = z13_pieces(g + 1) + z2_pieces(g + 1)
    for e in (0, 1, 2):
        deferred[("r", e)] = z13_pieces(e + 5) + z2_pieces(e + 5)
    n_cast = len(upfront) + sum(len(v) for v in deferred.values())

    cast_state = {"n": 0, "evs": []}

    def emit_piece(pc, look=2):
        nm, dst, src = pc
        evs = cast_state["evs"]
        waits = [evs[-look]] if len(evs) >= look else []
        sem = cast_sems[cast_state["n"] % len(cast_sems)]
        ev = tr.op("gpsimd", lambda g, dst=dst, src=src: g.dma_start(out=dst, in_=src),
                   writes=(nm,), sem=sem, inc=16, manual_waits=waits)
        evs.append(ev)
        cast_state["n"] += 1

    for pc in upfront:
        emit_piece(pc, look=2)

    def gu_src(g, half, i):
        if g < 4:
            return zgu, (SH_I if half else 0) + g * MOE_I, f"scr_gu_{'u' if half else 'g'}{g}"
        e = g - 4
        rbase = MOE_I if half else 0
        return z13[e], rbase, f"scr_13_{e}p{(rbase + i * 128) // 704}"

    def w2_src(g, ec):
        if g < 4:
            return zsd, g * 1024, "scr_sd"
        return z2[g - 4], 0, f"scr_2_{g - 4}p{(ec * 128) // 512}"

    def wrow(g):
        return 8 if g < 4 else g - 4

    tr.op("sync", lambda s: s.dma_start(out=s_xa[0:9, :], in_=gw9_d),
          writes=("xa",), sem="sq_m0", inc=16)
    tr.op("sync", lambda s: s.dma_start(out=s_out[0:1, 0:4 * D],
                                        in_=bqkvg_d.rearrange("j d -> (j d)")[None, :]),
          writes=("out",), sem="sq_m1", inc=16)
    tr.op("sync", lambda s: s.dma_start(out=s_sel[:, :], in_=sel_d[:, 8 * 128:9 * 128]),
          writes=("sel",), sem="sq_m2", inc=16)
    tr.op("sync", lambda s: s.dma_start(out=s_idf[:, :], in_=idf_d),
          writes=("idf",), sem="sq_m3", inc=16)
    tr.op("sync", lambda s: s.dma_start(out=s_idb[:, :], in_=idb_d),
          writes=("idb",), sem="sq_m4", inc=16)
    tr.op("sync", lambda s: s.dma_start(out=s_cst[:, :], in_=cst_d),
          writes=("cst",), sem="sq_m5", inc=16)
    tr.cop("vector", lambda v: v.memset(s_ones[:, :], 1.0), writes=("ones",))
    tr.cop("vector", lambda v: v.memset(s_carry[:, :], 0.0), writes=("carry",))
    tr.cop("vector", lambda v: v.memset(s_eps[:, :], EPS), writes=("eps",))
    tr.cop("vector", lambda v: v.memset(s_fill[:, :], PADV), writes=("fillc",))
    tr.cop("vector", lambda v: v.memset(s_xg[:, :], 0.0), writes=("xg0",))
    for dk in range(NDK):
        tr.cop("tensor",
               lambda t, dk=dk: t.matmul(psum[0][:, dk * 9:dk * 9 + 9],
                                         s_xa[0:9, dk * 128:(dk + 1) * 128],
                                         s_idf[0:9, 0:9], start=True, stop=True),
               reads=("xa", "idf"), writes=("p0",), close=(dk == NDK - 1))
    tr.cop("vector", lambda v: v.tensor_copy(out=s_gw9T[:, :], in_=psum[0][:, 0:NDK * 9]),
           reads=("p0",), writes=("gw9T",))
    for j in range(4):
        for c in range(NDK):
            tr.cop("tensor",
                   lambda t, j=j, c=c: t.matmul(
                       psum[1][:, j * NDK + c:j * NDK + c + 1],
                       s_out[0:1, j * D + c * 128: j * D + (c + 1) * 128],
                       s_idf[0:1, 0:1], start=True, stop=True),
                   reads=("out", "idf"), writes=("p1",),
                   close=(j == 3 and c == NDK - 1))
    tr.cop("vector", lambda v: v.tensor_copy(out=s_bias[:, :], in_=psum[1][:, 0:4 * NDK]),
           reads=("p1",), writes=("bias",))

    xslot = [("xa", "sq_xa", s_xa), ("xb", "sq_xb", s_xb)]
    s_xc = s_xnex[:, 0:4096].bitcast(F32)
    xslot3 = [("xa", "sq_xa", s_xa), ("xb", "sq_xb", s_xb),
              ("xc", "sq_xc", s_xc)]
    n_store = 0

    for p in range(NPASS):
        t0 = p * TT
        for tc in range(TT // 128):
            sb, ssem, stile = xslot3[tc % 3]
            row = t0 + tc * 128
            lq = ("sync", "scalar", "sync" if p == 0 else "gpsimd")[tc % 3]
            tr.op(lq,
                  lambda s, stile=stile, row=row: s.dma_start(
                      out=stile[:, :], in_=x_d[row:row + 128, :]),
                  writes=(sb,) if sb != "xc" else ("xc", "xnex"), sem=ssem, inc=16)
            for i in range(4):
                tr.cop("vector",
                       lambda v, i=i, stile=stile: v.bn_stats(
                           s_bn[:, i * 6:(i + 1) * 6], stile[:, i * 512:(i + 1) * 512]),
                       reads=(sb,), writes=("bn",))
            tr.cop("vector", lambda v: v.bn_aggr(s_mv[:, :], s_bn[:, :]),
                   reads=("bn",), writes=("mv",))
            tr.cop("scalar", lambda sc: sc.activation(out=s_lnv[:, :], in_=s_mv[:, 1:2],
                                                      func=AF.Ln, bias=s_eps[:, 0:1],
                                                      scale=1.0),
                   reads=("mv", "eps"), writes=("lnv",))
            tr.cop("scalar", lambda sc: sc.activation(out=s_rstd[:, :], in_=s_lnv[:, :],
                                                      func=AF.Exp, scale=-0.5),
                   reads=("lnv",), writes=("rstd",))
            tr.cop("vector",
                   lambda v, stile=stile: v.tensor_scalar(
                       out=s_xntm[:, :], in0=stile[:, :], scalar1=s_mv[:, 0:1],
                       scalar2=s_rstd[:, 0:1], op0=ALU.subtract, op1=ALU.mult),
                   reads=(sb, "mv", "rstd"), writes=("xntm",))
            tr.op("scalar",
                  lambda s, tc=tc: s.dma_start(
                      out=xtok_d[tc * 128:(tc + 1) * 128, :], in_=s_xntm[:, :]),
                  reads=("xntm",), writes=("xtok",), sem="sq_xt", inc=16)
            for quad in range(4):
                bank = quad % 2
                for dj in range(4):
                    dk = quad * 4 + dj
                    tr.cop("tensor",
                           lambda t, bank=bank, dj=dj, dk=dk: t.transpose(
                               pbf[bank][:, dj * 128:(dj + 1) * 128],
                               s_xntm[:, dk * 128:(dk + 1) * 128], s_idb[:, :]),
                           reads=("xntm", "idb"), writes=(f"p{bank}",),
                           close=(dj == 3))
                dk0 = quad * 4
                tr.cop("vector",
                       lambda v, bank=bank, dk0=dk0, tc=tc: v.tensor_copy(
                           out=s_xn[:, :].rearrange("p (k t) -> p k t", k=NDK)[
                               :, dk0:dk0 + 4, tc * 128:(tc + 1) * 128],
                           in_=pbf[bank][:, 0:512].rearrange("p (k t) -> p k t", k=4)),
                       reads=(f"p{bank}",), writes=("xn",))

        tr.cop("vector", lambda v: v.memset(s_base[:, :], 0.0), writes=("base",))
        for fh in range(4):
            tr.op("gpsimd", lambda g, fh=fh: g.dma_start(
                out=lists_d[fh * 1024:(fh + 1) * 1024, :].rearrange(
                    "(p a) b -> p (a b)", p=128),
                in_=s_fill[:, :]),
                  reads=("fillc",), writes=("lists",), sem="sq_lf", inc=16)
        def r_logits(tc):
            bl = 6 + tc % 2
            for dk in range(NDK):
                tr.cop("tensor",
                       lambda t, tc=tc, dk=dk, bl=bl: t.matmul(
                           psum[bl][:, 0:9],
                           s_xn[:, dk * TT + tc * 128: dk * TT + (tc + 1) * 128],
                           s_gw9T[:, dk * 9:(dk + 1) * 9],
                           start=(dk == 0), stop=(dk == NDK - 1)),
                       reads=("xn", "gw9T"), writes=(f"p{bl}",), close=(dk == NDK - 1))

        def r_chain_a(tc):
            bl = 6 + tc % 2
            tr.cop("vector", lambda v, bl=bl: v.max(s_i1[:, :], psum[bl][:, 0:8]),
                   reads=(f"p{bl}",), writes=("i1",))
            tr.cop("vector", lambda v, bl=bl: v.max_index(s_i2[:, :], s_i1[:, :], psum[bl][:, 0:8]),
                   reads=("i1", f"p{bl}"), writes=("i2",))

        def r_chain_b(tc):
            bl = 6 + tc % 2
            par = tc % 2
            w9c = par * 16
            mec = par * 8
            tr.cop("vector", lambda v: v.tensor_scalar(
                out=s_negm1[:, :], in0=s_i1[:, 0:1], scalar1=-1.0, scalar2=None,
                op0=ALU.mult),
                   reads=("i1",), writes=("negm1",))
            tr.cop("scalar", lambda sc, bl=bl: sc.activation(
                out=s_exp[:, :], in_=psum[bl][:, 0:8], func=AF.Exp,
                bias=s_negm1[:, 0:1], scale=1.0, accum_out=s_sum[:, 0:1]),
                   reads=(f"p{bl}", "negm1"), writes=("exp", "sum"))
            tr.cop("scalar", lambda sc, bl=bl, w9c=w9c: sc.activation(
                out=s_w9tm[:, w9c + 8:w9c + 9], in_=psum[bl][:, 8:9],
                func=AF.Sigmoid, scale=1.0),
                   reads=(f"p{bl}",), writes=(f"w9tm{par}",))
            tr.cop("vector", lambda v: v.tensor_copy(
                out=s_tmp8[:, 0:2], in_=s_i2[:, 0:2]),
                   reads=("i2",), writes=("tmp8",))
            tr.cop("vector", lambda v: v.tensor_single_scalar(
                out=s_oh1[:, :], in_=s_cst[:, 128:136], scalar=s_tmp8[:, 0:1],
                op=ALU.is_equal),
                   reads=("cst", "tmp8"), writes=("oh1",))
            tr.cop("vector", lambda v: v.tensor_single_scalar(
                out=s_oh2[:, :], in_=s_cst[:, 128:136], scalar=s_tmp8[:, 1:2],
                op=ALU.is_equal),
                   reads=("cst", "tmp8"), writes=("oh2",))
            tr.cop("vector", lambda v, mec=mec: v.tensor_tensor(
                out=s_me[:, mec:mec + 8], in0=s_oh1[:, :], in1=s_oh2[:, :], op=ALU.add),
                   reads=("oh1", "oh2"), writes=(f"me{par}",))
            tr.cop("vector", lambda v: v.reciprocal(s_rinv[:, :], s_sum[:, :]),
                   reads=("sum",), writes=("rinv",))
            tr.cop("vector", lambda v: v.tensor_scalar(
                out=s_exp[:, :], in0=s_exp[:, :], scalar1=s_rinv[:, 0:1], scalar2=None,
                op0=ALU.mult),
                   reads=("exp", "rinv"), writes=("exp",))
            tr.cop("vector", lambda v, w9c=w9c, mec=mec: v.tensor_tensor(
                out=s_w9tm[:, w9c:w9c + 8], in0=s_exp[:, :],
                in1=s_me[:, mec:mec + 8], op=ALU.mult),
                   reads=("exp", f"me{par}"), writes=(f"w9tm{par}",))

        def r_pe(tc):
            bp = tc % 2
            bb = 4 + tc % 2
            par = tc % 2
            w9c = par * 16
            mec = par * 8
            tr.cop("tensor",
                   lambda t, tc=tc, w9c=w9c: t.transpose(
                       pbf[3][0:9, tc * 128:(tc + 1) * 128],
                       s_w9tm[:, w9c:w9c + 9], s_idb[:, :]),
                   reads=(f"w9tm{par}", "idb"), writes=("p3",))
            tr.cop("tensor", lambda t, bb=bb, mec=mec: t.matmul(
                psum[bb][0:1, 0:8], s_cst[:, 137:138], s_me[:, mec:mec + 8],
                start=True, stop=True),
                   reads=("cst", f"me{par}"), writes=(f"p{bb}",))
            tr.cop("tensor", lambda t, bp=bp, mec=mec: t.matmul(
                psum[bp][:, 0:8], s_cst[:, 0:128], s_me[:, mec:mec + 8],
                start=True, stop=False),
                   reads=("cst", f"me{par}"), writes=(f"p{bp}",), close=False)
            tr.cop("tensor", lambda t, bp=bp: t.matmul(
                psum[bp][:, 0:8], s_cst[0:1, 0:128], s_base[0:1, :],
                start=False, stop=True),
                   reads=("cst", "base"), writes=(f"p{bp}",))

        def r_post(tc):
            bp = tc % 2
            bb = 4 + tc % 2
            par = tc % 2
            w9c = par * 16
            tr.cop("vector", lambda v, bb=bb: v.tensor_tensor(
                out=s_base[:, :], in0=s_base[:, :], in1=psum[bb][0:1, 0:8], op=ALU.add),
                   reads=("base", f"p{bb}"), writes=("base",))
            tr.cop("vector", lambda v, bp=bp: v.scalar_tensor_tensor(
                out=s_pe8[:, :], in0=s_cst[:, 128:136], scalar=512.0,
                in1=psum[bp][:, 0:8], op0=ALU.mult, op1=ALU.add),
                   reads=(f"p{bp}", "cst"), writes=("pe8",))
            for cls, oh, ohb, flt, flb, bgt, bgb, sct, scb in (
                    (1, s_oh1, "oh1", s_fl1, "fl1", s_bg1, "bg1", s_sc1, "sc1"),
                    (2, s_oh2, "oh2", s_fl2, "fl2", s_bg2, "bg2", s_sc2, "sc2")):
                tr.cop("vector", lambda v, oh=oh: v.tensor_tensor(
                    out=s_tmp8[:, :], in0=s_pe8[:, :], in1=oh[:, :], op=ALU.mult),
                       reads=("pe8", ohb), writes=("tmp8",))
                tr.cop("vector", lambda v, flt=flt: v.reduce_sum(
                    flt[:, :], s_tmp8[:, :], axis=AX.X),
                       reads=("tmp8",), writes=(flb,))
                tr.cop("vector", lambda v, flt=flt: v.tensor_scalar(
                    out=flt[:, :], in0=flt[:, :], scalar1=-1.0, scalar2=None,
                    op0=ALU.add),
                       reads=(flb,), writes=(flb,))
                tr.cop("vector", lambda v, flt=flt, bgt=bgt, tc=tc: v.tensor_copy(
                    out=bgt[:, tc:tc + 1], in_=flt[:, :]),
                       reads=(flb,), writes=(bgb,))
                tr.cop("vector", lambda v, sct=sct, par=par, tc=tc: v.tensor_scalar(
                    out=sct[:, 2 * par:2 * par + 1], in0=s_cst[:, 136:137],
                    scalar1=float(tc * 128), scalar2=None, op0=ALU.add),
                       reads=("cst",), writes=(scb,))
                tr.cop("vector", lambda v, oh=oh, w9c=w9c: v.tensor_tensor(
                    out=s_tmp8[:, :], in0=s_w9tm[:, w9c:w9c + 8], in1=oh[:, :],
                    op=ALU.mult),
                       reads=(f"w9tm{par}", ohb), writes=("tmp8",))
                tr.cop("vector", lambda v, sct=sct, par=par: v.reduce_sum(
                    sct[:, 2 * par + 1:2 * par + 2], s_tmp8[:, :], axis=AX.X),
                       reads=("tmp8",), writes=(scb,))
                tr.op("gpsimd",
                      lambda g, sct=sct, bgt=bgt, par=par, tc=tc: g.indirect_dma_start(
                          out=lists_d,
                          out_offset=bass.IndirectOffsetOnAxis(
                              ap=bgt[:, tc:tc + 1], axis=0),
                          in_=sct[:, 2 * par:2 * par + 2],
                          in_offset=None),
                      reads=(scb, bgb), writes=("lists",), sem="sq_sc", inc=16)

        NTC = TT // 128
        r_logits(0)
        r_logits(1)
        for tc in range(NTC):
            r_chain_a(tc)
            if tc > 0:
                r_post(tc - 1)
            r_chain_b(tc)
            if tc + 2 < NTC:
                r_logits(tc + 2)
            r_pe(tc)
        r_post(NTC - 1)
        tr.cop("vector", lambda v: v.tensor_copy(out=s_w9fm[:, :], in_=pbf[3][0:9, 0:TT]),
               reads=("p3",), writes=("w9fm",))

        wa = [("wa0", "sq_wa0", 0), ("wa1", "sq_wa1", 4096)]

        def fill_attn(c, half):
            buf, sm, base = wa[half]
            ch = c // 4
            srcs = (((zq, f"scr_q{ch}"), (zk, f"scr_k{ch}")) if half == 0
                    else ((zv, f"scr_v{ch}"), (zg, f"scr_g{ch}")))
            for pi, (zsrc, zb) in enumerate(srcs):
                off = base + pi * D
                tr.op("sync",
                      lambda s, zsrc=zsrc, c=c, off=off: s.dma_start(
                          out=s_ha[:, off:off + D],
                          in_=zsrc[c * 128:(c + 1) * 128, :]),
                      reads=(zb,), writes=(buf, "hA"), sem=sm, inc=16)

        fill_attn(0, 0)
        fill_attn(0, 1)
        for c in range(NDK):
            for pi, (buf, base) in enumerate(
                    (("wa0", 0), ("wa0", 2048), ("wa1", 4096), ("wa1", 6144))):
                for th in range(2):
                    bank = pi * 2 + th
                    for dk in range(NDK):
                        tr.cop("tensor",
                               lambda t, base=base, bank=bank, dk=dk, th=th: t.matmul(
                                   psum[bank][:, :],
                                   s_ha[:, base + dk * 128: base + (dk + 1) * 128],
                                   s_xn[:, dk * TT + th * 512: dk * TT + (th + 1) * 512],
                                   start=(dk == 0), stop=(dk == NDK - 1)),
                               reads=(buf, "xn"), writes=(f"p{bank}",),
                               close=(dk == NDK - 1))
                if pi == 1 and c < NDK - 1:
                    fill_attn(c + 1, 0)
                if pi == 3 and c < NDK - 1:
                    fill_attn(c + 1, 1)
            for th in range(2):
                tr.cop("scalar",
                       lambda sc, c=c, th=th: sc.activation(
                           out=s_q[:, th * 512:(th + 1) * 512], in_=psum[th][:, :],
                           func=AF.Identity, bias=s_bias[:, c:c + 1], scale=1.0),
                       reads=(f"p{th}", "bias"), writes=("q", "hB"))
                tr.cop("scalar",
                       lambda sc, c=c, th=th: sc.activation(
                           out=s_k[:, th * 512:(th + 1) * 512], in_=psum[2 + th][:, :],
                           func=AF.Identity, bias=s_bias[:, NDK + c:NDK + c + 1],
                           scale=1.0),
                       reads=(f"p{2 + th}", "bias"), writes=("k", "hB"))
                tr.cop("scalar",
                       lambda sc, c=c, th=th: sc.activation(
                           out=s_g[:, th * 512:(th + 1) * 512], in_=psum[6 + th][:, :],
                           func=AF.Sigmoid, bias=s_bias[:, 3 * NDK + c:3 * NDK + c + 1],
                           scale=1.0),
                       reads=(f"p{6 + th}", "bias"), writes=("g", "hB"))
                tr.cop("vector",
                       lambda v, c=c, th=th: v.scalar_tensor_tensor(
                           out=s_kv[:, th * 512:(th + 1) * 512], in0=psum[4 + th][:, :],
                           scalar=s_bias[:, 2 * NDK + c:2 * NDK + c + 1],
                           in1=s_k[:, th * 512:(th + 1) * 512],
                           op0=ALU.add, op1=ALU.mult),
                       reads=(f"p{4 + th}", "bias", "k"), writes=("kv", "utt0", "utt1", "eop3"))
            tr.cop("vector",
                   lambda v, c=c: v.tensor_tensor_scan(
                       out=s_cs[:, :], data0=s_ones[:, :], data1=s_kv[:, :],
                       initial=s_carry[:, c:c + 1], op0=ALU.mult, op1=ALU.add),
                   reads=("ones", "kv", "utt0", "utt1", "carry"), writes=("cs", "eop0", "eop1"))
            tr.cop("vector",
                   lambda v, c=c: v.tensor_copy(
                       out=s_carry[:, c:c + 1], in_=s_cs[:, TT - 1:TT]),
                   reads=("cs",), writes=("carry",))
            tr.cop("vector", lambda v: v.tensor_tensor(
                out=s_qg[:, :], in0=s_q[:, :], in1=s_g[:, :], op=ALU.mult),
                   reads=("q", "g"), writes=("qg", "sgt0", "sgt1", "eop2"))
            tr.cop("vector",
                   lambda v, c=c: v.tensor_tensor(
                       out=s_out[:, c * TT:(c + 1) * TT], in0=s_qg[:, :],
                       in1=s_cs[:, :], op=ALU.mult),
                   reads=("qg", "sgt0", "sgt1", "cs"), writes=("out",))

        wbs = [("wb0", s_wb0), ("wb1", s_wb1)]
        eop_regions = [
            (s_eop[:, 0:1024], "eop0", ("eop0",)),
            (s_eop[:, 1024:2048], "eop1", ("eop1",)),
            (s_sg[:, 0:1024], "eop2", ("eop2", "sgt0", "sgt1")),
            (s_ut[:, 0:1024], "eop3", ("eop3", "utt0", "utt1")),
        ]
        whs = [("wh0", "sq_wh0", 0), ("wh1", "sq_wh1", 4096)]
        wos = [("wo0", "sq_wo0", s_wo0), ("wo1", "sq_wo1", s_wo1)]

        def fill_wh(g, i, slot):
            buf, sm, base = whs[slot]
            for half in range(2):
                zsrc, rbase, zb = gu_src(g, half, i)
                r0 = rbase + i * 128
                off = base + half * D
                tr.op("scalar" if half == 0 else "sync",
                      lambda s, zsrc=zsrc, r0=r0, off=off: s.dma_start(
                          out=s_wh[:, off:off + D],
                          in_=zsrc[r0:r0 + 128, :]),
                      reads=(zb,), writes=(buf,), sem=sm, inc=16)

        def fill_wo(g, ec, slot):
            buf, sm, tile = wos[slot]
            zsrc, rbase, zb = w2_src(g, ec)
            tr.op("sync",
                  lambda s, zsrc=zsrc, rbase=rbase, ec=ec, tile=tile:
                  s.dma_start(
                      out=tile[:, :],
                      in_=zsrc[rbase + ec * 128: rbase + (ec + 1) * 128, :]),
                  reads=(zb,), writes=(buf,), sem=sm, inc=16)

        def make_wb(g, slot):
            wbuf, wtile = wbs[slot]
            r = wrow(g)
            for th in range(2):
                tr.cop("tensor",
                       lambda t, r=r, th=th: t.matmul(
                           psum[6][:, :], s_sel[:, 0:128],
                           s_w9fm[0:9, th * 512:(th + 1) * 512],
                           start=True, stop=True),
                       reads=("sel", "w9fm"), writes=("p6",), close=True)
                tr.cop("scalar",
                       lambda sc, wtile=wtile, th=th: sc.activation(
                           out=wtile[:, th * 512:(th + 1) * 512], in_=psum[6][:, :],
                           func=AF.Identity, scale=1.0),
                       reads=("p6",), writes=(wbuf,))

        def load_lists(e):
            b8 = (e % 2) * 8
            b4 = (e % 2) * 4
            tr.op("scalar",
                  lambda s, e=e, b8=b8: s.dma_start(
                      out=s_glst[:, b8:b8 + 8].rearrange("p (a b) -> p a b", b=2),
                      in_=lists_d[e * CAP:(e + 1) * CAP, :].rearrange(
                          "(a p) b -> p a b", p=128)),
                  reads=("lists",), writes=("glst",), sem="sq_ll", inc=16)
            tr.cop("vector",
                   lambda v, b8=b8, b4=b4: v.tensor_copy(
                       out=s_gif[:, b4:b4 + 4].rearrange("p (a c) -> p a c", c=1),
                       in_=s_glst[:, b8:b8 + 8].rearrange(
                           "p (a c) -> p a c", c=2)[:, :, 0:1]),
                   reads=("glst",), writes=("gif", "me0", "me1"))
            tr.cop("vector",
                   lambda v, b4=b4: v.tensor_scalar(
                       out=s_gif[:, b4:b4 + 4], in0=s_gif[:, b4:b4 + 4],
                       scalar1=float(TT), scalar2=None, op0=ALU.min),
                   reads=("gif",), writes=("gif",))
            tr.cop("vector",
                   lambda v, b4=b4: v.tensor_copy(
                       out=s_gidx[:, b4:b4 + 4], in_=s_gif[:, b4:b4 + 4]),
                   reads=("gif",), writes=("gidx",))
            b4w = (e % 4) * 4
            tr.cop("vector",
                   lambda v, b8=b8, b4w=b4w: v.tensor_copy(
                       out=s_gw[:, b4w:b4w + 4].rearrange("p (a c) -> p a c", c=1),
                       in_=s_glst[:, b8:b8 + 8].rearrange(
                           "p (a c) -> p a c", c=2)[:, :, 1:2]),
                   reads=("glst",), writes=("gw",))

        def gather_tile(e, jt):
            b4 = (e % 2) * 4
            tr.op("gpsimd",
                  lambda g, b4=b4, jt=jt: g.indirect_dma_start(
                      out=s_xg[:, :],
                      out_offset=None,
                      in_=xtok_d,
                      in_offset=bass.IndirectOffsetOnAxis(
                          ap=s_gidx[:, b4 + jt:b4 + jt + 1], axis=0)),
                  reads=("gidx", "xtok"), writes=("xg0",), sem="sq_gx", inc=16)

        def transpose_tile(e, jt):
            for hb in range(2):
                bank = 6 + hb
                for dj in range(8):
                    dk = hb * 8 + dj
                    tr.cop("tensor",
                           lambda t, dk=dk, dj=dj, bank=bank: t.transpose(
                               pbf[bank][:, dj * 128:(dj + 1) * 128],
                               s_xg[:, dk * 128:(dk + 1) * 128],
                               s_idb[:, :]),
                           reads=("xg0", "idb"), writes=(f"p{bank}",),
                           close=(dj == 7))
                tr.cop("vector",
                       lambda v, bank=bank, hb=hb, jt=jt: v.tensor_copy(
                           out=s_xnex[:, :].rearrange("p (k s) -> p k s", k=NDK)[
                               :, hb * 8:hb * 8 + 8, jt * 128:(jt + 1) * 128],
                           in_=pbf[bank][:, 0:1024].rearrange("p (k s) -> p k s", k=8)),
                       reads=(f"p{bank}",), writes=("xnex",))

        make_wb(0, 0)
        gchunk = 0
        fill_wh(0, 0, 0)
        for g in range(NGS):
            wbuf, wtile = wbs[g % 2]
            for i in range(NI):
                whbuf, whsm, whbase = whs[gchunk % 2]
                if i + 1 < NI:
                    fill_wh(g, i + 1, (gchunk + 1) % 2)
                elif g + 1 < NG:
                    fill_wh(g + 1, 0, (gchunk + 1) % 2)
                gchunk += 1
                if g == 2 and i == 0:
                    load_lists(0)
                if g == 2 and i == 5:
                    load_lists(1)
                if g == 3 and i < 8:
                    if i % 2 == 0:
                        gather_tile(0, i // 2)
                    else:
                        transpose_tile(0, i // 2)
                if p == 0 and g >= 1 and i % 2 == 0:
                    emit_piece(deferred[g][i // 2], look=3)
                for half in range(2):
                    for th in range(2):
                        bank = half * 2 + th
                        for dk in range(NDK):
                            tr.cop("tensor",
                                   lambda t, whbase=whbase, half=half, th=th,
                                   bank=bank, dk=dk: t.matmul(
                                       psum[bank][:, :],
                                       s_wh[:, whbase + half * 2048 + dk * 128:
                                            whbase + half * 2048 + (dk + 1) * 128],
                                       s_xn[:, dk * TT + th * 512:
                                            dk * TT + (th + 1) * 512],
                                       start=(dk == 0), stop=(dk == NDK - 1)),
                                   reads=(whbuf, "xn"), writes=(f"p{bank}",),
                                   close=(dk == NDK - 1))
                for th in range(2):
                    tr.cop("scalar",
                           lambda sc, th=th: sc.activation(
                               out=s_sg[:, th * 512:(th + 1) * 512], in_=psum[th][:, :],
                               func=AF.Silu, scale=1.0),
                           reads=(f"p{th}",), writes=(f"sgt{th}", "eop2"))
                    tr.cop("vector",
                           lambda v, th=th, wtile=wtile: v.tensor_tensor(
                               out=s_ut[:, th * 512:(th + 1) * 512],
                               in0=psum[2 + th][:, :],
                               in1=wtile[:, th * 512:(th + 1) * 512], op=ALU.mult),
                           reads=(f"p{2 + th}", wbuf), writes=(f"utt{th}", "eop3"))
                tr.cop("vector",
                       lambda v, i=i: v.tensor_tensor(
                           out=s_ha[:, i * 1024:(i + 1) * 1024], in0=s_sg[:, :],
                           in1=s_ut[:, :], op=ALU.mult),
                       reads=("sgt0", "sgt1", "utt0", "utt1"), writes=("hA" if i < 8 else "hB",))
            if g + 1 < NGS:
                make_wb(g + 1, (g + 1) % 2)
            fill_wo(g, 0, 0)
            for ec in range(8):
                wobuf, wosm, wotile = wos[ec % 2]
                if ec + 1 < 8:
                    fill_wo(g, ec + 1, (ec + 1) % 2)
                for el in range(2):
                    ek = ec * 2 + el
                    for th in range(2):
                        bank = 4 + (ek * 2 + th) % 2
                        for i in range(NI):
                            tr.cop("tensor",
                                   lambda t, wotile=wotile, i=i, el=el, th=th,
                                   bank=bank: t.matmul(
                                       psum[bank][:, :],
                                       wotile[:, i * 256 + el * 128:
                                              i * 256 + (el + 1) * 128],
                                       s_ha[:, i * 1024 + th * 512:
                                            i * 1024 + (th + 1) * 512],
                                       start=(i == 0), stop=(i == NI - 1)),
                                   reads=(wobuf, "hA" if i < 8 else "hB"),
                                   writes=(f"p{bank}",),
                                   close=(i == NI - 1))
                        tr.cop("vector",
                               lambda v, ek=ek, th=th, bank=bank: v.tensor_tensor(
                                   out=s_out[:, ek * TT + th * 512:
                                             ek * TT + (th + 1) * 512],
                                   in0=psum[bank][:, :],
                                   in1=s_out[:, ek * TT + th * 512:
                                             ek * TT + (th + 1) * 512],
                                   op=ALU.add),
                               reads=(f"p{bank}", "out"), writes=("out",))
        for e in range(NE):
            g = NGS + e
            b4w = (e % 4) * 4
            for i in range(NI):
                whbuf, whsm, whbase = whs[gchunk % 2]
                if i + 1 < NI:
                    fill_wh(g, i + 1, (gchunk + 1) % 2)
                elif g + 1 < NG:
                    fill_wh(g + 1, 0, (gchunk + 1) % 2)
                gchunk += 1
                if p == 0 and e in (0, 1, 2) and i % 2 == 0:
                    emit_piece(deferred[("r", e)][i // 2], look=3)
                if e + 2 < NE and i == 1:
                    load_lists(e + 2)
                ub = 2 * (i % 2)
                for half in range(2):
                    for dk in range(NDK):
                        tr.cop("tensor",
                               lambda t, whbase=whbase, half=half, dk=dk, ub=ub: t.matmul(
                                   psum[ub + half][:, :],
                                   s_wh[:, whbase + half * 2048 + dk * 128:
                                        whbase + half * 2048 + (dk + 1) * 128],
                                   s_xnex[:, dk * CAP:(dk + 1) * CAP],
                                   start=(dk == 0), stop=(dk == NDK - 1)),
                               reads=(whbuf, "xnex"), writes=(f"p{ub + half}",),
                               close=(dk == NDK - 1))
                par = i % 2
                tr.cop("scalar",
                       lambda sc, ub=ub, par=par: sc.activation(
                           out=s_sg[:, par * 512:(par + 1) * 512], in_=psum[ub][:, :],
                           func=AF.Silu, scale=1.0),
                       reads=(f"p{ub}",), writes=(f"sgt{par}", "eop2"))
                tr.cop("vector",
                       lambda v, ub=ub, par=par: v.tensor_copy(
                           out=s_ut[:, par * 512:(par + 1) * 512],
                           in_=psum[ub + 1][:, :]),
                       reads=(f"p{ub + 1}",), writes=(f"utt{par}", "eop3"))
                tr.cop("vector",
                       lambda v, i=i, par=par: v.tensor_tensor(
                           out=s_ha[:, i * 512:(i + 1) * 512],
                           in0=s_sg[:, par * 512:(par + 1) * 512],
                           in1=s_ut[:, par * 512:(par + 1) * 512], op=ALU.mult),
                       reads=(f"sgt{par}", f"utt{par}"), writes=("hA",))
            fill_wo(g, 0, 0)
            for ec in range(8):
                wobuf, wosm, wotile = wos[ec % 2]
                if ec + 1 < 8:
                    fill_wo(g, ec + 1, (ec + 1) % 2)
                if e + 1 < NE:
                    if ec % 2 == 0:
                        gather_tile(e + 1, ec // 2)
                    else:
                        transpose_tile(e + 1, ec // 2)
                for jt in range(CT):
                    for i in range(NI):
                        tr.cop("tensor",
                               lambda t, wotile=wotile, i=i, jt=jt: t.matmul(
                                   psum[2 + jt][:, 0:256],
                                   s_ha[:, i * 512 + jt * 128:
                                        i * 512 + (jt + 1) * 128],
                                   wotile[:, i * 256:(i + 1) * 256],
                                   start=(i == 0), stop=(i == NI - 1)),
                               reads=(wobuf, "hA"), writes=(f"p{2 + jt}",),
                               close=(i == NI - 1))
                    ereg, ename, ewrites = eop_regions[ec % 4]
                    tr.cop("scalar",
                           lambda sc, jt=jt, ereg=ereg, b4w=b4w: sc.activation(
                               out=ereg[:, jt * 256:(jt + 1) * 256],
                               in_=psum[2 + jt][:, 0:256], func=AF.Identity,
                               scale=s_gw[:, b4w + jt:b4w + jt + 1]),
                           reads=(f"p{2 + jt}", "gw"), writes=ewrites)
                    tr.op("scalar",
                          lambda s, e=e, jt=jt, ec=ec, ereg=ereg: s.dma_start(
                              out=eout_d[e * CAP + jt * 128: e * CAP + (jt + 1) * 128,
                                         ec * 256:(ec + 1) * 256],
                              in_=ereg[:, jt * 256:(jt + 1) * 256]),
                          reads=(ename,), writes=("eout",), sem="sq_eo", inc=16)
        def bg_slot(slot):
            if slot < 4:
                return s_xnex[:, slot * 2048:(slot + 1) * 2048]
            return s_ha[:, (slot - 4) * 2048:(slot - 3) * 2048]

        def back_gather(tc, cls):
            slot = (2 * tc + cls) % 8
            st = bg_slot(slot)
            bgt, bgb = (s_bg1, "bg1") if cls == 0 else (s_bg2, "bg2")
            tr.op("gpsimd",
                  lambda g, st=st, bgt=bgt, tc=tc: g.indirect_dma_start(
                      out=st, out_offset=None, in_=eout_d,
                      in_offset=bass.IndirectOffsetOnAxis(
                          ap=bgt[:, tc:tc + 1], axis=0)),
                  reads=(bgb, "eout"),
                  writes=(f"bgs{slot}", "xnex" if slot < 4 else "hA"),
                  sem="sq_bg", inc=16)

        for tc in range(4):
            back_gather(tc, 0)
            back_gather(tc, 1)
        otm = [(s_wh[:, 0:4096].bitcast(F32), ("wh0",)),
               (s_wh[:, 4096:8192].bitcast(F32), ("wh1",))]
        for tc in range(TT // 128):
            sb, ssem, stile = xslot[tc % 2]
            row = t0 + tc * 128
            tr.op("scalar",
                  lambda s, stile=stile, row=row: s.dma_start(
                      out=stile[:, :], in_=x_d[row:row + 128, :]),
                  writes=(sb,), sem=ssem, inc=16)
            ot, otbufs = otm[tc % 2]
            slot1 = (2 * tc) % 8
            slot2 = (2 * tc + 1) % 8
            g1, g2 = bg_slot(slot1), bg_slot(slot2)
            for quad in range(4):
                bank = quad
                gbank = 4 + quad
                for ej in range(4):
                    ek = quad * 4 + ej
                    tr.cop("tensor",
                           lambda t, bank=bank, ej=ej, ek=ek, tc=tc: t.transpose(
                               psum[bank][:, ej * 128:(ej + 1) * 128],
                               s_out[:, ek * TT + tc * 128: ek * TT + (tc + 1) * 128],
                               s_idf[:, :]),
                           reads=("out", "idf"), writes=(f"p{bank}",),
                           close=(ej == 3))
                tr.cop("tensor",
                       lambda t, gbank=gbank, quad=quad, g1=g1: t.matmul(
                           psum[gbank][:, 0:512],
                           s_idb[:, 0:128],
                           g1[:, quad * 512:(quad + 1) * 512],
                           start=True, stop=False),
                       reads=(f"bgs{slot1}", "idb",
                              "xnex" if slot1 < 4 else "hA"),
                       writes=(f"p{gbank}",), close=False)
                tr.cop("tensor",
                       lambda t, gbank=gbank, quad=quad, g2=g2: t.matmul(
                           psum[gbank][:, 0:512],
                           s_idb[:, 0:128],
                           g2[:, quad * 512:(quad + 1) * 512],
                           start=False, stop=True),
                       reads=(f"bgs{slot2}", "idb",
                              "xnex" if slot2 < 4 else "hA"),
                       writes=(f"p{gbank}",), close=True)
                tr.cop("vector",
                       lambda v, bank=bank, quad=quad, ot=ot, stile=stile:
                       v.tensor_tensor(
                           out=ot[:, quad * 512:(quad + 1) * 512],
                           in0=psum[bank][:, 0:512],
                           in1=stile[:, quad * 512:(quad + 1) * 512], op=ALU.add),
                       reads=(f"p{bank}", sb), writes=otbufs)
                tr.cop("vector",
                       lambda v, gbank=gbank, quad=quad, ot=ot:
                       v.tensor_tensor(
                           out=ot[:, quad * 512:(quad + 1) * 512],
                           in0=psum[gbank][:, 0:512],
                           in1=ot[:, quad * 512:(quad + 1) * 512], op=ALU.add),
                       reads=(f"p{gbank}",) + otbufs, writes=otbufs)
            tr.op("sync" if tc % 2 == 0 else "scalar",
                  lambda s, ot=ot, row=row: s.dma_start(
                      out=out_d[row:row + 128, :], in_=ot[:, :]),
                  reads=otbufs, writes=(f"sto{tc % 2}",),
                  sem=f"sq_st{tc % 2}", inc=16)
            n_store += 1
            if tc + 4 < TT // 128:
                back_gather(tc + 4, 0)
                back_gather(tc + 4, 1)

    tr.barrier_wait("vector", "sq_st0", 16 * (n_store // 2))
    tr.barrier_wait("vector", "sq_st1", 16 * (n_store // 2))

    block = E(nc.Block())
    tr.replay(block)
    ctx.close()
    return nc


_NC_CACHE = {}


def _get_nc():
    if "nc" not in _NC_CACHE:
        _NC_CACHE["nc"] = build_nc()
    return _NC_CACHE["nc"]


NEH = 8


def make_in_maps(inputs):
    x = np.ascontiguousarray(np.asarray(inputs["L0_attn_x"], dtype=np.float32))
    bqkvg = np.stack([np.asarray(inputs["L0_attn_bq"]), np.asarray(inputs["L0_attn_bk"]),
                      np.asarray(inputs["L0_attn_bv"]), np.asarray(inputs["L0_attn_bg"])],
                     axis=0)
    bqkvg = np.ascontiguousarray(bqkvg, dtype=np.float32)
    gw9 = np.concatenate([np.asarray(inputs["L0_ffn_gate_w"]),
                          np.asarray(inputs["L0_ffn_shared_gate_w"])], axis=0)
    gw9 = np.ascontiguousarray(gw9, dtype=np.float32)
    idf = np.eye(128, dtype=np.float32)
    idb = np.eye(128).astype(ml_dtypes.bfloat16)
    cst = np.zeros((128, 138), dtype=np.float32)
    cst[:, 0:128] = np.triu(np.ones((128, 128), dtype=np.float32))
    cst[:, 128:136] = np.arange(8, dtype=np.float32)[None, :]
    cst[:, 136] = np.arange(128, dtype=np.float32)
    cst[:, 137] = 1.0
    sel = np.zeros((9, 9, 128), dtype=np.float32)
    for r in range(9):
        sel[r, r, :] = 1.0
    sel = np.ascontiguousarray(sel.reshape(9, 9 * 128)).astype(ml_dtypes.bfloat16)
    def sw_up(w, rc):
        w = np.asarray(w, dtype=np.float32)
        return np.ascontiguousarray(
            w.reshape(rc, 128, 16, 128).transpose(0, 3, 2, 1).reshape(rc * 128, 2048))

    w2f = np.asarray(inputs["L0_ffn_w2"], dtype=np.float32)
    w2sw = np.ascontiguousarray(
        w2f.reshape(NEH, 8, 2, 128, 11, 128).transpose(0, 1, 5, 4, 2, 3)
        .reshape(NEH, 1024, 2816))
    wsdf = np.asarray(inputs["L0_ffn_shared_down_w"], dtype=np.float32)
    wsdsw = np.ascontiguousarray(
        wsdf.reshape(8, 2, 128, 4, 11, 128).transpose(3, 0, 5, 4, 1, 2)
        .reshape(4 * 1024, 2816))
    common = dict(
        wq=sw_up(inputs["L0_attn_Wq"], 16),
        wk=sw_up(inputs["L0_attn_Wk"], 16),
        wv=sw_up(inputs["L0_attn_Wv"], 16),
        wg=sw_up(inputs["L0_attn_Wg"], 16),
        bqkvg=bqkvg, gw9=gw9,
        wgu=sw_up(inputs["L0_ffn_shared_gate_up_w"], 88),
        wsd=wsdsw,
        w13=np.ascontiguousarray(
            np.asarray(inputs["L0_ffn_w13"], dtype=np.float32)
            .reshape(NEH, 22, 128, 16, 128).transpose(0, 1, 4, 3, 2)
            .reshape(NEH, 2816, 2048)),
        w2=w2sw,
        idf=idf, idb=idb, sel=sel, cst=cst,
    )
    n = x.shape[0]
    return [dict(common, x=np.ascontiguousarray(x[i])) for i in range(n)]


def kernel(**inputs):
    in_maps = make_in_maps(inputs)
    nc = _get_nc()
    res = run_bass_kernel_spmd(nc, in_maps, core_ids=list(range(8)))
    return np.stack([r["out"] for r in res.results], axis=0)


# version 10
# speedup vs baseline: 1.1533x; 1.0533x over previous
import numpy as np
import ml_dtypes
import concourse.bass as bass
import concourse.mybir as mybir
from concourse.bass_utils import run_bass_kernel_spmd

F32 = mybir.dt.float32
BF16 = mybir.dt.bfloat16
AF = mybir.ActivationFunctionType
ALU = mybir.AluOpType
AX = mybir.AxisListType

S = 4096
D = 2048
TT = 1024
NPASS = S // TT
NDK = D // 128
MOE_I = 1408
SH_I = 5632
NE = 8
NG = 12
NGS = 4
NI = MOE_I // 128
EPS = 1e-5
CAP = 512
CT = CAP // 128
PADV = 1.0e6

ENGS = ("tensor", "vector", "scalar", "sync", "gpsimd")
GAP = 8


class Tracker:

    def __init__(self, nc):
        self.nc = nc
        self.q = {e: [] for e in ENGS}
        self.cnt = {}
        self.bufs = {}
        self.opidx = {e: 0 for e in ENGS}
        self.waited = {e: {} for e in ENGS}
        self.pending = []
        self.esem = {}

    def sem(self, name):
        self.cnt.setdefault(name, 0)
        return name

    def engine_sems(self, names):
        for e, s in names.items():
            self.esem[e] = self.sem(s)

    def _want(self, need, sem, val):
        if need.get(sem, 0) < val:
            need[sem] = val

    def _edge(self, eng, weng, widx, wdma):
        if weng is None:
            return False
        if wdma or weng != eng:
            return True
        return (self.opidx[eng] - widx) < GAP

    def op(self, eng, fn, reads=(), writes=(), sem=None, inc=0, close=True,
           manual_waits=()):
        need = {}
        for (msem, mval) in manual_waits:
            self._want(need, msem, mval)
        for b in reads:
            rec = self.bufs.get(b)
            if rec and rec["w"] and self._edge(eng, rec["weng"], rec["widx"], rec["wdma"]):
                self._want(need, rec["w"][0], rec["w"][1])
        for b in writes:
            rec = self.bufs.get(b)
            if rec:
                if rec["w"] and self._edge(eng, rec["weng"], rec["widx"], rec["wdma"]):
                    self._want(need, rec["w"][0], rec["w"][1])
                for rsem, (rcnt, reng, ridx, rdma) in rec["r"].items():
                    if self._edge(eng, reng, ridx, rdma):
                        self._want(need, rsem, rcnt)
        for s, v in sorted(need.items()):
            if self.waited[eng].get(s, 0) < v:
                self.q[eng].append(("wait", s, v))
                self.waited[eng][s] = v
        idx = self.opidx[eng]
        self.opidx[eng] += 1
        is_dma = inc >= 16
        ev = None
        if inc:
            self.cnt[sem] += inc
            ev = (sem, self.cnt[sem])
        self.q[eng].append(("op", fn, sem if inc else None, inc))
        if not close:
            self.pending.append((eng, idx, tuple(reads), tuple(writes)))
            return None
        if ev:
            for (peng, pidx, preads, pwrites) in self.pending:
                if peng != eng:
                    continue
                for b in preads:
                    self._note_read(b, ev, peng, pidx, is_dma)
                for b in pwrites:
                    self._note_write(b, ev, peng, pidx, is_dma)
            self.pending = [p for p in self.pending if p[0] != eng]
        for b in reads:
            self._note_read(b, ev, eng, idx, is_dma)
        for b in writes:
            self._note_write(b, ev, eng, idx, is_dma)
        return ev

    def _note_read(self, b, ev, eng, idx, dma):
        rec = self.bufs.setdefault(
            b, {"w": None, "weng": None, "widx": -1 << 30, "wdma": False, "r": {}})
        if ev is None:
            return
        rec["r"][ev[0]] = (ev[1], eng, idx, dma)

    def _note_write(self, b, ev, eng, idx, dma):
        self.bufs[b] = {"w": ev, "weng": eng, "widx": idx, "wdma": dma, "r": {}}

    def cop(self, eng, fn, reads=(), writes=(), close=True, manual_waits=()):
        return self.op(eng, fn, reads, writes, sem=self.esem[eng],
                       inc=1 if close else 0, close=close,
                       manual_waits=manual_waits)

    def barrier_wait(self, eng, sem, val):
        if self.waited[eng].get(sem, 0) < val:
            self.q[eng].append(("wait", sem, val))
            self.waited[eng][sem] = val

    def replay(self, block):
        for name in ENGS:
            items = self.q[name]

            def make(items=items):
                def body(eng):
                    for it in items:
                        if it[0] == "wait":
                            eng.wait_ge(self.nc._sems[it[1]], it[2])
                        else:
                            _, fn, sm, inc = it
                            ins = fn(eng)
                            if sm is not None:
                                ins.then_inc(self.nc._sems[sm], inc)
                return body

            getattr(block, name)(make())


def build_nc():
    nc = bass.Bass("TRN2", target_bir_lowering=False)

    x_d = nc.dram_tensor("x", [S, D], F32, kind="ExternalInput").ap()
    wq_d = nc.dram_tensor("wq", [D, D], F32, kind="ExternalInput").ap()
    wk_d = nc.dram_tensor("wk", [D, D], F32, kind="ExternalInput").ap()
    wv_d = nc.dram_tensor("wv", [D, D], F32, kind="ExternalInput").ap()
    wg_d = nc.dram_tensor("wg", [D, D], F32, kind="ExternalInput").ap()
    bqkvg_d = nc.dram_tensor("bqkvg", [4, D], F32, kind="ExternalInput").ap()
    gw9_d = nc.dram_tensor("gw9", [9, D], F32, kind="ExternalInput").ap()
    wgu_d = nc.dram_tensor("wgu", [2 * SH_I, D], F32, kind="ExternalInput").ap()
    wsd_d = nc.dram_tensor("wsd", [4 * 1024, 2816], F32, kind="ExternalInput").ap()
    w13_d = nc.dram_tensor("w13", [NE, 2 * MOE_I, D], F32, kind="ExternalInput").ap()
    w2_d = nc.dram_tensor("w2", [NE, 1024, 2816], F32, kind="ExternalInput").ap()
    idf_d = nc.dram_tensor("idf", [128, 128], F32, kind="ExternalInput").ap()
    idb_d = nc.dram_tensor("idb", [128, 128], BF16, kind="ExternalInput").ap()
    sel_d = nc.dram_tensor("sel", [9, 9 * 128], BF16, kind="ExternalInput").ap()
    cst_d = nc.dram_tensor("cst", [128, 138], F32, kind="ExternalInput").ap()
    out_d = nc.dram_tensor("out", [S, D], F32, kind="ExternalOutput").ap()

    lists_d = nc.dram_tensor("lists", [NE * CAP, 2], F32, kind="Internal").ap()
    eout_d = nc.dram_tensor("eout", [NE * CAP, D], BF16, kind="Internal").ap()
    xtok_d = nc.dram_tensor("xtok", [TT + 128, D], BF16, kind="Internal").ap()

    zq = nc.dram_tensor("zq", [D, D], BF16, kind="Internal").ap()
    zk = nc.dram_tensor("zk", [D, D], BF16, kind="Internal").ap()
    zv = nc.dram_tensor("zv", [D, D], BF16, kind="Internal").ap()
    zg = nc.dram_tensor("zg", [D, D], BF16, kind="Internal").ap()
    zgu = nc.dram_tensor("zgu", [2 * SH_I, D], BF16, kind="Internal").ap()
    zsd = nc.dram_tensor("zsd", [4 * 1024, 2816], BF16, kind="Internal").ap()
    z13 = nc.dram_tensor("z13", [NE, 2 * MOE_I, D], BF16, kind="Internal").ap()
    z2 = nc.dram_tensor("z2", [NE, 1024, 2816], BF16, kind="Internal").ap()

    from contextlib import ExitStack
    ctx = ExitStack()
    E = ctx.enter_context

    s_xn = E(nc.sbuf_tensor("s_xn", [128, NDK * TT], BF16))
    s_out = E(nc.sbuf_tensor("s_out", [128, NDK * TT], F32))
    s_ha = E(nc.sbuf_tensor("s_ha", [128, 11264], BF16))
    s_q = s_ha[:, 8192:8192 + TT]
    s_k = s_ha[:, 9216:9216 + TT]
    s_g = s_ha[:, 10240:10240 + TT]
    s_wh = E(nc.sbuf_tensor("s_wh", [128, 8192], BF16))
    s_wo0 = E(nc.sbuf_tensor("s_wo0", [128, NI * 256], BF16))
    s_wo1 = E(nc.sbuf_tensor("s_wo1", [128, NI * 256], BF16))
    s_xa = E(nc.sbuf_tensor("s_xa", [128, D], F32))
    s_xb = E(nc.sbuf_tensor("s_xb", [128, D], F32))
    s_xntm = E(nc.sbuf_tensor("s_xntm", [128, D], BF16))
    s_sg = E(nc.sbuf_tensor("s_sg", [128, TT], BF16))
    s_qg = s_sg
    s_ut = E(nc.sbuf_tensor("s_ut", [128, TT], BF16))
    s_kv = s_ut
    s_wb0 = E(nc.sbuf_tensor("s_wb0", [128, TT], BF16))
    s_wb1 = E(nc.sbuf_tensor("s_wb1", [128, TT], BF16))
    s_w9fm = E(nc.sbuf_tensor("s_w9fm", [9, TT], BF16))
    s_ones = E(nc.sbuf_tensor("s_ones", [128, TT], BF16))
    s_carry = E(nc.sbuf_tensor("s_carry", [128, NDK], F32))
    s_bias = E(nc.sbuf_tensor("s_bias", [128, 4 * NDK], F32))
    s_gw9T = E(nc.sbuf_tensor("s_gw9T", [128, NDK * 9], BF16))
    s_sel = E(nc.sbuf_tensor("s_sel", [9, 128], BF16))
    s_idf = E(nc.sbuf_tensor("s_idf", [128, 128], F32))
    s_idb = E(nc.sbuf_tensor("s_idb", [128, 128], BF16))
    s_w9tm = E(nc.sbuf_tensor("s_w9tm", [128, 32], BF16))
    s_m1 = E(nc.sbuf_tensor("s_m1", [128, 1], F32))
    s_m2 = E(nc.sbuf_tensor("s_m2", [128, 1], F32))
    s_negm1 = E(nc.sbuf_tensor("s_negm1", [128, 1], F32))
    s_sum = E(nc.sbuf_tensor("s_sum", [128, 1], F32))
    s_rinv = E(nc.sbuf_tensor("s_rinv", [128, 1], F32))
    s_exp = E(nc.sbuf_tensor("s_exp", [128, 8], F32))
    s_bn = E(nc.sbuf_tensor("s_bn", [128, 24], F32))
    s_mv = E(nc.sbuf_tensor("s_mv", [128, 2], F32))
    s_lnv = E(nc.sbuf_tensor("s_lnv", [128, 1], F32))
    s_rstd = E(nc.sbuf_tensor("s_rstd", [128, 1], F32))
    s_eps = E(nc.sbuf_tensor("s_eps", [128, 1], F32))
    U32 = mybir.dt.uint32
    s_cst = E(nc.sbuf_tensor("s_cst", [128, 138], F32))
    s_xnex = E(nc.sbuf_tensor("s_xnex", [128, NDK * CAP], BF16))
    s_xg = E(nc.sbuf_tensor("s_xg", [128, D], BF16))
    s_glst = E(nc.sbuf_tensor("s_glst", [128, 16], F32))
    s_gidx = E(nc.sbuf_tensor("s_gidx", [128, 8], U32))
    s_gw = E(nc.sbuf_tensor("s_gw", [128, 16], F32))
    s_i1 = E(nc.sbuf_tensor("s_i1", [128, 8], F32))
    s_i2 = E(nc.sbuf_tensor("s_i2", [128, 8], U32))
    s_oh1 = E(nc.sbuf_tensor("s_oh1", [128, 8], F32))
    s_oh2 = E(nc.sbuf_tensor("s_oh2", [128, 8], F32))
    s_me = E(nc.sbuf_tensor("s_me", [128, 16], F32))
    s_gif = s_me
    s_pe8 = E(nc.sbuf_tensor("s_pe8", [128, 8], F32))
    s_tmp8 = E(nc.sbuf_tensor("s_tmp8", [128, 8], F32))
    s_base = E(nc.sbuf_tensor("s_base", [1, 8], F32))
    s_fl1 = E(nc.sbuf_tensor("s_fl1", [128, 1], F32))
    s_fl2 = E(nc.sbuf_tensor("s_fl2", [128, 1], F32))
    s_bg1 = E(nc.sbuf_tensor("s_bg1", [128, 8], U32))
    s_bg2 = E(nc.sbuf_tensor("s_bg2", [128, 8], U32))
    s_sc1 = E(nc.sbuf_tensor("s_sc1", [128, 4], F32))
    s_sc2 = E(nc.sbuf_tensor("s_sc2", [128, 4], F32))
    s_eop = E(nc.sbuf_tensor("s_eop", [128, 2048], BF16))
    s_cs = s_eop[:, :].bitcast(F32)
    s_fill = E(nc.sbuf_tensor("s_fill", [128, 16], F32))

    psum = [E(nc.psum_tensor(f"p{i}", [128, 512], F32)) for i in range(8)]
    pbf = [psum[i][:, :].bitcast(BF16) for i in range(8)]

    tr = Tracker(nc)
    sem_names = ["se_pe", "se_dve", "se_act", "sq_wa0", "sq_wa1",
                 "sq_wh0", "sq_wh1", "sq_wo0", "sq_wo1", "sq_xa", "sq_xb",
                 "sq_xc",
                 "sq_st0", "sq_st1", "sq_m0", "sq_m1", "sq_m2", "sq_m3", "sq_m4",
                 "sq_m5", "sq_sc", "sq_ll", "sq_gx", "sq_bg", "sq_eo", "sq_xt",
                 "sq_lf"]
    cast_sems = [f"sq_c{i}" for i in range(29)]
    sem_names += cast_sems
    nc._sems = {}
    for s in sem_names:
        nc._sems[s] = E(nc.semaphore(s))
        tr.sem(s)
    tr.engine_sems({"tensor": "se_pe", "vector": "se_dve", "scalar": "se_act"})

    def z13_pieces(e):
        return [(f"scr_13_{e}p{j}", z13[e][j * 704:(j + 1) * 704, :],
                 w13_d[e][j * 704:(j + 1) * 704, :]) for j in range(4)]

    def z2_pieces(e):
        return [(f"scr_2_{e}p{j}", z2[e][j * 512:(j + 1) * 512, :],
                 w2_d[e][j * 512:(j + 1) * 512, :]) for j in range(2)]

    upfront = []
    attn_mats = [("q", zq, wq_d), ("k", zk, wk_d), ("v", zv, wv_d), ("g", zg, wg_d)]
    for ch in range(4):
        for nm, zdst, wsrc in attn_mats:
            upfront.append((f"scr_{nm}{ch}", zdst[ch * 512:(ch + 1) * 512, :],
                            wsrc[ch * 512:(ch + 1) * 512, :]))
    for gi in range(4):
        upfront.append((f"scr_gu_g{gi}", zgu[gi * MOE_I:(gi + 1) * MOE_I, :],
                        wgu_d[gi * MOE_I:(gi + 1) * MOE_I, :]))
        upfront.append((f"scr_gu_u{gi}", zgu[SH_I + gi * MOE_I:SH_I + (gi + 1) * MOE_I, :],
                        wgu_d[SH_I + gi * MOE_I:SH_I + (gi + 1) * MOE_I, :]))
    upfront.append(("scr_sd", zsd, wsd_d))
    for e in (0, 1):
        upfront += z13_pieces(e) + z2_pieces(e)
    deferred = {}
    for g in (1, 2, 3):
        deferred[g] = z13_pieces(g + 1) + z2_pieces(g + 1)
    for e in (0, 1, 2):
        deferred[("r", e)] = z13_pieces(e + 5) + z2_pieces(e + 5)
    n_cast = len(upfront) + sum(len(v) for v in deferred.values())

    cast_state = {"n": 0, "evs": []}

    def emit_piece(pc, look=2):
        nm, dst, src = pc
        evs = cast_state["evs"]
        waits = [evs[-look]] if len(evs) >= look else []
        sem = cast_sems[cast_state["n"] % len(cast_sems)]
        ev = tr.op("gpsimd", lambda g, dst=dst, src=src: g.dma_start(out=dst, in_=src),
                   writes=(nm,), sem=sem, inc=16, manual_waits=waits)
        evs.append(ev)
        cast_state["n"] += 1

    for pc in upfront:
        emit_piece(pc, look=2)

    def gu_src(g, half, i):
        if g < 4:
            return zgu, (SH_I if half else 0) + g * MOE_I, f"scr_gu_{'u' if half else 'g'}{g}"
        e = g - 4
        rbase = MOE_I if half else 0
        return z13[e], rbase, f"scr_13_{e}p{(rbase + i * 128) // 704}"

    def w2_src(g, ec):
        if g < 4:
            return zsd, g * 1024, "scr_sd"
        return z2[g - 4], 0, f"scr_2_{g - 4}p{(ec * 128) // 512}"

    def wrow(g):
        return 8 if g < 4 else g - 4

    tr.op("sync", lambda s: s.dma_start(out=s_xa[0:9, :], in_=gw9_d),
          writes=("xa",), sem="sq_m0", inc=16)
    tr.op("sync", lambda s: s.dma_start(out=s_out[0:1, 0:4 * D],
                                        in_=bqkvg_d.rearrange("j d -> (j d)")[None, :]),
          writes=("out",), sem="sq_m1", inc=16)
    tr.op("sync", lambda s: s.dma_start(out=s_sel[:, :], in_=sel_d[:, 8 * 128:9 * 128]),
          writes=("sel",), sem="sq_m2", inc=16)
    tr.op("sync", lambda s: s.dma_start(out=s_idf[:, :], in_=idf_d),
          writes=("idf",), sem="sq_m3", inc=16)
    tr.op("sync", lambda s: s.dma_start(out=s_idb[:, :], in_=idb_d),
          writes=("idb",), sem="sq_m4", inc=16)
    tr.op("sync", lambda s: s.dma_start(out=s_cst[:, :], in_=cst_d),
          writes=("cst",), sem="sq_m5", inc=16)
    tr.cop("vector", lambda v: v.memset(s_ones[:, :], 1.0), writes=("ones",))
    tr.cop("vector", lambda v: v.memset(s_carry[:, :], 0.0), writes=("carry",))
    tr.cop("vector", lambda v: v.memset(s_eps[:, :], EPS), writes=("eps",))
    tr.cop("vector", lambda v: v.memset(s_fill[:, :], PADV), writes=("fillc",))
    tr.cop("vector", lambda v: v.memset(s_xg[:, :], 0.0), writes=("xg0",))
    for dk in range(NDK):
        tr.cop("tensor",
               lambda t, dk=dk: t.matmul(psum[0][:, dk * 9:dk * 9 + 9],
                                         s_xa[0:9, dk * 128:(dk + 1) * 128],
                                         s_idf[0:9, 0:9], start=True, stop=True),
               reads=("xa", "idf"), writes=("p0",), close=(dk == NDK - 1))
    tr.cop("vector", lambda v: v.tensor_copy(out=s_gw9T[:, :], in_=psum[0][:, 0:NDK * 9]),
           reads=("p0",), writes=("gw9T",))
    for j in range(4):
        for c in range(NDK):
            tr.cop("tensor",
                   lambda t, j=j, c=c: t.matmul(
                       psum[1][:, j * NDK + c:j * NDK + c + 1],
                       s_out[0:1, j * D + c * 128: j * D + (c + 1) * 128],
                       s_idf[0:1, 0:1], start=True, stop=True),
                   reads=("out", "idf"), writes=("p1",),
                   close=(j == 3 and c == NDK - 1))
    tr.cop("vector", lambda v: v.tensor_copy(out=s_bias[:, :], in_=psum[1][:, 0:4 * NDK]),
           reads=("p1",), writes=("bias",))

    xslot = [("xa", "sq_xa", s_xa), ("xb", "sq_xb", s_xb)]
    s_xc = s_xnex[:, 0:4096].bitcast(F32)
    xslot3 = [("xa", "sq_xa", s_xa), ("xb", "sq_xb", s_xb),
              ("xc", "sq_xc", s_xc)]
    eop_regions = [
        (s_eop[:, 0:1024], "eop0", ()),
        (s_eop[:, 1024:2048], "eop1", ()),
        (s_sg[:, 0:1024], "eop2", ("sgt0", "sgt1")),
        (s_ut[:, 0:1024], "eop3", ("utt0", "utt1")),
    ]
    eopj = {r[1]: tuple(f"{r[1]}j{jt}" for jt in range(4)) for r in eop_regions}
    n_store = 0

    for p in range(NPASS):
        t0 = p * TT
        for tc in range(TT // 128):
            sb, ssem, stile = xslot3[tc % 3]
            row = t0 + tc * 128
            lq = ("sync", "scalar", "sync" if p == 0 else "gpsimd")[tc % 3]
            tr.op(lq,
                  lambda s, stile=stile, row=row: s.dma_start(
                      out=stile[:, :], in_=x_d[row:row + 128, :]),
                  writes=(sb,) if sb != "xc" else ("xc", "xnex"), sem=ssem, inc=16)
            for i in range(4):
                tr.cop("vector",
                       lambda v, i=i, stile=stile: v.bn_stats(
                           s_bn[:, i * 6:(i + 1) * 6], stile[:, i * 512:(i + 1) * 512]),
                       reads=(sb,), writes=("bn",))
            tr.cop("vector", lambda v: v.bn_aggr(s_mv[:, :], s_bn[:, :]),
                   reads=("bn",), writes=("mv",))
            tr.cop("scalar", lambda sc: sc.activation(out=s_lnv[:, :], in_=s_mv[:, 1:2],
                                                      func=AF.Ln, bias=s_eps[:, 0:1],
                                                      scale=1.0),
                   reads=("mv", "eps"), writes=("lnv",))
            tr.cop("scalar", lambda sc: sc.activation(out=s_rstd[:, :], in_=s_lnv[:, :],
                                                      func=AF.Exp, scale=-0.5),
                   reads=("lnv",), writes=("rstd",))
            tr.cop("vector",
                   lambda v, stile=stile: v.tensor_scalar(
                       out=s_xntm[:, :], in0=stile[:, :], scalar1=s_mv[:, 0:1],
                       scalar2=s_rstd[:, 0:1], op0=ALU.subtract, op1=ALU.mult),
                   reads=(sb, "mv", "rstd"), writes=("xntm",))
            tr.op("scalar",
                  lambda s, tc=tc: s.dma_start(
                      out=xtok_d[tc * 128:(tc + 1) * 128, :], in_=s_xntm[:, :]),
                  reads=("xntm",), writes=("xtok",), sem="sq_xt", inc=16)
            for quad in range(4):
                bank = quad % 2
                for dj in range(4):
                    dk = quad * 4 + dj
                    tr.cop("tensor",
                           lambda t, bank=bank, dj=dj, dk=dk: t.transpose(
                               pbf[bank][:, dj * 128:(dj + 1) * 128],
                               s_xntm[:, dk * 128:(dk + 1) * 128], s_idb[:, :]),
                           reads=("xntm", "idb"), writes=(f"p{bank}",),
                           close=(dj == 3))
                dk0 = quad * 4
                tr.cop("vector",
                       lambda v, bank=bank, dk0=dk0, tc=tc: v.tensor_copy(
                           out=s_xn[:, :].rearrange("p (k t) -> p k t", k=NDK)[
                               :, dk0:dk0 + 4, tc * 128:(tc + 1) * 128],
                           in_=pbf[bank][:, 0:512].rearrange("p (k t) -> p k t", k=4)),
                       reads=(f"p{bank}",), writes=("xn",))

        tr.cop("vector", lambda v: v.memset(s_base[:, :], 0.0), writes=("base",))
        for fh in range(4):
            tr.op("gpsimd", lambda g, fh=fh: g.dma_start(
                out=lists_d[fh * 1024:(fh + 1) * 1024, :].rearrange(
                    "(p a) b -> p (a b)", p=128),
                in_=s_fill[:, :]),
                  reads=("fillc",), writes=("lists",), sem="sq_lf", inc=16)
        def r_logits(tc):
            bl = 6 + tc % 2
            for dk in range(NDK):
                tr.cop("tensor",
                       lambda t, tc=tc, dk=dk, bl=bl: t.matmul(
                           psum[bl][:, 0:9],
                           s_xn[:, dk * TT + tc * 128: dk * TT + (tc + 1) * 128],
                           s_gw9T[:, dk * 9:(dk + 1) * 9],
                           start=(dk == 0), stop=(dk == NDK - 1)),
                       reads=("xn", "gw9T"), writes=(f"p{bl}",), close=(dk == NDK - 1))

        def r_chain_a(tc):
            bl = 6 + tc % 2
            tr.cop("vector", lambda v, bl=bl: v.max(s_i1[:, :], psum[bl][:, 0:8]),
                   reads=(f"p{bl}",), writes=("i1",))
            tr.cop("vector", lambda v, bl=bl: v.max_index(s_i2[:, :], s_i1[:, :], psum[bl][:, 0:8]),
                   reads=("i1", f"p{bl}"), writes=("i2",))

        def r_chain_b(tc):
            bl = 6 + tc % 2
            par = tc % 2
            w9c = par * 16
            mec = par * 8
            tr.cop("vector", lambda v: v.tensor_scalar(
                out=s_negm1[:, :], in0=s_i1[:, 0:1], scalar1=-1.0, scalar2=None,
                op0=ALU.mult),
                   reads=("i1",), writes=("negm1",))
            tr.cop("scalar", lambda sc, bl=bl: sc.activation(
                out=s_exp[:, :], in_=psum[bl][:, 0:8], func=AF.Exp,
                bias=s_negm1[:, 0:1], scale=1.0, accum_out=s_sum[:, 0:1]),
                   reads=(f"p{bl}", "negm1"), writes=("exp", "sum"))
            tr.cop("scalar", lambda sc, bl=bl, w9c=w9c: sc.activation(
                out=s_w9tm[:, w9c + 8:w9c + 9], in_=psum[bl][:, 8:9],
                func=AF.Sigmoid, scale=1.0),
                   reads=(f"p{bl}",), writes=(f"w9tm{par}",))
            tr.cop("vector", lambda v: v.tensor_copy(
                out=s_tmp8[:, 0:2], in_=s_i2[:, 0:2]),
                   reads=("i2",), writes=("tmp8",))
            tr.cop("vector", lambda v: v.tensor_single_scalar(
                out=s_oh1[:, :], in_=s_cst[:, 128:136], scalar=s_tmp8[:, 0:1],
                op=ALU.is_equal),
                   reads=("cst", "tmp8"), writes=("oh1",))
            tr.cop("vector", lambda v: v.tensor_single_scalar(
                out=s_oh2[:, :], in_=s_cst[:, 128:136], scalar=s_tmp8[:, 1:2],
                op=ALU.is_equal),
                   reads=("cst", "tmp8"), writes=("oh2",))
            tr.cop("vector", lambda v, mec=mec: v.tensor_tensor(
                out=s_me[:, mec:mec + 8], in0=s_oh1[:, :], in1=s_oh2[:, :], op=ALU.add),
                   reads=("oh1", "oh2"), writes=(f"me{par}",))
            tr.cop("vector", lambda v: v.reciprocal(s_rinv[:, :], s_sum[:, :]),
                   reads=("sum",), writes=("rinv",))
            tr.cop("vector", lambda v: v.tensor_scalar(
                out=s_exp[:, :], in0=s_exp[:, :], scalar1=s_rinv[:, 0:1], scalar2=None,
                op0=ALU.mult),
                   reads=("exp", "rinv"), writes=("exp",))
            tr.cop("vector", lambda v, w9c=w9c, mec=mec: v.tensor_tensor(
                out=s_w9tm[:, w9c:w9c + 8], in0=s_exp[:, :],
                in1=s_me[:, mec:mec + 8], op=ALU.mult),
                   reads=("exp", f"me{par}"), writes=(f"w9tm{par}",))

        def r_pe(tc):
            bp = tc % 2
            bb = 4 + tc % 2
            par = tc % 2
            w9c = par * 16
            mec = par * 8
            tr.cop("tensor",
                   lambda t, tc=tc, w9c=w9c: t.transpose(
                       pbf[3][0:9, tc * 128:(tc + 1) * 128],
                       s_w9tm[:, w9c:w9c + 9], s_idb[:, :]),
                   reads=(f"w9tm{par}", "idb"), writes=("p3",))
            tr.cop("tensor", lambda t, bb=bb, mec=mec: t.matmul(
                psum[bb][0:1, 0:8], s_cst[:, 137:138], s_me[:, mec:mec + 8],
                start=True, stop=True),
                   reads=("cst", f"me{par}"), writes=(f"p{bb}",))
            tr.cop("tensor", lambda t, bp=bp, mec=mec: t.matmul(
                psum[bp][:, 0:8], s_cst[:, 0:128], s_me[:, mec:mec + 8],
                start=True, stop=False),
                   reads=("cst", f"me{par}"), writes=(f"p{bp}",), close=False)
            tr.cop("tensor", lambda t, bp=bp: t.matmul(
                psum[bp][:, 0:8], s_cst[0:1, 0:128], s_base[0:1, :],
                start=False, stop=True),
                   reads=("cst", "base"), writes=(f"p{bp}",))

        def r_post(tc):
            bp = tc % 2
            bb = 4 + tc % 2
            par = tc % 2
            w9c = par * 16
            tr.cop("vector", lambda v, bb=bb: v.tensor_tensor(
                out=s_base[:, :], in0=s_base[:, :], in1=psum[bb][0:1, 0:8], op=ALU.add),
                   reads=("base", f"p{bb}"), writes=("base",))
            tr.cop("vector", lambda v, bp=bp: v.scalar_tensor_tensor(
                out=s_pe8[:, :], in0=s_cst[:, 128:136], scalar=512.0,
                in1=psum[bp][:, 0:8], op0=ALU.mult, op1=ALU.add),
                   reads=(f"p{bp}", "cst"), writes=("pe8",))
            for cls, oh, ohb, flt, flb, bgt, bgb, sct, scb in (
                    (1, s_oh1, "oh1", s_fl1, "fl1", s_bg1, "bg1", s_sc1, "sc1"),
                    (2, s_oh2, "oh2", s_fl2, "fl2", s_bg2, "bg2", s_sc2, "sc2")):
                tr.cop("vector", lambda v, oh=oh: v.tensor_tensor(
                    out=s_tmp8[:, :], in0=s_pe8[:, :], in1=oh[:, :], op=ALU.mult),
                       reads=("pe8", ohb), writes=("tmp8",))
                tr.cop("vector", lambda v, flt=flt: v.reduce_sum(
                    flt[:, :], s_tmp8[:, :], axis=AX.X),
                       reads=("tmp8",), writes=(flb,))
                tr.cop("vector", lambda v, flt=flt: v.tensor_scalar(
                    out=flt[:, :], in0=flt[:, :], scalar1=-1.0, scalar2=None,
                    op0=ALU.add),
                       reads=(flb,), writes=(flb,))
                tr.cop("vector", lambda v, flt=flt, bgt=bgt, tc=tc: v.tensor_copy(
                    out=bgt[:, tc:tc + 1], in_=flt[:, :]),
                       reads=(flb,), writes=(bgb,))
                tr.cop("vector", lambda v, sct=sct, par=par, tc=tc: v.tensor_scalar(
                    out=sct[:, 2 * par:2 * par + 1], in0=s_cst[:, 136:137],
                    scalar1=float(tc * 128), scalar2=None, op0=ALU.add),
                       reads=("cst",), writes=(scb,))
                tr.cop("vector", lambda v, oh=oh, w9c=w9c: v.tensor_tensor(
                    out=s_tmp8[:, :], in0=s_w9tm[:, w9c:w9c + 8], in1=oh[:, :],
                    op=ALU.mult),
                       reads=(f"w9tm{par}", ohb), writes=("tmp8",))
                tr.cop("vector", lambda v, sct=sct, par=par: v.reduce_sum(
                    sct[:, 2 * par + 1:2 * par + 2], s_tmp8[:, :], axis=AX.X),
                       reads=("tmp8",), writes=(scb,))
                tr.op("gpsimd",
                      lambda g, sct=sct, bgt=bgt, par=par, tc=tc: g.indirect_dma_start(
                          out=lists_d,
                          out_offset=bass.IndirectOffsetOnAxis(
                              ap=bgt[:, tc:tc + 1], axis=0),
                          in_=sct[:, 2 * par:2 * par + 2],
                          in_offset=None),
                      reads=(scb, bgb), writes=("lists",), sem="sq_sc", inc=16)

        NTC = TT // 128
        r_logits(0)
        r_logits(1)
        for tc in range(NTC):
            r_chain_a(tc)
            if tc > 0:
                r_post(tc - 1)
            r_chain_b(tc)
            if tc + 2 < NTC:
                r_logits(tc + 2)
            r_pe(tc)
        r_post(NTC - 1)
        tr.cop("vector", lambda v: v.tensor_copy(out=s_w9fm[:, :], in_=pbf[3][0:9, 0:TT]),
               reads=("p3",), writes=("w9fm",))

        wa = [("wa0", "sq_wa0", 0), ("wa1", "sq_wa1", 4096)]

        def fill_attn(c, half):
            buf, sm, base = wa[half]
            ch = c // 4
            srcs = (((zq, f"scr_q{ch}"), (zk, f"scr_k{ch}")) if half == 0
                    else ((zv, f"scr_v{ch}"), (zg, f"scr_g{ch}")))
            for pi, (zsrc, zb) in enumerate(srcs):
                off = base + pi * D
                tr.op("sync",
                      lambda s, zsrc=zsrc, c=c, off=off: s.dma_start(
                          out=s_ha[:, off:off + D],
                          in_=zsrc[c * 128:(c + 1) * 128, :]),
                      reads=(zb,), writes=(buf, "hA"), sem=sm, inc=16)

        fill_attn(0, 0)
        fill_attn(0, 1)
        for c in range(NDK):
            for pi, (buf, base) in enumerate(
                    (("wa0", 0), ("wa0", 2048), ("wa1", 4096), ("wa1", 6144))):
                for th in range(2):
                    bank = pi * 2 + th
                    for dk in range(NDK):
                        tr.cop("tensor",
                               lambda t, base=base, bank=bank, dk=dk, th=th: t.matmul(
                                   psum[bank][:, :],
                                   s_ha[:, base + dk * 128: base + (dk + 1) * 128],
                                   s_xn[:, dk * TT + th * 512: dk * TT + (th + 1) * 512],
                                   start=(dk == 0), stop=(dk == NDK - 1)),
                               reads=(buf, "xn"), writes=(f"p{bank}",),
                               close=(dk == NDK - 1))
                if pi == 1 and c < NDK - 1:
                    fill_attn(c + 1, 0)
                if pi == 3 and c < NDK - 1:
                    fill_attn(c + 1, 1)
            for th in range(2):
                tr.cop("scalar",
                       lambda sc, c=c, th=th: sc.activation(
                           out=s_q[:, th * 512:(th + 1) * 512], in_=psum[th][:, :],
                           func=AF.Identity, bias=s_bias[:, c:c + 1], scale=1.0),
                       reads=(f"p{th}", "bias"), writes=("q", "hB"))
                tr.cop("scalar",
                       lambda sc, c=c, th=th: sc.activation(
                           out=s_k[:, th * 512:(th + 1) * 512], in_=psum[2 + th][:, :],
                           func=AF.Identity, bias=s_bias[:, NDK + c:NDK + c + 1],
                           scale=1.0),
                       reads=(f"p{2 + th}", "bias"), writes=("k", "hB"))
                tr.cop("scalar",
                       lambda sc, c=c, th=th: sc.activation(
                           out=s_g[:, th * 512:(th + 1) * 512], in_=psum[6 + th][:, :],
                           func=AF.Sigmoid, bias=s_bias[:, 3 * NDK + c:3 * NDK + c + 1],
                           scale=1.0),
                       reads=(f"p{6 + th}", "bias"), writes=("g", "hB"))
                tr.cop("vector",
                       lambda v, c=c, th=th: v.scalar_tensor_tensor(
                           out=s_kv[:, th * 512:(th + 1) * 512], in0=psum[4 + th][:, :],
                           scalar=s_bias[:, 2 * NDK + c:2 * NDK + c + 1],
                           in1=s_k[:, th * 512:(th + 1) * 512],
                           op0=ALU.add, op1=ALU.mult),
                       reads=(f"p{4 + th}", "bias", "k"), writes=("kv", "utt0", "utt1") + eopj["eop3"])
            tr.cop("vector",
                   lambda v, c=c: v.tensor_tensor_scan(
                       out=s_cs[:, :], data0=s_ones[:, :], data1=s_kv[:, :],
                       initial=s_carry[:, c:c + 1], op0=ALU.mult, op1=ALU.add),
                   reads=("ones", "kv", "utt0", "utt1", "carry"), writes=("cs",) + eopj["eop0"] + eopj["eop1"])
            tr.cop("vector",
                   lambda v, c=c: v.tensor_copy(
                       out=s_carry[:, c:c + 1], in_=s_cs[:, TT - 1:TT]),
                   reads=("cs",), writes=("carry",))
            tr.cop("vector", lambda v: v.tensor_tensor(
                out=s_qg[:, :], in0=s_q[:, :], in1=s_g[:, :], op=ALU.mult),
                   reads=("q", "g"), writes=("qg", "sgt0", "sgt1") + eopj["eop2"])
            tr.cop("vector",
                   lambda v, c=c: v.tensor_tensor(
                       out=s_out[:, c * TT:(c + 1) * TT], in0=s_qg[:, :],
                       in1=s_cs[:, :], op=ALU.mult),
                   reads=("qg", "sgt0", "sgt1", "cs"), writes=("out",))

        wbs = [("wb0", s_wb0), ("wb1", s_wb1)]
        whs = [("wh0", "sq_wh0", 0), ("wh1", "sq_wh1", 4096)]
        wos = [("wo0", "sq_wo0", s_wo0), ("wo1", "sq_wo1", s_wo1)]

        def fill_wh(g, i, slot):
            buf, sm, base = whs[slot]
            for half in range(2):
                zsrc, rbase, zb = gu_src(g, half, i)
                r0 = rbase + i * 128
                off = base + half * D
                tr.op("scalar" if half == 0 else "sync",
                      lambda s, zsrc=zsrc, r0=r0, off=off: s.dma_start(
                          out=s_wh[:, off:off + D],
                          in_=zsrc[r0:r0 + 128, :]),
                      reads=(zb,), writes=(buf,), sem=sm, inc=16)

        def fill_wo(g, ec, slot):
            buf, sm, tile = wos[slot]
            zsrc, rbase, zb = w2_src(g, ec)
            tr.op("sync",
                  lambda s, zsrc=zsrc, rbase=rbase, ec=ec, tile=tile:
                  s.dma_start(
                      out=tile[:, :],
                      in_=zsrc[rbase + ec * 128: rbase + (ec + 1) * 128, :]),
                  reads=(zb,), writes=(buf,), sem=sm, inc=16)

        def make_wb(g, slot):
            wbuf, wtile = wbs[slot]
            r = wrow(g)
            for th in range(2):
                tr.cop("tensor",
                       lambda t, r=r, th=th: t.matmul(
                           psum[6][:, :], s_sel[:, 0:128],
                           s_w9fm[0:9, th * 512:(th + 1) * 512],
                           start=True, stop=True),
                       reads=("sel", "w9fm"), writes=("p6",), close=True)
                tr.cop("scalar",
                       lambda sc, wtile=wtile, th=th: sc.activation(
                           out=wtile[:, th * 512:(th + 1) * 512], in_=psum[6][:, :],
                           func=AF.Identity, scale=1.0),
                       reads=("p6",), writes=(wbuf,))

        def load_lists(e):
            b8 = (e % 2) * 8
            b4 = (e % 2) * 4
            tr.op("scalar",
                  lambda s, e=e, b8=b8: s.dma_start(
                      out=s_glst[:, b8:b8 + 8].rearrange("p (a b) -> p a b", b=2),
                      in_=lists_d[e * CAP:(e + 1) * CAP, :].rearrange(
                          "(a p) b -> p a b", p=128)),
                  reads=("lists",), writes=("glst",), sem="sq_ll", inc=16)
            tr.cop("vector",
                   lambda v, b8=b8, b4=b4: v.tensor_copy(
                       out=s_gif[:, b4:b4 + 4].rearrange("p (a c) -> p a c", c=1),
                       in_=s_glst[:, b8:b8 + 8].rearrange(
                           "p (a c) -> p a c", c=2)[:, :, 0:1]),
                   reads=("glst",), writes=("gif", "me0", "me1"))
            tr.cop("vector",
                   lambda v, b4=b4: v.tensor_scalar(
                       out=s_gif[:, b4:b4 + 4], in0=s_gif[:, b4:b4 + 4],
                       scalar1=float(TT), scalar2=None, op0=ALU.min),
                   reads=("gif",), writes=("gif",))
            tr.cop("vector",
                   lambda v, b4=b4: v.tensor_copy(
                       out=s_gidx[:, b4:b4 + 4], in_=s_gif[:, b4:b4 + 4]),
                   reads=("gif",), writes=("gidx",))
            b4w = (e % 4) * 4
            tr.cop("vector",
                   lambda v, b8=b8, b4w=b4w: v.tensor_copy(
                       out=s_gw[:, b4w:b4w + 4].rearrange("p (a c) -> p a c", c=1),
                       in_=s_glst[:, b8:b8 + 8].rearrange(
                           "p (a c) -> p a c", c=2)[:, :, 1:2]),
                   reads=("glst",), writes=("gw",))

        def gather_tile(e, jt):
            b4 = (e % 2) * 4
            tr.op("gpsimd",
                  lambda g, b4=b4, jt=jt: g.indirect_dma_start(
                      out=s_xg[:, :],
                      out_offset=None,
                      in_=xtok_d,
                      in_offset=bass.IndirectOffsetOnAxis(
                          ap=s_gidx[:, b4 + jt:b4 + jt + 1], axis=0)),
                  reads=("gidx", "xtok"), writes=("xg0",), sem="sq_gx", inc=16)

        def transpose_tile(e, jt):
            for hb in range(2):
                bank = 6 + hb
                for dj in range(8):
                    dk = hb * 8 + dj
                    tr.cop("tensor",
                           lambda t, dk=dk, dj=dj, bank=bank: t.transpose(
                               pbf[bank][:, dj * 128:(dj + 1) * 128],
                               s_xg[:, dk * 128:(dk + 1) * 128],
                               s_idb[:, :]),
                           reads=("xg0", "idb"), writes=(f"p{bank}",),
                           close=(dj == 7))
                tr.cop("vector",
                       lambda v, bank=bank, hb=hb, jt=jt: v.tensor_copy(
                           out=s_xnex[:, :].rearrange("p (k s) -> p k s", k=NDK)[
                               :, hb * 8:hb * 8 + 8, jt * 128:(jt + 1) * 128],
                           in_=pbf[bank][:, 0:1024].rearrange("p (k s) -> p k s", k=8)),
                       reads=(f"p{bank}",), writes=("xnex",))

        make_wb(0, 0)
        gchunk = 0
        fill_wh(0, 0, 0)
        for g in range(NGS):
            wbuf, wtile = wbs[g % 2]
            for i in range(NI):
                whbuf, whsm, whbase = whs[gchunk % 2]
                if i + 1 < NI:
                    fill_wh(g, i + 1, (gchunk + 1) % 2)
                elif g + 1 < NG:
                    fill_wh(g + 1, 0, (gchunk + 1) % 2)
                gchunk += 1
                if g == 2 and i == 0:
                    load_lists(0)
                if g == 2 and i == 5:
                    load_lists(1)
                if g == 3 and i < 8:
                    if i % 2 == 0:
                        gather_tile(0, i // 2)
                    else:
                        transpose_tile(0, i // 2)
                if p == 0 and g >= 1 and i % 2 == 0:
                    emit_piece(deferred[g][i // 2], look=3)
                for half in range(2):
                    for th in range(2):
                        bank = half * 2 + th
                        for dk in range(NDK):
                            tr.cop("tensor",
                                   lambda t, whbase=whbase, half=half, th=th,
                                   bank=bank, dk=dk: t.matmul(
                                       psum[bank][:, :],
                                       s_wh[:, whbase + half * 2048 + dk * 128:
                                            whbase + half * 2048 + (dk + 1) * 128],
                                       s_xn[:, dk * TT + th * 512:
                                            dk * TT + (th + 1) * 512],
                                       start=(dk == 0), stop=(dk == NDK - 1)),
                                   reads=(whbuf, "xn"), writes=(f"p{bank}",),
                                   close=(dk == NDK - 1))
                for th in range(2):
                    tr.cop("scalar",
                           lambda sc, th=th: sc.activation(
                               out=s_sg[:, th * 512:(th + 1) * 512], in_=psum[th][:, :],
                               func=AF.Silu, scale=1.0),
                           reads=(f"p{th}",), writes=(f"sgt{th}",) + eopj["eop2"])
                    tr.cop("vector",
                           lambda v, th=th, wtile=wtile: v.tensor_tensor(
                               out=s_ut[:, th * 512:(th + 1) * 512],
                               in0=psum[2 + th][:, :],
                               in1=wtile[:, th * 512:(th + 1) * 512], op=ALU.mult),
                           reads=(f"p{2 + th}", wbuf), writes=(f"utt{th}",) + eopj["eop3"])
                tr.cop("vector",
                       lambda v, i=i: v.tensor_tensor(
                           out=s_ha[:, i * 1024:(i + 1) * 1024], in0=s_sg[:, :],
                           in1=s_ut[:, :], op=ALU.mult),
                       reads=("sgt0", "sgt1", "utt0", "utt1"), writes=("hA" if i < 8 else "hB",))
            if g + 1 < NGS:
                make_wb(g + 1, (g + 1) % 2)
            fill_wo(g, 0, 0)
            for ec in range(8):
                wobuf, wosm, wotile = wos[ec % 2]
                if ec + 1 < 8:
                    fill_wo(g, ec + 1, (ec + 1) % 2)
                for el in range(2):
                    ek = ec * 2 + el
                    for th in range(2):
                        bank = 4 + (ek * 2 + th) % 2
                        for i in range(NI):
                            tr.cop("tensor",
                                   lambda t, wotile=wotile, i=i, el=el, th=th,
                                   bank=bank: t.matmul(
                                       psum[bank][:, :],
                                       wotile[:, i * 256 + el * 128:
                                              i * 256 + (el + 1) * 128],
                                       s_ha[:, i * 1024 + th * 512:
                                            i * 1024 + (th + 1) * 512],
                                       start=(i == 0), stop=(i == NI - 1)),
                                   reads=(wobuf, "hA" if i < 8 else "hB"),
                                   writes=(f"p{bank}",),
                                   close=(i == NI - 1))
                        tr.cop("vector",
                               lambda v, ek=ek, th=th, bank=bank: v.tensor_tensor(
                                   out=s_out[:, ek * TT + th * 512:
                                             ek * TT + (th + 1) * 512],
                                   in0=psum[bank][:, :],
                                   in1=s_out[:, ek * TT + th * 512:
                                             ek * TT + (th + 1) * 512],
                                   op=ALU.add),
                               reads=(f"p{bank}", "out"), writes=("out",))
        for e in range(NE):
            g = NGS + e
            b4w = (e % 4) * 4
            for i in range(NI):
                whbuf, whsm, whbase = whs[gchunk % 2]
                if i + 1 < NI:
                    fill_wh(g, i + 1, (gchunk + 1) % 2)
                elif g + 1 < NG:
                    fill_wh(g + 1, 0, (gchunk + 1) % 2)
                gchunk += 1
                if p == 0 and e in (0, 1, 2) and i % 2 == 0:
                    emit_piece(deferred[("r", e)][i // 2], look=3)
                if e + 2 < NE and i == 1:
                    load_lists(e + 2)
                ub = 2 * (i % 2)
                for half in range(2):
                    for dk in range(NDK):
                        tr.cop("tensor",
                               lambda t, whbase=whbase, half=half, dk=dk, ub=ub: t.matmul(
                                   psum[ub + half][:, :],
                                   s_wh[:, whbase + half * 2048 + dk * 128:
                                        whbase + half * 2048 + (dk + 1) * 128],
                                   s_xnex[:, dk * CAP:(dk + 1) * CAP],
                                   start=(dk == 0), stop=(dk == NDK - 1)),
                               reads=(whbuf, "xnex"), writes=(f"p{ub + half}",),
                               close=(dk == NDK - 1))
                par = i % 2
                tr.cop("scalar",
                       lambda sc, ub=ub, par=par: sc.activation(
                           out=s_sg[:, par * 512:(par + 1) * 512], in_=psum[ub][:, :],
                           func=AF.Silu, scale=1.0),
                       reads=(f"p{ub}",), writes=(f"sgt{par}",) + eopj["eop2"])
                tr.cop("vector",
                       lambda v, ub=ub, par=par: v.tensor_copy(
                           out=s_ut[:, par * 512:(par + 1) * 512],
                           in_=psum[ub + 1][:, :]),
                       reads=(f"p{ub + 1}",), writes=(f"utt{par}",) + eopj["eop3"])
                tr.cop("vector",
                       lambda v, i=i, par=par: v.tensor_tensor(
                           out=s_ha[:, i * 512:(i + 1) * 512],
                           in0=s_sg[:, par * 512:(par + 1) * 512],
                           in1=s_ut[:, par * 512:(par + 1) * 512], op=ALU.mult),
                       reads=(f"sgt{par}", f"utt{par}"), writes=("hA",))
            fill_wo(g, 0, 0)
            for ec in range(8):
                wobuf, wosm, wotile = wos[ec % 2]
                if ec + 1 < 8:
                    fill_wo(g, ec + 1, (ec + 1) % 2)
                if e + 1 < NE:
                    if ec % 2 == 0:
                        gather_tile(e + 1, ec // 2)
                    else:
                        transpose_tile(e + 1, ec // 2)
                for jt in range(CT):
                    for i in range(NI):
                        tr.cop("tensor",
                               lambda t, wotile=wotile, i=i, jt=jt: t.matmul(
                                   psum[2 + jt][:, 0:256],
                                   s_ha[:, i * 512 + jt * 128:
                                        i * 512 + (jt + 1) * 128],
                                   wotile[:, i * 256:(i + 1) * 256],
                                   start=(i == 0), stop=(i == NI - 1)),
                               reads=(wobuf, "hA"), writes=(f"p{2 + jt}",),
                               close=(i == NI - 1))
                    ereg, ename, extra = eop_regions[ec % 4]
                    jn = eopj[ename][jt]
                    tr.cop("scalar",
                           lambda sc, jt=jt, ereg=ereg, b4w=b4w: sc.activation(
                               out=ereg[:, jt * 256:(jt + 1) * 256],
                               in_=psum[2 + jt][:, 0:256], func=AF.Identity,
                               scale=s_gw[:, b4w + jt:b4w + jt + 1]),
                           reads=(f"p{2 + jt}", "gw"), writes=(jn,) + extra)
                    tr.op("scalar",
                          lambda s, e=e, jt=jt, ec=ec, ereg=ereg: s.dma_start(
                              out=eout_d[e * CAP + jt * 128: e * CAP + (jt + 1) * 128,
                                         ec * 256:(ec + 1) * 256],
                              in_=ereg[:, jt * 256:(jt + 1) * 256]),
                          reads=(jn,), writes=("eout",), sem="sq_eo", inc=16)
        def bg_slot(slot):
            if slot < 4:
                return s_xnex[:, slot * 2048:(slot + 1) * 2048]
            return s_ha[:, (slot - 4) * 2048:(slot - 3) * 2048]

        def back_gather(tc, cls):
            slot = (2 * tc + cls) % 8
            st = bg_slot(slot)
            bgt, bgb = (s_bg1, "bg1") if cls == 0 else (s_bg2, "bg2")
            tr.op("gpsimd",
                  lambda g, st=st, bgt=bgt, tc=tc: g.indirect_dma_start(
                      out=st, out_offset=None, in_=eout_d,
                      in_offset=bass.IndirectOffsetOnAxis(
                          ap=bgt[:, tc:tc + 1], axis=0)),
                  reads=(bgb, "eout"),
                  writes=(f"bgs{slot}", "xnex" if slot < 4 else "hA"),
                  sem="sq_bg", inc=16)

        for tc in range(4):
            back_gather(tc, 0)
            back_gather(tc, 1)
        otm = [(s_wh[:, 0:4096].bitcast(F32), ("wh0",)),
               (s_wh[:, 4096:8192].bitcast(F32), ("wh1",))]
        for tc in range(TT // 128):
            sb, ssem, stile = xslot[tc % 2]
            row = t0 + tc * 128
            tr.op("scalar",
                  lambda s, stile=stile, row=row: s.dma_start(
                      out=stile[:, :], in_=x_d[row:row + 128, :]),
                  writes=(sb,), sem=ssem, inc=16)
            ot, otbufs = otm[tc % 2]
            slot1 = (2 * tc) % 8
            slot2 = (2 * tc + 1) % 8
            g1, g2 = bg_slot(slot1), bg_slot(slot2)
            for quad in range(4):
                bank = quad
                gbank = 4 + quad
                for ej in range(4):
                    ek = quad * 4 + ej
                    tr.cop("tensor",
                           lambda t, bank=bank, ej=ej, ek=ek, tc=tc: t.transpose(
                               psum[bank][:, ej * 128:(ej + 1) * 128],
                               s_out[:, ek * TT + tc * 128: ek * TT + (tc + 1) * 128],
                               s_idf[:, :]),
                           reads=("out", "idf"), writes=(f"p{bank}",),
                           close=(ej == 3))
                tr.cop("tensor",
                       lambda t, gbank=gbank, quad=quad, g1=g1: t.matmul(
                           psum[gbank][:, 0:512],
                           s_idb[:, 0:128],
                           g1[:, quad * 512:(quad + 1) * 512],
                           start=True, stop=False),
                       reads=(f"bgs{slot1}", "idb",
                              "xnex" if slot1 < 4 else "hA"),
                       writes=(f"p{gbank}",), close=False)
                tr.cop("tensor",
                       lambda t, gbank=gbank, quad=quad, g2=g2: t.matmul(
                           psum[gbank][:, 0:512],
                           s_idb[:, 0:128],
                           g2[:, quad * 512:(quad + 1) * 512],
                           start=False, stop=True),
                       reads=(f"bgs{slot2}", "idb",
                              "xnex" if slot2 < 4 else "hA"),
                       writes=(f"p{gbank}",), close=True)
                tr.cop("vector",
                       lambda v, bank=bank, quad=quad, ot=ot, stile=stile:
                       v.tensor_tensor(
                           out=ot[:, quad * 512:(quad + 1) * 512],
                           in0=psum[bank][:, 0:512],
                           in1=stile[:, quad * 512:(quad + 1) * 512], op=ALU.add),
                       reads=(f"p{bank}", sb), writes=otbufs)
                tr.cop("vector",
                       lambda v, gbank=gbank, quad=quad, ot=ot:
                       v.tensor_tensor(
                           out=ot[:, quad * 512:(quad + 1) * 512],
                           in0=psum[gbank][:, 0:512],
                           in1=ot[:, quad * 512:(quad + 1) * 512], op=ALU.add),
                       reads=(f"p{gbank}",) + otbufs, writes=otbufs)
            tr.op("sync" if tc % 2 == 0 else "scalar",
                  lambda s, ot=ot, row=row: s.dma_start(
                      out=out_d[row:row + 128, :], in_=ot[:, :]),
                  reads=otbufs, writes=(f"sto{tc % 2}",),
                  sem=f"sq_st{tc % 2}", inc=16)
            n_store += 1
            if tc + 4 < TT // 128:
                back_gather(tc + 4, 0)
                back_gather(tc + 4, 1)

    tr.barrier_wait("vector", "sq_st0", 16 * (n_store // 2))
    tr.barrier_wait("vector", "sq_st1", 16 * (n_store // 2))

    block = E(nc.Block())
    tr.replay(block)
    ctx.close()
    return nc


_NC_CACHE = {}


def _get_nc():
    if "nc" not in _NC_CACHE:
        _NC_CACHE["nc"] = build_nc()
    return _NC_CACHE["nc"]


NEH = 8


def make_in_maps(inputs):
    x = np.ascontiguousarray(np.asarray(inputs["L0_attn_x"], dtype=np.float32))
    bqkvg = np.stack([np.asarray(inputs["L0_attn_bq"]), np.asarray(inputs["L0_attn_bk"]),
                      np.asarray(inputs["L0_attn_bv"]), np.asarray(inputs["L0_attn_bg"])],
                     axis=0)
    bqkvg = np.ascontiguousarray(bqkvg, dtype=np.float32)
    gw9 = np.concatenate([np.asarray(inputs["L0_ffn_gate_w"]),
                          np.asarray(inputs["L0_ffn_shared_gate_w"])], axis=0)
    gw9 = np.ascontiguousarray(gw9, dtype=np.float32)
    idf = np.eye(128, dtype=np.float32)
    idb = np.eye(128).astype(ml_dtypes.bfloat16)
    cst = np.zeros((128, 138), dtype=np.float32)
    cst[:, 0:128] = np.triu(np.ones((128, 128), dtype=np.float32))
    cst[:, 128:136] = np.arange(8, dtype=np.float32)[None, :]
    cst[:, 136] = np.arange(128, dtype=np.float32)
    cst[:, 137] = 1.0
    sel = np.zeros((9, 9, 128), dtype=np.float32)
    for r in range(9):
        sel[r, r, :] = 1.0
    sel = np.ascontiguousarray(sel.reshape(9, 9 * 128)).astype(ml_dtypes.bfloat16)
    def sw_up(w, rc):
        w = np.asarray(w, dtype=np.float32)
        return np.ascontiguousarray(
            w.reshape(rc, 128, 16, 128).transpose(0, 3, 2, 1).reshape(rc * 128, 2048))

    w2f = np.asarray(inputs["L0_ffn_w2"], dtype=np.float32)
    w2sw = np.ascontiguousarray(
        w2f.reshape(NEH, 8, 2, 128, 11, 128).transpose(0, 1, 5, 4, 2, 3)
        .reshape(NEH, 1024, 2816))
    wsdf = np.asarray(inputs["L0_ffn_shared_down_w"], dtype=np.float32)
    wsdsw = np.ascontiguousarray(
        wsdf.reshape(8, 2, 128, 4, 11, 128).transpose(3, 0, 5, 4, 1, 2)
        .reshape(4 * 1024, 2816))
    common = dict(
        wq=sw_up(inputs["L0_attn_Wq"], 16),
        wk=sw_up(inputs["L0_attn_Wk"], 16),
        wv=sw_up(inputs["L0_attn_Wv"], 16),
        wg=sw_up(inputs["L0_attn_Wg"], 16),
        bqkvg=bqkvg, gw9=gw9,
        wgu=sw_up(inputs["L0_ffn_shared_gate_up_w"], 88),
        wsd=wsdsw,
        w13=np.ascontiguousarray(
            np.asarray(inputs["L0_ffn_w13"], dtype=np.float32)
            .reshape(NEH, 22, 128, 16, 128).transpose(0, 1, 4, 3, 2)
            .reshape(NEH, 2816, 2048)),
        w2=w2sw,
        idf=idf, idb=idb, sel=sel, cst=cst,
    )
    n = x.shape[0]
    return [dict(common, x=np.ascontiguousarray(x[i])) for i in range(n)]


def kernel(**inputs):
    in_maps = make_in_maps(inputs)
    nc = _get_nc()
    res = run_bass_kernel_spmd(nc, in_maps, core_ids=list(range(8)))
    return np.stack([r["out"] for r in res.results], axis=0)


# version 12 (final)
# speedup vs baseline: 1.2218x; 1.0594x over previous
import numpy as np
import ml_dtypes
import concourse.bass as bass
import concourse.mybir as mybir
from concourse.bass_utils import run_bass_kernel_spmd

F32 = mybir.dt.float32
BF16 = mybir.dt.bfloat16
AF = mybir.ActivationFunctionType
ALU = mybir.AluOpType
AX = mybir.AxisListType

S = 4096
D = 2048
TT = 1024
NPASS = S // TT
NDK = D // 128
MOE_I = 1408
SH_I = 5632
NE = 8
NG = 12
NGS = 4
NI = MOE_I // 128
EPS = 1e-5
CAP = 512
CT = CAP // 128
PADV = 1.0e6

ENGS = ("tensor", "vector", "scalar", "sync", "gpsimd")
GAP = 8


class Tracker:

    def __init__(self, nc):
        self.nc = nc
        self.q = {e: [] for e in ENGS}
        self.cnt = {}
        self.bufs = {}
        self.opidx = {e: 0 for e in ENGS}
        self.waited = {e: {} for e in ENGS}
        self.pending = []
        self.esem = {}

    def sem(self, name):
        self.cnt.setdefault(name, 0)
        return name

    def engine_sems(self, names):
        for e, s in names.items():
            self.esem[e] = self.sem(s)

    def _want(self, need, sem, val):
        if need.get(sem, 0) < val:
            need[sem] = val

    def _edge(self, eng, weng, widx, wdma):
        if weng is None:
            return False
        if wdma or weng != eng:
            return True
        return (self.opidx[eng] - widx) < GAP

    def op(self, eng, fn, reads=(), writes=(), sem=None, inc=0, close=True,
           manual_waits=()):
        need = {}
        for (msem, mval) in manual_waits:
            self._want(need, msem, mval)
        for b in reads:
            rec = self.bufs.get(b)
            if rec and rec["w"] and self._edge(eng, rec["weng"], rec["widx"], rec["wdma"]):
                self._want(need, rec["w"][0], rec["w"][1])
        for b in writes:
            rec = self.bufs.get(b)
            if rec:
                if rec["w"] and self._edge(eng, rec["weng"], rec["widx"], rec["wdma"]):
                    self._want(need, rec["w"][0], rec["w"][1])
                for rsem, (rcnt, reng, ridx, rdma) in rec["r"].items():
                    if self._edge(eng, reng, ridx, rdma):
                        self._want(need, rsem, rcnt)
        for s, v in sorted(need.items()):
            if self.waited[eng].get(s, 0) < v:
                self.q[eng].append(("wait", s, v))
                self.waited[eng][s] = v
        idx = self.opidx[eng]
        self.opidx[eng] += 1
        is_dma = inc >= 16
        ev = None
        if inc:
            self.cnt[sem] += inc
            ev = (sem, self.cnt[sem])
        self.q[eng].append(("op", fn, sem if inc else None, inc))
        if not close:
            self.pending.append((eng, idx, tuple(reads), tuple(writes)))
            return None
        if ev:
            for (peng, pidx, preads, pwrites) in self.pending:
                if peng != eng:
                    continue
                for b in preads:
                    self._note_read(b, ev, peng, pidx, is_dma)
                for b in pwrites:
                    self._note_write(b, ev, peng, pidx, is_dma)
            self.pending = [p for p in self.pending if p[0] != eng]
        for b in reads:
            self._note_read(b, ev, eng, idx, is_dma)
        for b in writes:
            self._note_write(b, ev, eng, idx, is_dma)
        return ev

    def _note_read(self, b, ev, eng, idx, dma):
        rec = self.bufs.setdefault(
            b, {"w": None, "weng": None, "widx": -1 << 30, "wdma": False, "r": {}})
        if ev is None:
            return
        rec["r"][ev[0]] = (ev[1], eng, idx, dma)

    def _note_write(self, b, ev, eng, idx, dma):
        self.bufs[b] = {"w": ev, "weng": eng, "widx": idx, "wdma": dma, "r": {}}

    def cop(self, eng, fn, reads=(), writes=(), close=True, manual_waits=()):
        return self.op(eng, fn, reads, writes, sem=self.esem[eng],
                       inc=1 if close else 0, close=close,
                       manual_waits=manual_waits)

    def barrier_wait(self, eng, sem, val):
        if self.waited[eng].get(sem, 0) < val:
            self.q[eng].append(("wait", sem, val))
            self.waited[eng][sem] = val

    def replay(self, block):
        for name in ENGS:
            items = self.q[name]

            def make(items=items):
                def body(eng):
                    for it in items:
                        if it[0] == "wait":
                            eng.wait_ge(self.nc._sems[it[1]], it[2])
                        else:
                            _, fn, sm, inc = it
                            ins = fn(eng)
                            if sm is not None:
                                ins.then_inc(self.nc._sems[sm], inc)
                return body

            getattr(block, name)(make())


def build_nc():
    nc = bass.Bass("TRN2", target_bir_lowering=False)

    x_d = nc.dram_tensor("x", [S, D], F32, kind="ExternalInput").ap()
    wq_d = nc.dram_tensor("wq", [D, D], F32, kind="ExternalInput").ap()
    wk_d = nc.dram_tensor("wk", [D, D], F32, kind="ExternalInput").ap()
    wv_d = nc.dram_tensor("wv", [D, D], F32, kind="ExternalInput").ap()
    wg_d = nc.dram_tensor("wg", [D, D], F32, kind="ExternalInput").ap()
    bqkvg_d = nc.dram_tensor("bqkvg", [4, D], F32, kind="ExternalInput").ap()
    gw9_d = nc.dram_tensor("gw9", [9, D], F32, kind="ExternalInput").ap()
    wgu_d = nc.dram_tensor("wgu", [2 * SH_I, D], F32, kind="ExternalInput").ap()
    wsd_d = nc.dram_tensor("wsd", [4 * 1024, 2816], F32, kind="ExternalInput").ap()
    w13_d = nc.dram_tensor("w13", [NE, 2 * MOE_I, D], F32, kind="ExternalInput").ap()
    w2_d = nc.dram_tensor("w2", [NE, 1024, 2816], F32, kind="ExternalInput").ap()
    idf_d = nc.dram_tensor("idf", [128, 128], F32, kind="ExternalInput").ap()
    idb_d = nc.dram_tensor("idb", [128, 128], BF16, kind="ExternalInput").ap()
    sel_d = nc.dram_tensor("sel", [9, 9 * 128], BF16, kind="ExternalInput").ap()
    cst_d = nc.dram_tensor("cst", [128, 138], F32, kind="ExternalInput").ap()
    out_d = nc.dram_tensor("out", [S, D], F32, kind="ExternalOutput").ap()

    lists_d = nc.dram_tensor("lists", [NE * CAP, 2], F32, kind="Internal").ap()
    eout_d = nc.dram_tensor("eout", [NE * CAP, D], BF16, kind="Internal").ap()
    xtok_d = nc.dram_tensor("xtok", [TT + 128, D], BF16, kind="Internal").ap()

    zq = nc.dram_tensor("zq", [D, D], BF16, kind="Internal").ap()
    zk = nc.dram_tensor("zk", [D, D], BF16, kind="Internal").ap()
    zv = nc.dram_tensor("zv", [D, D], BF16, kind="Internal").ap()
    zg = nc.dram_tensor("zg", [D, D], BF16, kind="Internal").ap()
    zgu = nc.dram_tensor("zgu", [2 * SH_I, D], BF16, kind="Internal").ap()
    zsd = nc.dram_tensor("zsd", [4 * 1024, 2816], BF16, kind="Internal").ap()
    z13 = nc.dram_tensor("z13", [NE, 2 * MOE_I, D], BF16, kind="Internal").ap()
    z2 = nc.dram_tensor("z2", [NE, 1024, 2816], BF16, kind="Internal").ap()

    from contextlib import ExitStack
    ctx = ExitStack()
    E = ctx.enter_context

    s_xn = E(nc.sbuf_tensor("s_xn", [128, NDK * TT], BF16))
    s_out = E(nc.sbuf_tensor("s_out", [128, NDK * TT], F32))
    s_ha = E(nc.sbuf_tensor("s_ha", [128, 11264], BF16))
    s_q = s_ha[:, 8192:8192 + TT]
    s_k = s_ha[:, 9216:9216 + TT]
    s_g = s_ha[:, 10240:10240 + TT]
    s_wh = E(nc.sbuf_tensor("s_wh", [128, 8192], BF16))
    s_wo0 = E(nc.sbuf_tensor("s_wo0", [128, NI * 256], BF16))
    s_wo1 = E(nc.sbuf_tensor("s_wo1", [128, NI * 256], BF16))
    s_xa = E(nc.sbuf_tensor("s_xa", [128, D], F32))
    s_xb = E(nc.sbuf_tensor("s_xb", [128, D], F32))
    s_xntm = E(nc.sbuf_tensor("s_xntm", [128, D], BF16))
    s_sg = E(nc.sbuf_tensor("s_sg", [128, TT], BF16))
    s_qg = s_sg
    s_ut = E(nc.sbuf_tensor("s_ut", [128, TT], BF16))
    s_kv = s_ut
    s_wb0 = E(nc.sbuf_tensor("s_wb0", [128, TT], BF16))
    s_wb1 = E(nc.sbuf_tensor("s_wb1", [128, TT], BF16))
    s_w9fm = E(nc.sbuf_tensor("s_w9fm", [9, TT], BF16))
    s_ones = E(nc.sbuf_tensor("s_ones", [128, TT], BF16))
    s_carry = E(nc.sbuf_tensor("s_carry", [128, NDK], F32))
    s_bias = E(nc.sbuf_tensor("s_bias", [128, 4 * NDK], F32))
    s_gw9T = E(nc.sbuf_tensor("s_gw9T", [128, NDK * 9], BF16))
    s_sel = E(nc.sbuf_tensor("s_sel", [9, 128], BF16))
    s_idf = E(nc.sbuf_tensor("s_idf", [128, 128], F32))
    s_idb = E(nc.sbuf_tensor("s_idb", [128, 128], BF16))
    s_w9tm = E(nc.sbuf_tensor("s_w9tm", [128, 32], BF16))
    s_negm1 = E(nc.sbuf_tensor("s_negm1", [128, 1], F32))
    s_sum = E(nc.sbuf_tensor("s_sum", [128, 1], F32))
    s_rinv = E(nc.sbuf_tensor("s_rinv", [128, 1], F32))
    s_exp = E(nc.sbuf_tensor("s_exp", [128, 8], F32))
    s_bn = E(nc.sbuf_tensor("s_bn", [128, 24], F32))
    s_mv = E(nc.sbuf_tensor("s_mv", [128, 2], F32))
    s_lnv = E(nc.sbuf_tensor("s_lnv", [128, 1], F32))
    s_rstd = E(nc.sbuf_tensor("s_rstd", [128, 1], F32))
    s_eps = E(nc.sbuf_tensor("s_eps", [128, 1], F32))
    U32 = mybir.dt.uint32
    s_cst = E(nc.sbuf_tensor("s_cst", [128, 138], F32))
    s_xnex = E(nc.sbuf_tensor("s_xnex", [128, NDK * CAP], BF16))
    s_xg = E(nc.sbuf_tensor("s_xg", [128, D], BF16))
    s_glst = E(nc.sbuf_tensor("s_glst", [128, 16], F32))
    s_gidx = E(nc.sbuf_tensor("s_gidx", [128, 8], U32))
    s_gw = E(nc.sbuf_tensor("s_gw", [128, 16], F32))
    s_i1 = E(nc.sbuf_tensor("s_i1", [128, 8], F32))
    s_i2 = E(nc.sbuf_tensor("s_i2", [128, 8], U32))
    s_oh1 = E(nc.sbuf_tensor("s_oh1", [128, 8], F32))
    s_oh2 = E(nc.sbuf_tensor("s_oh2", [128, 8], F32))
    s_me = E(nc.sbuf_tensor("s_me", [128, 16], F32))
    s_gif = s_me
    s_pe8 = E(nc.sbuf_tensor("s_pe8", [128, 8], F32))
    s_tmp8 = E(nc.sbuf_tensor("s_tmp8", [128, 8], F32))
    s_base = E(nc.sbuf_tensor("s_base", [1, 8], F32))
    s_fl1 = E(nc.sbuf_tensor("s_fl1", [128, 1], F32))
    s_fl2 = E(nc.sbuf_tensor("s_fl2", [128, 1], F32))
    s_bg1 = E(nc.sbuf_tensor("s_bg1", [128, 8], U32))
    s_bg2 = E(nc.sbuf_tensor("s_bg2", [128, 8], U32))
    s_sc1 = E(nc.sbuf_tensor("s_sc1", [128, 8], F32))
    s_sc2 = E(nc.sbuf_tensor("s_sc2", [128, 8], F32))
    s_eop = E(nc.sbuf_tensor("s_eop", [128, 2048], BF16))
    s_cs = s_eop[:, :].bitcast(F32)
    s_fill = E(nc.sbuf_tensor("s_fill", [128, 16], F32))

    psum = [E(nc.psum_tensor(f"p{i}", [128, 512], F32)) for i in range(8)]
    pbf = [psum[i][:, :].bitcast(BF16) for i in range(8)]

    tr = Tracker(nc)
    sem_names = ["se_pe", "se_dve", "se_act", "sq_wa0", "sq_wa1",
                 "sq_wh0", "sq_wh1", "sq_wo0", "sq_wo1", "sq_xa", "sq_xb",
                 "sq_xc",
                 "sq_st0", "sq_st1", "sq_m0", "sq_m1", "sq_m2", "sq_m3", "sq_m4",
                 "sq_m5", "sq_sc", "sq_ll", "sq_gx", "sq_bg", "sq_eo", "sq_xt",
                 "sq_lf"]
    cast_sems = [f"sq_c{i}" for i in range(29)]
    sem_names += cast_sems
    nc._sems = {}
    for s in sem_names:
        nc._sems[s] = E(nc.semaphore(s))
        tr.sem(s)
    tr.engine_sems({"tensor": "se_pe", "vector": "se_dve", "scalar": "se_act"})

    def z13_pieces(e):
        return [(f"scr_13_{e}p{j}", z13[e][j * 704:(j + 1) * 704, :],
                 w13_d[e][j * 704:(j + 1) * 704, :]) for j in range(4)]

    def z2_pieces(e):
        return [(f"scr_2_{e}p{j}", z2[e][j * 512:(j + 1) * 512, :],
                 w2_d[e][j * 512:(j + 1) * 512, :]) for j in range(2)]

    upfront = []
    attn_mats = [("q", zq, wq_d), ("k", zk, wk_d), ("v", zv, wv_d), ("g", zg, wg_d)]
    for ch in range(4):
        for nm, zdst, wsrc in attn_mats:
            upfront.append((f"scr_{nm}{ch}", zdst[ch * 512:(ch + 1) * 512, :],
                            wsrc[ch * 512:(ch + 1) * 512, :]))
    for gi in range(4):
        upfront.append((f"scr_gu_g{gi}", zgu[gi * MOE_I:(gi + 1) * MOE_I, :],
                        wgu_d[gi * MOE_I:(gi + 1) * MOE_I, :]))
        upfront.append((f"scr_gu_u{gi}", zgu[SH_I + gi * MOE_I:SH_I + (gi + 1) * MOE_I, :],
                        wgu_d[SH_I + gi * MOE_I:SH_I + (gi + 1) * MOE_I, :]))
    for sg in range(4):
        upfront.append((f"scr_sd{sg}", zsd[sg * 1024:(sg + 1) * 1024, :],
                        wsd_d[sg * 1024:(sg + 1) * 1024, :]))
    for e in (0, 1):
        upfront += z13_pieces(e) + z2_pieces(e)
    deferred = {}
    for g in (1, 2, 3):
        deferred[g] = z13_pieces(g + 1) + z2_pieces(g + 1)
    for e in (0, 1, 2):
        deferred[("r", e)] = z13_pieces(e + 5) + z2_pieces(e + 5)
    n_cast = len(upfront) + sum(len(v) for v in deferred.values())

    cast_state = {"n": 0, "evs": []}

    def emit_piece(pc, look=2):
        nm, dst, src = pc
        evs = cast_state["evs"]
        waits = [evs[-look]] if len(evs) >= look else []
        sem = cast_sems[cast_state["n"] % len(cast_sems)]
        ev = tr.op("gpsimd", lambda g, dst=dst, src=src: g.dma_start(out=dst, in_=src),
                   writes=(nm,), sem=sem, inc=16, manual_waits=waits)
        evs.append(ev)
        cast_state["n"] += 1

    for pc in upfront:
        emit_piece(pc, look=5)

    def gu_src(g, half, i):
        if g < 4:
            return zgu, (SH_I if half else 0) + g * MOE_I, f"scr_gu_{'u' if half else 'g'}{g}"
        e = g - 4
        rbase = MOE_I if half else 0
        return z13[e], rbase, f"scr_13_{e}p{(rbase + i * 128) // 704}"

    def w2_src(g, ec):
        if g < 4:
            return zsd, g * 1024, f"scr_sd{g}"
        return z2[g - 4], 0, f"scr_2_{g - 4}p{(ec * 128) // 512}"

    def wrow(g):
        return 8 if g < 4 else g - 4

    tr.op("sync", lambda s: s.dma_start(out=s_xa[0:9, :], in_=gw9_d),
          writes=("xa",), sem="sq_m0", inc=16)
    tr.op("sync", lambda s: s.dma_start(out=s_out[0:1, 0:4 * D],
                                        in_=bqkvg_d.rearrange("j d -> (j d)")[None, :]),
          writes=("out",), sem="sq_m1", inc=16)
    tr.op("sync", lambda s: s.dma_start(out=s_sel[:, :], in_=sel_d[:, 8 * 128:9 * 128]),
          writes=("sel",), sem="sq_m2", inc=16)
    tr.op("sync", lambda s: s.dma_start(out=s_idf[:, :], in_=idf_d),
          writes=("idf",), sem="sq_m3", inc=16)
    tr.op("sync", lambda s: s.dma_start(out=s_idb[:, :], in_=idb_d),
          writes=("idb",), sem="sq_m4", inc=16)
    tr.op("sync", lambda s: s.dma_start(out=s_cst[:, :], in_=cst_d),
          writes=("cst",), sem="sq_m5", inc=16)
    tr.cop("vector", lambda v: v.memset(s_ones[:, :], 1.0), writes=("ones",))
    tr.cop("vector", lambda v: v.memset(s_carry[:, :], 0.0), writes=("carry",))
    tr.cop("vector", lambda v: v.memset(s_eps[:, :], EPS), writes=("eps",))
    tr.cop("vector", lambda v: v.memset(s_fill[:, :], PADV), writes=("fillc",))
    tr.cop("vector", lambda v: v.memset(s_xg[:, :], 0.0), writes=("xg0",))
    for dk in range(NDK):
        tr.cop("tensor",
               lambda t, dk=dk: t.matmul(psum[0][:, dk * 9:dk * 9 + 9],
                                         s_xa[0:9, dk * 128:(dk + 1) * 128],
                                         s_idf[0:9, 0:9], start=True, stop=True),
               reads=("xa", "idf"), writes=("p0",), close=(dk == NDK - 1))
    tr.cop("vector", lambda v: v.tensor_copy(out=s_gw9T[:, :], in_=psum[0][:, 0:NDK * 9]),
           reads=("p0",), writes=("gw9T",))
    for j in range(4):
        for c in range(NDK):
            tr.cop("tensor",
                   lambda t, j=j, c=c: t.matmul(
                       psum[1][:, j * NDK + c:j * NDK + c + 1],
                       s_out[0:1, j * D + c * 128: j * D + (c + 1) * 128],
                       s_idf[0:1, 0:1], start=True, stop=True),
                   reads=("out", "idf"), writes=("p1",),
                   close=(j == 3 and c == NDK - 1))
    tr.cop("vector", lambda v: v.tensor_copy(out=s_bias[:, :], in_=psum[1][:, 0:4 * NDK]),
           reads=("p1",), writes=("bias",))

    xslot = [("xa", "sq_xa", s_xa), ("xb", "sq_xb", s_xb)]
    s_xc = s_xnex[:, 0:4096].bitcast(F32)
    xslot3 = [("xa", "sq_xa", s_xa), ("xb", "sq_xb", s_xb),
              ("xc", "sq_xc", s_xc)]
    eop_regions = [
        (s_eop[:, 0:1024], "eop0", ()),
        (s_eop[:, 1024:2048], "eop1", ()),
        (s_sg[:, 0:1024], "eop2", ("sgt0", "sgt1")),
        (s_ut[:, 0:1024], "eop3", ("utt0", "utt1")),
    ]
    eopj = {r[1]: tuple(f"{r[1]}j{jt}" for jt in range(4)) for r in eop_regions}
    n_store = 0

    for p in range(NPASS):
        t0 = p * TT
        for tc in range(TT // 128):
            sb, ssem, stile = xslot3[tc % 3]
            row = t0 + tc * 128
            lq = ("sync", "scalar", "sync" if p == 0 else "gpsimd")[tc % 3]
            tr.op(lq,
                  lambda s, stile=stile, row=row: s.dma_start(
                      out=stile[:, :], in_=x_d[row:row + 128, :]),
                  writes=(sb,) if sb != "xc" else ("xc", "xnex"), sem=ssem, inc=16)
            for i in range(4):
                tr.cop("vector",
                       lambda v, i=i, stile=stile: v.bn_stats(
                           s_bn[:, i * 6:(i + 1) * 6], stile[:, i * 512:(i + 1) * 512]),
                       reads=(sb,), writes=("bn",))
            tr.cop("vector", lambda v: v.bn_aggr(s_mv[:, :], s_bn[:, :]),
                   reads=("bn",), writes=("mv",))
            tr.cop("scalar", lambda sc: sc.activation(out=s_lnv[:, :], in_=s_mv[:, 1:2],
                                                      func=AF.Ln, bias=s_eps[:, 0:1],
                                                      scale=1.0),
                   reads=("mv", "eps"), writes=("lnv",))
            tr.cop("scalar", lambda sc: sc.activation(out=s_rstd[:, :], in_=s_lnv[:, :],
                                                      func=AF.Exp, scale=-0.5),
                   reads=("lnv",), writes=("rstd",))
            tr.cop("vector",
                   lambda v, stile=stile: v.tensor_scalar(
                       out=s_xntm[:, :], in0=stile[:, :], scalar1=s_mv[:, 0:1],
                       scalar2=s_rstd[:, 0:1], op0=ALU.subtract, op1=ALU.mult),
                   reads=(sb, "mv", "rstd"), writes=("xntm",))
            tr.op("scalar",
                  lambda s, tc=tc: s.dma_start(
                      out=xtok_d[tc * 128:(tc + 1) * 128, :], in_=s_xntm[:, :]),
                  reads=("xntm",), writes=("xtok",), sem="sq_xt", inc=16)
            for quad in range(4):
                bank = quad % 2
                for dj in range(4):
                    dk = quad * 4 + dj
                    tr.cop("tensor",
                           lambda t, bank=bank, dj=dj, dk=dk: t.transpose(
                               pbf[bank][:, dj * 128:(dj + 1) * 128],
                               s_xntm[:, dk * 128:(dk + 1) * 128], s_idb[:, :]),
                           reads=("xntm", "idb"), writes=(f"p{bank}",),
                           close=(dj == 3))
                dk0 = quad * 4
                tr.cop("vector",
                       lambda v, bank=bank, dk0=dk0, tc=tc: v.tensor_copy(
                           out=s_xn[:, :].rearrange("p (k t) -> p k t", k=NDK)[
                               :, dk0:dk0 + 4, tc * 128:(tc + 1) * 128],
                           in_=pbf[bank][:, 0:512].rearrange("p (k t) -> p k t", k=4)),
                       reads=(f"p{bank}",), writes=("xn",))

        tr.cop("vector", lambda v: v.memset(s_base[:, :], 0.0), writes=("base",))
        for fh in range(4):
            tr.op("gpsimd", lambda g, fh=fh: g.dma_start(
                out=lists_d[fh * 1024:(fh + 1) * 1024, :].rearrange(
                    "(p a) b -> p (a b)", p=128),
                in_=s_fill[:, :]),
                  reads=("fillc",), writes=("lists",), sem="sq_lf", inc=16)
        def r_logits(tc):
            bl = 6 + tc % 2
            for dk in range(NDK):
                tr.cop("tensor",
                       lambda t, tc=tc, dk=dk, bl=bl: t.matmul(
                           psum[bl][:, 0:9],
                           s_xn[:, dk * TT + tc * 128: dk * TT + (tc + 1) * 128],
                           s_gw9T[:, dk * 9:(dk + 1) * 9],
                           start=(dk == 0), stop=(dk == NDK - 1)),
                       reads=("xn", "gw9T"), writes=(f"p{bl}",), close=(dk == NDK - 1))

        def r_chain_a(tc):
            bl = 6 + tc % 2
            tr.cop("vector", lambda v, bl=bl: v.max(s_i1[:, :], psum[bl][:, 0:8]),
                   reads=(f"p{bl}",), writes=("i1",))
            tr.cop("vector", lambda v, bl=bl: v.max_index(s_i2[:, :], s_i1[:, :], psum[bl][:, 0:8]),
                   reads=("i1", f"p{bl}"), writes=("i2",))

        def r_chain_b(tc):
            bl = 6 + tc % 2
            par = tc % 2
            w9c = par * 16
            mec = par * 8
            tr.cop("vector", lambda v: v.tensor_scalar(
                out=s_negm1[:, :], in0=s_i1[:, 0:1], scalar1=-1.0, scalar2=None,
                op0=ALU.mult),
                   reads=("i1",), writes=("negm1",))
            tr.cop("scalar", lambda sc, bl=bl: sc.activation(
                out=s_exp[:, :], in_=psum[bl][:, 0:8], func=AF.Exp,
                bias=s_negm1[:, 0:1], scale=1.0, accum_out=s_sum[:, 0:1]),
                   reads=(f"p{bl}", "negm1"), writes=("exp", "sum"))
            tr.cop("scalar", lambda sc, bl=bl, w9c=w9c: sc.activation(
                out=s_w9tm[:, w9c + 8:w9c + 9], in_=psum[bl][:, 8:9],
                func=AF.Sigmoid, scale=1.0),
                   reads=(f"p{bl}",), writes=(f"w9tm{par}",))
            tr.cop("vector", lambda v: v.tensor_copy(
                out=s_tmp8[:, 0:2], in_=s_i2[:, 0:2]),
                   reads=("i2",), writes=("tmp8",))
            tr.cop("vector", lambda v: v.tensor_single_scalar(
                out=s_oh1[:, :], in_=s_cst[:, 128:136], scalar=s_tmp8[:, 0:1],
                op=ALU.is_equal),
                   reads=("cst", "tmp8"), writes=("oh1",))
            tr.cop("vector", lambda v: v.tensor_single_scalar(
                out=s_oh2[:, :], in_=s_cst[:, 128:136], scalar=s_tmp8[:, 1:2],
                op=ALU.is_equal),
                   reads=("cst", "tmp8"), writes=("oh2",))
            tr.cop("vector", lambda v, mec=mec: v.tensor_tensor(
                out=s_me[:, mec:mec + 8], in0=s_oh1[:, :], in1=s_oh2[:, :], op=ALU.add),
                   reads=("oh1", "oh2"), writes=(f"me{par}",))
            tr.cop("vector", lambda v: v.reciprocal(s_rinv[:, :], s_sum[:, :]),
                   reads=("sum",), writes=("rinv",))
            tr.cop("vector", lambda v: v.tensor_scalar(
                out=s_exp[:, :], in0=s_exp[:, :], scalar1=s_rinv[:, 0:1], scalar2=None,
                op0=ALU.mult),
                   reads=("exp", "rinv"), writes=("exp",))
            tr.cop("vector", lambda v, w9c=w9c, mec=mec: v.tensor_tensor(
                out=s_w9tm[:, w9c:w9c + 8], in0=s_exp[:, :],
                in1=s_me[:, mec:mec + 8], op=ALU.mult),
                   reads=("exp", f"me{par}"), writes=(f"w9tm{par}",))

        def r_pe(tc):
            bp = tc % 2
            bb = 4 + tc % 2
            par = tc % 2
            w9c = par * 16
            mec = par * 8
            tr.cop("tensor",
                   lambda t, tc=tc, w9c=w9c: t.transpose(
                       pbf[3][0:9, tc * 128:(tc + 1) * 128],
                       s_w9tm[:, w9c:w9c + 9], s_idb[:, :]),
                   reads=(f"w9tm{par}", "idb"), writes=("p3",))
            tr.cop("tensor", lambda t, bb=bb, mec=mec: t.matmul(
                psum[bb][0:1, 0:8], s_cst[:, 137:138], s_me[:, mec:mec + 8],
                start=True, stop=True),
                   reads=("cst", f"me{par}"), writes=(f"p{bb}",))
            tr.cop("tensor", lambda t, bp=bp, mec=mec: t.matmul(
                psum[bp][:, 0:8], s_cst[:, 0:128], s_me[:, mec:mec + 8],
                start=True, stop=False),
                   reads=("cst", f"me{par}"), writes=(f"p{bp}",), close=False)
            tr.cop("tensor", lambda t, bp=bp: t.matmul(
                psum[bp][:, 0:8], s_cst[0:1, 0:128], s_base[0:1, :],
                start=False, stop=True),
                   reads=("cst", "base"), writes=(f"p{bp}",))

        def r_post(tc):
            bp = tc % 2
            bb = 4 + tc % 2
            par = tc % 2
            w9c = par * 16
            tr.cop("vector", lambda v, bb=bb: v.tensor_tensor(
                out=s_base[:, :], in0=s_base[:, :], in1=psum[bb][0:1, 0:8], op=ALU.add),
                   reads=("base", f"p{bb}"), writes=("base",))
            tr.cop("vector", lambda v, bp=bp: v.scalar_tensor_tensor(
                out=s_pe8[:, :], in0=s_cst[:, 128:136], scalar=512.0,
                in1=psum[bp][:, 0:8], op0=ALU.mult, op1=ALU.add),
                   reads=(f"p{bp}", "cst"), writes=("pe8",))
            for cls, oh, ohb, flt, flb, bgt, bgb, sct, scb in (
                    (1, s_oh1, "oh1", s_fl1, "fl1", s_bg1, "bg1", s_sc1, "sc1"),
                    (2, s_oh2, "oh2", s_fl2, "fl2", s_bg2, "bg2", s_sc2, "sc2")):
                tr.cop("vector", lambda v, oh=oh: v.tensor_tensor(
                    out=s_tmp8[:, :], in0=s_pe8[:, :], in1=oh[:, :], op=ALU.mult),
                       reads=("pe8", ohb), writes=("tmp8",))
                tr.cop("vector", lambda v, flt=flt: v.reduce_sum(
                    flt[:, :], s_tmp8[:, :], axis=AX.X),
                       reads=("tmp8",), writes=(flb,))
                tr.cop("vector", lambda v, flt=flt: v.tensor_scalar(
                    out=flt[:, :], in0=flt[:, :], scalar1=-1.0, scalar2=None,
                    op0=ALU.add),
                       reads=(flb,), writes=(flb,))
                tr.cop("vector", lambda v, flt=flt, bgt=bgt, tc=tc: v.tensor_copy(
                    out=bgt[:, tc:tc + 1], in_=flt[:, :]),
                       reads=(flb,), writes=(bgb,))
                sl = tc % 4
                scn = f"{scb}{sl}"
                tr.cop("vector", lambda v, sct=sct, sl=sl, tc=tc: v.tensor_scalar(
                    out=sct[:, 2 * sl:2 * sl + 1], in0=s_cst[:, 136:137],
                    scalar1=float(tc * 128), scalar2=None, op0=ALU.add),
                       reads=("cst",), writes=(scn,))
                tr.cop("vector", lambda v, oh=oh, w9c=w9c: v.tensor_tensor(
                    out=s_tmp8[:, :], in0=s_w9tm[:, w9c:w9c + 8], in1=oh[:, :],
                    op=ALU.mult),
                       reads=(f"w9tm{par}", ohb), writes=("tmp8",))
                tr.cop("vector", lambda v, sct=sct, sl=sl: v.reduce_sum(
                    sct[:, 2 * sl + 1:2 * sl + 2], s_tmp8[:, :], axis=AX.X),
                       reads=("tmp8",), writes=(scn,))
                tr.op("gpsimd",
                      lambda g, sct=sct, bgt=bgt, sl=sl, tc=tc: g.indirect_dma_start(
                          out=lists_d,
                          out_offset=bass.IndirectOffsetOnAxis(
                              ap=bgt[:, tc:tc + 1], axis=0),
                          in_=sct[:, 2 * sl:2 * sl + 2],
                          in_offset=None),
                      reads=(scn, bgb), writes=("lists",), sem="sq_sc", inc=16)

        NTC = TT // 128
        r_logits(0)
        r_logits(1)
        for tc in range(NTC):
            r_chain_a(tc)
            if tc > 0:
                r_post(tc - 1)
            r_chain_b(tc)
            if tc + 2 < NTC:
                r_logits(tc + 2)
            r_pe(tc)
        r_post(NTC - 1)
        tr.cop("vector", lambda v: v.tensor_copy(out=s_w9fm[:, :], in_=pbf[3][0:9, 0:TT]),
               reads=("p3",), writes=("w9fm",))

        wa = [("wa0", "sq_wa0", 0), ("wa1", "sq_wa1", 4096)]

        def fill_attn(c, half):
            buf, sm, base = wa[half]
            ch = c // 4
            srcs = (((zq, f"scr_q{ch}"), (zk, f"scr_k{ch}")) if half == 0
                    else ((zv, f"scr_v{ch}"), (zg, f"scr_g{ch}")))
            for pi, (zsrc, zb) in enumerate(srcs):
                off = base + pi * D
                tr.op("sync",
                      lambda s, zsrc=zsrc, c=c, off=off: s.dma_start(
                          out=s_ha[:, off:off + D],
                          in_=zsrc[c * 128:(c + 1) * 128, :]),
                      reads=(zb,), writes=(buf, "hA"), sem=sm, inc=16)

        fill_attn(0, 0)
        fill_attn(0, 1)
        for c in range(NDK):
            for pi, (buf, base) in enumerate(
                    (("wa0", 0), ("wa0", 2048), ("wa1", 4096), ("wa1", 6144))):
                for th in range(2):
                    bank = pi * 2 + th
                    for dk in range(NDK):
                        tr.cop("tensor",
                               lambda t, base=base, bank=bank, dk=dk, th=th: t.matmul(
                                   psum[bank][:, :],
                                   s_ha[:, base + dk * 128: base + (dk + 1) * 128],
                                   s_xn[:, dk * TT + th * 512: dk * TT + (th + 1) * 512],
                                   start=(dk == 0), stop=(dk == NDK - 1)),
                               reads=(buf, "xn"), writes=(f"p{bank}",),
                               close=(dk == NDK - 1))
                if pi == 1 and c < NDK - 1:
                    fill_attn(c + 1, 0)
                if pi == 3 and c < NDK - 1:
                    fill_attn(c + 1, 1)
            for th in range(2):
                tr.cop("scalar",
                       lambda sc, c=c, th=th: sc.activation(
                           out=s_q[:, th * 512:(th + 1) * 512], in_=psum[th][:, :],
                           func=AF.Identity, bias=s_bias[:, c:c + 1], scale=1.0),
                       reads=(f"p{th}", "bias"), writes=("q", "hB"))
                tr.cop("scalar",
                       lambda sc, c=c, th=th: sc.activation(
                           out=s_k[:, th * 512:(th + 1) * 512], in_=psum[2 + th][:, :],
                           func=AF.Identity, bias=s_bias[:, NDK + c:NDK + c + 1],
                           scale=1.0),
                       reads=(f"p{2 + th}", "bias"), writes=("k", "hB"))
                tr.cop("scalar",
                       lambda sc, c=c, th=th: sc.activation(
                           out=s_g[:, th * 512:(th + 1) * 512], in_=psum[6 + th][:, :],
                           func=AF.Sigmoid, bias=s_bias[:, 3 * NDK + c:3 * NDK + c + 1],
                           scale=1.0),
                       reads=(f"p{6 + th}", "bias"), writes=("g", "hB"))
                tr.cop("vector",
                       lambda v, c=c, th=th: v.scalar_tensor_tensor(
                           out=s_kv[:, th * 512:(th + 1) * 512], in0=psum[4 + th][:, :],
                           scalar=s_bias[:, 2 * NDK + c:2 * NDK + c + 1],
                           in1=s_k[:, th * 512:(th + 1) * 512],
                           op0=ALU.add, op1=ALU.mult),
                       reads=(f"p{4 + th}", "bias", "k"), writes=("kv", "utt0", "utt1") + eopj["eop3"])
            tr.cop("vector",
                   lambda v, c=c: v.tensor_tensor_scan(
                       out=s_cs[:, :], data0=s_ones[:, :], data1=s_kv[:, :],
                       initial=s_carry[:, c:c + 1], op0=ALU.mult, op1=ALU.add),
                   reads=("ones", "kv", "utt0", "utt1", "carry"), writes=("cs",) + eopj["eop0"] + eopj["eop1"])
            tr.cop("vector",
                   lambda v, c=c: v.tensor_copy(
                       out=s_carry[:, c:c + 1], in_=s_cs[:, TT - 1:TT]),
                   reads=("cs",), writes=("carry",))
            tr.cop("vector", lambda v: v.tensor_tensor(
                out=s_qg[:, :], in0=s_q[:, :], in1=s_g[:, :], op=ALU.mult),
                   reads=("q", "g"), writes=("qg", "sgt0", "sgt1") + eopj["eop2"])
            tr.cop("vector",
                   lambda v, c=c: v.tensor_tensor(
                       out=s_out[:, c * TT:(c + 1) * TT], in0=s_qg[:, :],
                       in1=s_cs[:, :], op=ALU.mult),
                   reads=("qg", "sgt0", "sgt1", "cs"), writes=("out",))

        wbs = [("wb0", s_wb0), ("wb1", s_wb1)]
        whs = [("wh0", "sq_wh0", 0), ("wh1", "sq_wh1", 4096)]
        wos = [("wo0", "sq_wo0", s_wo0), ("wo1", "sq_wo1", s_wo1)]

        def fill_wh(g, i, slot):
            buf, sm, base = whs[slot]
            for half in range(2):
                zsrc, rbase, zb = gu_src(g, half, i)
                r0 = rbase + i * 128
                off = base + half * D
                tr.op("scalar" if half == 0 else "sync",
                      lambda s, zsrc=zsrc, r0=r0, off=off: s.dma_start(
                          out=s_wh[:, off:off + D],
                          in_=zsrc[r0:r0 + 128, :]),
                      reads=(zb,), writes=(buf,), sem=sm, inc=16)

        def fill_wo(g, ec, slot):
            buf, sm, tile = wos[slot]
            zsrc, rbase, zb = w2_src(g, ec)
            tr.op("sync",
                  lambda s, zsrc=zsrc, rbase=rbase, ec=ec, tile=tile:
                  s.dma_start(
                      out=tile[:, :],
                      in_=zsrc[rbase + ec * 128: rbase + (ec + 1) * 128, :]),
                  reads=(zb,), writes=(buf,), sem=sm, inc=16)

        def make_wb(g, slot):
            wbuf, wtile = wbs[slot]
            r = wrow(g)
            for th in range(2):
                tr.cop("tensor",
                       lambda t, r=r, th=th: t.matmul(
                           psum[6][:, :], s_sel[:, 0:128],
                           s_w9fm[0:9, th * 512:(th + 1) * 512],
                           start=True, stop=True),
                       reads=("sel", "w9fm"), writes=("p6",), close=True)
                tr.cop("scalar",
                       lambda sc, wtile=wtile, th=th: sc.activation(
                           out=wtile[:, th * 512:(th + 1) * 512], in_=psum[6][:, :],
                           func=AF.Identity, scale=1.0),
                       reads=("p6",), writes=(wbuf,))

        def load_lists(e):
            b8 = (e % 2) * 8
            b4 = (e % 2) * 4
            tr.op("scalar",
                  lambda s, e=e, b8=b8: s.dma_start(
                      out=s_glst[:, b8:b8 + 8].rearrange("p (a b) -> p a b", b=2),
                      in_=lists_d[e * CAP:(e + 1) * CAP, :].rearrange(
                          "(a p) b -> p a b", p=128)),
                  reads=("lists",), writes=("glst",), sem="sq_ll", inc=16)
            tr.cop("vector",
                   lambda v, b8=b8, b4=b4: v.tensor_copy(
                       out=s_gif[:, b4:b4 + 4].rearrange("p (a c) -> p a c", c=1),
                       in_=s_glst[:, b8:b8 + 8].rearrange(
                           "p (a c) -> p a c", c=2)[:, :, 0:1]),
                   reads=("glst",), writes=("gif", "me0", "me1"))
            tr.cop("vector",
                   lambda v, b4=b4: v.tensor_scalar(
                       out=s_gif[:, b4:b4 + 4], in0=s_gif[:, b4:b4 + 4],
                       scalar1=float(TT), scalar2=None, op0=ALU.min),
                   reads=("gif",), writes=("gif",))
            tr.cop("vector",
                   lambda v, b4=b4: v.tensor_copy(
                       out=s_gidx[:, b4:b4 + 4], in_=s_gif[:, b4:b4 + 4]),
                   reads=("gif",), writes=("gidx",))
            b4w = (e % 4) * 4
            tr.cop("vector",
                   lambda v, b8=b8, b4w=b4w: v.tensor_copy(
                       out=s_gw[:, b4w:b4w + 4].rearrange("p (a c) -> p a c", c=1),
                       in_=s_glst[:, b8:b8 + 8].rearrange(
                           "p (a c) -> p a c", c=2)[:, :, 1:2]),
                   reads=("glst",), writes=("gw",))

        def gather_tile(e, jt):
            b4 = (e % 2) * 4
            tr.op("gpsimd",
                  lambda g, b4=b4, jt=jt: g.indirect_dma_start(
                      out=s_xg[:, :],
                      out_offset=None,
                      in_=xtok_d,
                      in_offset=bass.IndirectOffsetOnAxis(
                          ap=s_gidx[:, b4 + jt:b4 + jt + 1], axis=0)),
                  reads=("gidx", "xtok"), writes=("xg0",), sem="sq_gx", inc=16)

        def transpose_tile(e, jt):
            for hb in range(2):
                bank = 6 + hb
                for dj in range(8):
                    dk = hb * 8 + dj
                    tr.cop("tensor",
                           lambda t, dk=dk, dj=dj, bank=bank: t.transpose(
                               pbf[bank][:, dj * 128:(dj + 1) * 128],
                               s_xg[:, dk * 128:(dk + 1) * 128],
                               s_idb[:, :]),
                           reads=("xg0", "idb"), writes=(f"p{bank}",),
                           close=(dj == 7))
                tr.cop("vector",
                       lambda v, bank=bank, hb=hb, jt=jt: v.tensor_copy(
                           out=s_xnex[:, :].rearrange("p (k s) -> p k s", k=NDK)[
                               :, hb * 8:hb * 8 + 8, jt * 128:(jt + 1) * 128],
                           in_=pbf[bank][:, 0:1024].rearrange("p (k s) -> p k s", k=8)),
                       reads=(f"p{bank}",), writes=("xnex",))

        make_wb(0, 0)
        gchunk = 0
        fill_wh(0, 0, 0)
        for g in range(NGS):
            wbuf, wtile = wbs[g % 2]
            for i in range(NI):
                whbuf, whsm, whbase = whs[gchunk % 2]
                if i + 1 < NI:
                    fill_wh(g, i + 1, (gchunk + 1) % 2)
                elif g + 1 < NG:
                    fill_wh(g + 1, 0, (gchunk + 1) % 2)
                gchunk += 1
                if g == 2 and i == 0:
                    load_lists(0)
                if g == 2 and i == 5:
                    load_lists(1)
                if g == 3 and i < 8:
                    if i % 2 == 0:
                        gather_tile(0, i // 2)
                    else:
                        transpose_tile(0, i // 2)
                if p == 0 and g >= 1 and i % 2 == 0:
                    emit_piece(deferred[g][i // 2], look=5)
                for half in range(2):
                    for th in range(2):
                        bank = half * 2 + th
                        for dk in range(NDK):
                            tr.cop("tensor",
                                   lambda t, whbase=whbase, half=half, th=th,
                                   bank=bank, dk=dk: t.matmul(
                                       psum[bank][:, :],
                                       s_wh[:, whbase + half * 2048 + dk * 128:
                                            whbase + half * 2048 + (dk + 1) * 128],
                                       s_xn[:, dk * TT + th * 512:
                                            dk * TT + (th + 1) * 512],
                                       start=(dk == 0), stop=(dk == NDK - 1)),
                                   reads=(whbuf, "xn"), writes=(f"p{bank}",),
                                   close=(dk == NDK - 1))
                for th in range(2):
                    tr.cop("scalar",
                           lambda sc, th=th: sc.activation(
                               out=s_sg[:, th * 512:(th + 1) * 512], in_=psum[th][:, :],
                               func=AF.Silu, scale=1.0),
                           reads=(f"p{th}",), writes=(f"sgt{th}",) + eopj["eop2"])
                    tr.cop("vector",
                           lambda v, th=th, wtile=wtile: v.tensor_tensor(
                               out=s_ut[:, th * 512:(th + 1) * 512],
                               in0=psum[2 + th][:, :],
                               in1=wtile[:, th * 512:(th + 1) * 512], op=ALU.mult),
                           reads=(f"p{2 + th}", wbuf), writes=(f"utt{th}",) + eopj["eop3"])
                tr.cop("vector",
                       lambda v, i=i: v.tensor_tensor(
                           out=s_ha[:, i * 1024:(i + 1) * 1024], in0=s_sg[:, :],
                           in1=s_ut[:, :], op=ALU.mult),
                       reads=("sgt0", "sgt1", "utt0", "utt1"), writes=("hA" if i < 8 else "hB",))
            if g + 1 < NGS:
                make_wb(g + 1, (g + 1) % 2)
            fill_wo(g, 0, 0)
            for ec in range(8):
                wobuf, wosm, wotile = wos[ec % 2]
                if ec + 1 < 8:
                    fill_wo(g, ec + 1, (ec + 1) % 2)
                for el in range(2):
                    ek = ec * 2 + el
                    for th in range(2):
                        bank = 4 + (ek * 2 + th) % 2
                        for i in range(NI):
                            tr.cop("tensor",
                                   lambda t, wotile=wotile, i=i, el=el, th=th,
                                   bank=bank: t.matmul(
                                       psum[bank][:, :],
                                       wotile[:, i * 256 + el * 128:
                                              i * 256 + (el + 1) * 128],
                                       s_ha[:, i * 1024 + th * 512:
                                            i * 1024 + (th + 1) * 512],
                                       start=(i == 0), stop=(i == NI - 1)),
                                   reads=(wobuf, "hA" if i < 8 else "hB"),
                                   writes=(f"p{bank}",),
                                   close=(i == NI - 1))
                        tr.cop("vector",
                               lambda v, ek=ek, th=th, bank=bank: v.tensor_tensor(
                                   out=s_out[:, ek * TT + th * 512:
                                             ek * TT + (th + 1) * 512],
                                   in0=psum[bank][:, :],
                                   in1=s_out[:, ek * TT + th * 512:
                                             ek * TT + (th + 1) * 512],
                                   op=ALU.add),
                               reads=(f"p{bank}", "out"), writes=("out",))
        for e in range(NE):
            g = NGS + e
            b4w = (e % 4) * 4
            for i in range(NI):
                whbuf, whsm, whbase = whs[gchunk % 2]
                if i + 1 < NI:
                    fill_wh(g, i + 1, (gchunk + 1) % 2)
                elif g + 1 < NG:
                    fill_wh(g + 1, 0, (gchunk + 1) % 2)
                gchunk += 1
                if p == 0 and e in (0, 1, 2) and i % 2 == 0:
                    emit_piece(deferred[("r", e)][i // 2], look=5)
                if e + 2 < NE and i == 1:
                    load_lists(e + 2)
                ub = 2 * (i % 2)
                for half in range(2):
                    for dk in range(NDK):
                        tr.cop("tensor",
                               lambda t, whbase=whbase, half=half, dk=dk, ub=ub: t.matmul(
                                   psum[ub + half][:, :],
                                   s_wh[:, whbase + half * 2048 + dk * 128:
                                        whbase + half * 2048 + (dk + 1) * 128],
                                   s_xnex[:, dk * CAP:(dk + 1) * CAP],
                                   start=(dk == 0), stop=(dk == NDK - 1)),
                               reads=(whbuf, "xnex"), writes=(f"p{ub + half}",),
                               close=(dk == NDK - 1))
                par = i % 2
                tr.cop("scalar",
                       lambda sc, ub=ub, par=par: sc.activation(
                           out=s_sg[:, par * 512:(par + 1) * 512], in_=psum[ub][:, :],
                           func=AF.Silu, scale=1.0),
                       reads=(f"p{ub}",), writes=(f"sgt{par}",) + eopj["eop2"])
                tr.cop("vector",
                       lambda v, ub=ub, par=par: v.tensor_copy(
                           out=s_ut[:, par * 512:(par + 1) * 512],
                           in_=psum[ub + 1][:, :]),
                       reads=(f"p{ub + 1}",), writes=(f"utt{par}",) + eopj["eop3"])
                tr.cop("vector",
                       lambda v, i=i, par=par: v.tensor_tensor(
                           out=s_ha[:, i * 512:(i + 1) * 512],
                           in0=s_sg[:, par * 512:(par + 1) * 512],
                           in1=s_ut[:, par * 512:(par + 1) * 512], op=ALU.mult),
                       reads=(f"sgt{par}", f"utt{par}"), writes=("hA",))
            fill_wo(g, 0, 0)
            for ec in range(8):
                wobuf, wosm, wotile = wos[ec % 2]
                if ec + 1 < 8:
                    fill_wo(g, ec + 1, (ec + 1) % 2)
                if e + 1 < NE:
                    if ec % 2 == 0:
                        gather_tile(e + 1, ec // 2)
                    else:
                        transpose_tile(e + 1, ec // 2)
                for jt in range(CT):
                    for i in range(NI):
                        tr.cop("tensor",
                               lambda t, wotile=wotile, i=i, jt=jt: t.matmul(
                                   psum[2 + jt][:, 0:256],
                                   s_ha[:, i * 512 + jt * 128:
                                        i * 512 + (jt + 1) * 128],
                                   wotile[:, i * 256:(i + 1) * 256],
                                   start=(i == 0), stop=(i == NI - 1)),
                               reads=(wobuf, "hA"), writes=(f"p{2 + jt}",),
                               close=(i == NI - 1))
                    ereg, ename, extra = eop_regions[ec % 4]
                    jn = eopj[ename][jt]
                    tr.cop("scalar",
                           lambda sc, jt=jt, ereg=ereg, b4w=b4w: sc.activation(
                               out=ereg[:, jt * 256:(jt + 1) * 256],
                               in_=psum[2 + jt][:, 0:256], func=AF.Identity,
                               scale=s_gw[:, b4w + jt:b4w + jt + 1]),
                           reads=(f"p{2 + jt}", "gw"), writes=(jn,) + extra)
                    if jt == CT - 1:
                        tr.op("sync",
                              lambda s, e=e, ec=ec, ereg=ereg: s.dma_start(
                                  out=eout_d[e * CAP:(e + 1) * CAP,
                                             ec * 256:(ec + 1) * 256].rearrange(
                                      "(a p) c -> p a c", p=128),
                                  in_=ereg[:, 0:1024].rearrange(
                                      "p (a c) -> p a c", c=256)),
                              reads=eopj[ename], writes=("eout",), sem="sq_eo", inc=16)
        def bg_slot(slot):
            if slot < 4:
                return s_xnex[:, slot * 2048:(slot + 1) * 2048]
            return s_ha[:, (slot - 4) * 2048:(slot - 3) * 2048]

        def back_gather(tc, cls):
            slot = (2 * tc + cls) % 8
            st = bg_slot(slot)
            bgt, bgb = (s_bg1, "bg1") if cls == 0 else (s_bg2, "bg2")
            tr.op("gpsimd",
                  lambda g, st=st, bgt=bgt, tc=tc: g.indirect_dma_start(
                      out=st, out_offset=None, in_=eout_d,
                      in_offset=bass.IndirectOffsetOnAxis(
                          ap=bgt[:, tc:tc + 1], axis=0)),
                  reads=(bgb, "eout"),
                  writes=(f"bgs{slot}", "xnex" if slot < 4 else "hA"),
                  sem="sq_bg", inc=16)

        for tc in range(4):
            back_gather(tc, 0)
            back_gather(tc, 1)
        otm = [(s_wh[:, 0:4096].bitcast(F32), ("wh0",)),
               (s_wh[:, 4096:8192].bitcast(F32), ("wh1",))]
        for tc in range(TT // 128):
            sb, ssem, stile = xslot[tc % 2]
            row = t0 + tc * 128
            tr.op("scalar",
                  lambda s, stile=stile, row=row: s.dma_start(
                      out=stile[:, :], in_=x_d[row:row + 128, :]),
                  writes=(sb,), sem=ssem, inc=16)
            ot, otbufs = otm[tc % 2]
            slot1 = (2 * tc) % 8
            slot2 = (2 * tc + 1) % 8
            g1, g2 = bg_slot(slot1), bg_slot(slot2)
            for quad in range(4):
                bank = quad
                gbank = 4 + quad
                for ej in range(4):
                    ek = quad * 4 + ej
                    tr.cop("tensor",
                           lambda t, bank=bank, ej=ej, ek=ek, tc=tc: t.transpose(
                               psum[bank][:, ej * 128:(ej + 1) * 128],
                               s_out[:, ek * TT + tc * 128: ek * TT + (tc + 1) * 128],
                               s_idf[:, :]),
                           reads=("out", "idf"), writes=(f"p{bank}",),
                           close=(ej == 3))
                tr.cop("tensor",
                       lambda t, gbank=gbank, quad=quad, g1=g1: t.matmul(
                           psum[gbank][:, 0:512],
                           s_idb[:, 0:128],
                           g1[:, quad * 512:(quad + 1) * 512],
                           start=True, stop=False),
                       reads=(f"bgs{slot1}", "idb",
                              "xnex" if slot1 < 4 else "hA"),
                       writes=(f"p{gbank}",), close=False)
                tr.cop("tensor",
                       lambda t, gbank=gbank, quad=quad, g2=g2: t.matmul(
                           psum[gbank][:, 0:512],
                           s_idb[:, 0:128],
                           g2[:, quad * 512:(quad + 1) * 512],
                           start=False, stop=True),
                       reads=(f"bgs{slot2}", "idb",
                              "xnex" if slot2 < 4 else "hA"),
                       writes=(f"p{gbank}",), close=True)
                tr.cop("vector",
                       lambda v, bank=bank, quad=quad, ot=ot, stile=stile:
                       v.tensor_tensor(
                           out=ot[:, quad * 512:(quad + 1) * 512],
                           in0=psum[bank][:, 0:512],
                           in1=stile[:, quad * 512:(quad + 1) * 512], op=ALU.add),
                       reads=(f"p{bank}", sb), writes=otbufs)
                tr.cop("vector",
                       lambda v, gbank=gbank, quad=quad, ot=ot:
                       v.tensor_tensor(
                           out=ot[:, quad * 512:(quad + 1) * 512],
                           in0=psum[gbank][:, 0:512],
                           in1=ot[:, quad * 512:(quad + 1) * 512], op=ALU.add),
                       reads=(f"p{gbank}",) + otbufs, writes=otbufs)
            tr.op("sync" if tc % 2 == 0 else "scalar",
                  lambda s, ot=ot, row=row: s.dma_start(
                      out=out_d[row:row + 128, :], in_=ot[:, :]),
                  reads=otbufs, writes=(f"sto{tc % 2}",),
                  sem=f"sq_st{tc % 2}", inc=16)
            n_store += 1
            if tc + 4 < TT // 128:
                back_gather(tc + 4, 0)
                back_gather(tc + 4, 1)

    tr.barrier_wait("vector", "sq_st0", 16 * (n_store // 2))
    tr.barrier_wait("vector", "sq_st1", 16 * (n_store // 2))

    block = E(nc.Block())
    tr.replay(block)
    ctx.close()
    return nc


_NC_CACHE = {}


def _get_nc():
    if "nc" not in _NC_CACHE:
        _NC_CACHE["nc"] = build_nc()
    return _NC_CACHE["nc"]


NEH = 8


def make_in_maps(inputs):
    x = np.ascontiguousarray(np.asarray(inputs["L0_attn_x"], dtype=np.float32))
    bqkvg = np.stack([np.asarray(inputs["L0_attn_bq"]), np.asarray(inputs["L0_attn_bk"]),
                      np.asarray(inputs["L0_attn_bv"]), np.asarray(inputs["L0_attn_bg"])],
                     axis=0)
    bqkvg = np.ascontiguousarray(bqkvg, dtype=np.float32)
    gw9 = np.concatenate([np.asarray(inputs["L0_ffn_gate_w"]),
                          np.asarray(inputs["L0_ffn_shared_gate_w"])], axis=0)
    gw9 = np.ascontiguousarray(gw9, dtype=np.float32)
    idf = np.eye(128, dtype=np.float32)
    idb = np.eye(128).astype(ml_dtypes.bfloat16)
    cst = np.zeros((128, 138), dtype=np.float32)
    cst[:, 0:128] = np.triu(np.ones((128, 128), dtype=np.float32))
    cst[:, 128:136] = np.arange(8, dtype=np.float32)[None, :]
    cst[:, 136] = np.arange(128, dtype=np.float32)
    cst[:, 137] = 1.0
    sel = np.zeros((9, 9, 128), dtype=np.float32)
    for r in range(9):
        sel[r, r, :] = 1.0
    sel = np.ascontiguousarray(sel.reshape(9, 9 * 128)).astype(ml_dtypes.bfloat16)
    def sw_up(w, rc):
        w = np.asarray(w, dtype=np.float32)
        return np.ascontiguousarray(
            w.reshape(rc, 128, 16, 128).transpose(0, 3, 2, 1).reshape(rc * 128, 2048))

    w2f = np.asarray(inputs["L0_ffn_w2"], dtype=np.float32)
    w2sw = np.ascontiguousarray(
        w2f.reshape(NEH, 8, 2, 128, 11, 128).transpose(0, 1, 5, 4, 2, 3)
        .reshape(NEH, 1024, 2816))
    wsdf = np.asarray(inputs["L0_ffn_shared_down_w"], dtype=np.float32)
    wsdsw = np.ascontiguousarray(
        wsdf.reshape(8, 2, 128, 4, 11, 128).transpose(3, 0, 5, 4, 1, 2)
        .reshape(4 * 1024, 2816))
    common = dict(
        wq=sw_up(inputs["L0_attn_Wq"], 16),
        wk=sw_up(inputs["L0_attn_Wk"], 16),
        wv=sw_up(inputs["L0_attn_Wv"], 16),
        wg=sw_up(inputs["L0_attn_Wg"], 16),
        bqkvg=bqkvg, gw9=gw9,
        wgu=sw_up(inputs["L0_ffn_shared_gate_up_w"], 88),
        wsd=wsdsw,
        w13=np.ascontiguousarray(
            np.asarray(inputs["L0_ffn_w13"], dtype=np.float32)
            .reshape(NEH, 22, 128, 16, 128).transpose(0, 1, 4, 3, 2)
            .reshape(NEH, 2816, 2048)),
        w2=w2sw,
        idf=idf, idb=idb, sel=sel, cst=cst,
    )
    n = x.shape[0]
    return [dict(common, x=np.ascontiguousarray(x[i])) for i in range(n)]


def kernel(**inputs):
    in_maps = make_in_maps(inputs)
    nc = _get_nc()
    res = run_bass_kernel_spmd(nc, in_maps, core_ids=list(range(8)))
    return np.stack([r["out"] for r in res.results], axis=0)
